# version 03
# baseline (speedup 1.0000x reference)
import math
from contextlib import ExitStack

import numpy as np
import ml_dtypes

import concourse.bass as bass
import concourse.mybir as mybir
from concourse.bass_utils import run_bass_kernel_spmd

F32 = mybir.dt.float32
BF16 = mybir.dt.bfloat16
I32 = mybir.dt.int32
AX = mybir.AxisListType.X
OP = mybir.AluOpType
AF = mybir.ActivationFunctionType

D = 2048
T = 4096
TOK = 2048
NB = 4
H = 16
HD = 128
KVH = 4
I_FFN = 8192
FFN1 = 1408
NE = 8
EPS = 1e-5
RS128 = 1.0 / math.sqrt(128.0)
NEG = -1e30

SLOPES = [2.0 ** (-(h + 1) / 2.0) for h in range(H)]


def _heads(role):
    return [h for h in range(H) if h % 4 in (2 * role, 2 * role + 1)]



def _tiles2(w, kb, mb):
    return np.ascontiguousarray(
        w.reshape(kb, 128, mb, 128).transpose(0, 2, 1, 3))


def _colblock(v):
    return np.ascontiguousarray(v.reshape(16, 128).T)


def _stage_core(inputs, core):
    role = core % 2
    batch = core // 2
    mine = _heads(role)
    theirs = _heads(1 - role)
    groups = (2 * role, 2 * role + 1)

    d = {}
    xb = np.asarray(inputs["L0_attn_x"][batch], np.float32)
    xT = np.ascontiguousarray(xb.T)
    d["xf"] = xT
    d["xo"] = np.ascontiguousarray(xT[:, role * TOK:(role + 1) * TOK])

    wq = np.asarray(inputs["L0_attn_Wq"], np.float32)
    d["wq_t"] = np.stack([_tiles2(wq[:, h * 128:(h + 1) * 128], 16, 1)[:, 0]
                          for h in mine])
    wkv = np.asarray(inputs["L0_attn_Wkv"], np.float32)
    cols = [wkv[:, g * 128:(g + 1) * 128] for g in groups] + \
           [wkv[:, 512 + g * 128:512 + (g + 1) * 128] for g in groups]
    d["wkv_t"] = np.stack([_tiles2(c, 16, 1)[:, 0] for c in cols])

    wo = np.asarray(inputs["L0_attn_Wo"], np.float32)
    perm = []
    for h in mine + theirs:
        perm.extend(range(h * 128, (h + 1) * 128))
    d["wo0_t"] = np.ascontiguousarray(_tiles2(wo[perm, :], 16, 16).transpose(0, 2, 1, 3))
    d["bo0c"] = _colblock(np.asarray(inputs["L0_attn_bo"], np.float32))

    wg = np.asarray(inputs["L0_ffn_w_gate"], np.float32)
    wu = np.asarray(inputs["L0_ffn_w_up"], np.float32)
    wd = np.asarray(inputs["L0_ffn_w_down"], np.float32)
    wg_p = np.ascontiguousarray(wg.reshape(D, I_FFN // 4, 4).transpose(0, 2, 1).reshape(D, I_FFN))
    wu_p = np.ascontiguousarray(wu.reshape(D, I_FFN // 4, 4).transpose(0, 2, 1).reshape(D, I_FFN))
    wd_p = np.ascontiguousarray(wd.reshape(I_FFN // 4, 4, D).transpose(1, 0, 2).reshape(I_FFN, D))
    d["wg_t"] = np.ascontiguousarray(_tiles2(wg_p, 16, 64).transpose(1, 0, 2, 3))
    d["wu_t"] = np.ascontiguousarray(_tiles2(wu_p, 16, 64).transpose(1, 0, 2, 3))
    d["wd_t"] = _tiles2(wd_p, 64, 16)

    for nm, key in (("wk_t", "L1_attn_Wk"), ("wv_t", "L1_attn_Wv"),
                    ("wr_t", "L1_attn_Wr"), ("wo1_t", "L1_attn_Wo")):
        w = np.ascontiguousarray(np.asarray(inputs[key], np.float32).T)
        d[nm] = np.ascontiguousarray(_tiles2(w, 16, 16).transpose(0, 2, 1, 3))
    for nm, key in (("bkc", "L1_attn_bk"), ("bvc", "L1_attn_bv"),
                    ("brc", "L1_attn_br"), ("bo1c", "L1_attn_bo"),
                    ("uc", "L1_attn_u"), ("wc", "L1_attn_w"),
                    ("tmkc", "L1_attn_tmk"), ("tmvc", "L1_attn_tmv"),
                    ("tmrc", "L1_attn_tmr")):
        d[nm] = _colblock(np.asarray(inputs[key], np.float32))

    w1 = np.asarray(inputs["L1_ffn_w1"], np.float32)
    w2 = np.asarray(inputs["L1_ffn_w2"], np.float32)
    d["w1_t"] = np.stack([_tiles2(w1[e], 16, 22) for e in range(NE)])
    d["w2_t"] = np.stack([np.ascontiguousarray(_tiles2(w2[e], 11, 16).transpose(0, 2, 1, 3))
                          for e in range(NE)])
    wgr = np.asarray(inputs["L1_ffn_wg"], np.float32)
    d["wgr_t"] = np.ascontiguousarray(wgr.T.reshape(16, 128, 8))

    masks = np.zeros((4, 128, 512), np.float32)
    for si, s in enumerate((0, 128, 256, 384)):
        i = np.arange(128)[:, None]
        j = np.arange(512)[None, :]
        masks[si] = np.where(j >= i + s, 0.0, NEG)
    d["maskc"] = masks
    alib = np.zeros((128, 256), np.float32)
    for hh, h in enumerate(mine):
        for kb in range(32):
            alib[:, hh * 32 + kb] = SLOPES[h] * (kb * 128 + np.arange(128))
    d["alibic"] = alib
    sqr = np.zeros((1, 8 * T), np.float32)
    for hh, h in enumerate(mine):
        sqr[0, hh * T:(hh + 1) * T] = SLOPES[h] * np.arange(T)
    d["sqrow"] = sqr
    d["eye"] = np.eye(128, dtype=np.float32)
    d["sumcol"] = np.full((128, 1), 1.0 / D, np.float32)
    d["onecol"] = np.ones((128, 1), np.float32)
    d["maskcol"] = np.full((128, 1), float(role), np.float32)
    return d


def _stage_all(inputs):
    per_core = [_stage_core(inputs, c) for c in range(8)]
    rng = np.random.default_rng()
    for k in range(4):
        nonce = rng.integers(1 << 16, (1 << 31) - 1, size=(18, 1)).astype(np.int32)
        per_core[2 * k]["nonce"] = nonce
        per_core[2 * k + 1]["nonce"] = nonce
    return per_core


CATALOG = {
    "xf": ((D, T), F32), "xo": ((D, TOK), F32),
    "wq_t": ((8, 16, 128, 128), F32), "wkv_t": ((4, 16, 128, 128), F32),
    "wo0_t": ((16, 128, 16, 128), F32), "bo0c": ((128, 16), F32),
    "wg_t": ((64, 16, 128, 128), F32), "wu_t": ((64, 16, 128, 128), F32),
    "wd_t": ((64, 16, 128, 128), F32),
    "wk_t": ((16, 128, 16, 128), F32), "wv_t": ((16, 128, 16, 128), F32),
    "wr_t": ((16, 128, 16, 128), F32), "wo1_t": ((16, 128, 16, 128), F32),
    "bkc": ((128, 16), F32), "bvc": ((128, 16), F32), "brc": ((128, 16), F32),
    "bo1c": ((128, 16), F32), "uc": ((128, 16), F32), "wc": ((128, 16), F32),
    "tmkc": ((128, 16), F32), "tmvc": ((128, 16), F32), "tmrc": ((128, 16), F32),
    "w1_t": ((8, 16, 22, 128, 128), F32), "w2_t": ((8, 11, 128, 16, 128), F32),
    "wgr_t": ((16, 128, 8), F32),
    "maskc": ((4, 128, 512), F32), "alibic": ((128, 256), F32),
    "sqrow": ((1, 8 * T), F32), "eye": ((128, 128), F32),
    "sumcol": ((128, 1), F32), "onecol": ((128, 1), F32),
    "maskcol": ((128, 1), F32), "nonce": ((18, 1), I32),
    "xnf": ((D, T), BF16), "xno": ((D, TOK), BF16),
    "kT": ((256, T), BF16), "vTok": ((T, 256), BF16),
    "qsc": ((1024, T), BF16), "rng_": ((1, 8 * T), BF16),
    "qn2": ((1, 8 * T), F32),
    "g0": ((D, TOK), F32), "x1": ((D, TOK), F32),
    "xn1": ((D, TOK), BF16),
    "kt1": ((D, TOK), BF16), "v1": ((D, TOK), BF16), "r1": ((D, TOK), BF16),
    "rwkv": ((D, TOK), BF16), "s1": ((D, TOK), F32), "crow": ((8, TOK), F32),
    "outT": ((D, TOK), F32),
}

SHARED = {
    "sh_ao": ((2, 1024, TOK), BF16),
    "sh_bcol": ((2, 128, 16), BF16),
    "sh_carry": ((2, 128, 32), F32),
    "sh_flag": ((18, 2), I32),
}

STAGE_IO = {
    "ln0kvq": (["xf", "wkv_t", "wq_t", "sumcol", "onecol", "eye", "sqrow", "nonce"],
               ["xnf", "xno", "kT", "vTok", "qsc", "rng_", "qn2"]),
    "ffn0": (["xno", "wg_t", "wu_t", "wd_t"], ["g0"]),
    "attn": (["kT", "vTok", "qsc", "rng_", "maskc", "alibic", "onecol",
              "wo0_t", "bo0c", "xo", "g0", "nonce"], ["x1"]),
    "ln1mix": (["x1", "sumcol", "onecol", "wk_t", "wv_t", "wr_t", "bkc", "bvc", "brc",
                "uc", "tmkc", "tmvc", "tmrc", "maskcol", "nonce"],
               ["xn1", "kt1", "v1", "r1"]),
    "wkv": (["kt1", "v1", "r1", "wc", "maskcol", "nonce"], ["rwkv"]),
    "moe": (["rwkv", "wo1_t", "bo1c", "xn1", "x1", "wgr_t", "w1_t", "w2_t",
             "onecol", "eye"], ["s1", "outT", "crow"]),
}

LAUNCH_PLAN = [["ln0kvq"], ["ffn0"], ["attn"], ["ln1mix"], ["wkv"], ["moe"]]


class S:

    def __init__(self, nc, es, name):
        self.h = es.enter_context(nc.semaphore(name))
        self.n = 0

    def inc(self, instr, k=1):
        instr.then_inc(self.h, k)
        self.n += k
        return self.n

    def wait(self, eng, at=None):
        eng.wait_ge(self.h, self.n if at is None else at)

    def seal(self, eng, k=1):
        d = eng.drain()
        d.then_inc(self.h, k)
        self.n += k
        return self.n


class B:

    def __init__(self, stages, dev=False):
        self.nc = bass.Bass("TRN2", target_bir_lowering=False, debug=False,
                            detect_race_conditions=False)
        self.es = ExitStack()
        self.stages = stages
        self.dev = dev
        self.d = {}
        self.sems = {}
        self._armcnt = 0
        reads = set()
        writes = set()
        for st in stages:
            r, w = STAGE_IO[st]
            reads |= set(r)
            writes |= set(w)
        produced_before = set()
        consumed_after = set()
        order = [s for gl in LAUNCH_PLAN for s in gl]
        my_idx = max(order.index(s) for s in stages)
        for st in order:
            if order.index(st) < min(order.index(s) for s in stages):
                produced_before |= set(STAGE_IO[st][1])
            if order.index(st) > my_idx:
                consumed_after |= set(STAGE_IO[st][0])
        self.input_names = set()
        self.output_names = set()
        for name, (shape, dt) in CATALOG.items():
            if name not in reads and name not in writes:
                continue
            if name in writes:
                kind = "ExternalOutput" if (name == "outT" or name in consumed_after or dev) else "Internal"
                if kind == "ExternalOutput":
                    self.output_names.add(name)
            else:
                kind = "ExternalInput"
                self.input_names.add(name)
            self.d[name] = self.nc.dram_tensor(name, list(shape), dt, kind=kind).ap()
        for name, (shape, dt) in SHARED.items():
            self.d[name] = self.nc.dram_tensor(name, list(shape), dt, kind="Internal",
                                               addr_space="Shared").ap()
        self.blk = self.es.enter_context(self.nc.Block())
        for nm in ("wload", "aload", "store", "pe", "ve", "sc", "gp", "prot", "poll"):
            self.sems[nm] = S(self.nc, self.es, "s_" + nm)

    @property
    def PE(self):
        return self.nc.tensor

    @property
    def VE(self):
        return self.nc.vector

    @property
    def SC(self):
        return self.nc.scalar

    @property
    def SY(self):
        return self.nc.sync

    @property
    def GP(self):
        return self.nc.gpsimd

    def sb(self, es, name, shape, dt):
        return es.enter_context(self.nc.sbuf_tensor(name, list(shape), dt))

    def ps(self, es, name, shape, dt=F32):
        return es.enter_context(self.nc.psum_tensor(name, list(shape), dt))

    def barrier(self):
        for eng in (self.SY, self.GP):
            for nm in ("pe", "ve", "sc"):
                eng.wait_ge(self.sems[nm].h, self.sems[nm].n)
        self.SY.wait_ge(self.sems["store"].h, self.sems["store"].n)
        self.GP.wait_ge(self.sems["store"].h, self.sems["store"].n)

    def arm_pair(self, tag, fn):
        self._armcnt += 1
        nc = self.nc
        gp = self.GP
        engt = [gp.engine]
        base = f"role{self._armcnt}_{tag}"
        for arm in nc.switch_core_id(engines=[gp], n=128):
            nc.br(f"{base}_r{arm.logical % 2}", engines=engt)
        snap = {k: v.n for k, v in self.sems.items()}
        for role in (0, 1):
            for k, v in self.sems.items():
                v.n = snap[k]
            with nc.body(f"{base}_r{role}", valid_engines=engt):
                fn(role)
                nc.br(f"{base}_join", engines=engt)
            if role == 0:
                after0 = {k: v.n for k, v in self.sems.items()}
        after1 = {k: v.n for k, v in self.sems.items()}
        assert after0 == after1, f"arm {tag}: sem counts diverge {after0} vs {after1}"
        nc.switch_bb(f"{base}_join")

    def poll(self, tag, event, role):
        nc = self.nc
        gp = self.GP
        engt = [gp.engine]
        other = 1 - role
        lbl = f"poll{self._armcnt}_{tag}_{role}"
        fl = self._flag_sb
        nn = self._nonce_sb
        cnt = self._poll_cnt
        flag_reg = self._flag_reg
        nonce_reg = self._nonce_reg
        gp.reg_load(nonce_reg, nn[event:event + 1, 0:1])
        nc.br(f"{lbl}_loop", engines=engt)
        with nc.body(f"{lbl}_loop", valid_engines=engt):
            gp.dma_start(out=fl[0:1, 0:1], in_=self.d["sh_flag"][event:event + 1, other:other + 1]
                         ).then_inc(self.sems["poll"].h, 16)
            gp.reg_add(cnt, cnt, 16)
            gp.wait_ge(self.sems["poll"].h, cnt)
            gp.reg_load(flag_reg, fl[0:1, 0:1])
            nc.br_cmp(flag_reg, nonce_reg, on_true=f"{lbl}_done", on_false=f"{lbl}_loop",
                      comp_op="IS_EQ", engines=engt)
        nc.switch_bb(f"{lbl}_done")

    def init_protocol(self, es):
        nc = self.nc
        self._flag_sb = self.sb(es, "flag_sb", (1, 1), I32)
        self._nonce_sb = self.sb(es, "nonce_sb", (18, 1), I32)
        self._poll_cnt = es.enter_context(self.GP.register("pollcnt"))
        self._flag_reg = es.enter_context(self.GP.register("flagv"))
        self._nonce_reg = es.enter_context(self.GP.register("noncev"))
        sa = self.sems["aload"]
        sa.inc(self.SY.dma_start(out=self._nonce_sb[:], in_=self.d["nonce"][:, :]), 16)
        self.GP.reg_mov(self._poll_cnt, 0)
        sa.wait(self.GP)

    def run(self, in_map_list, **kw):
        self.es.close()
        return run_bass_kernel_spmd(self.nc, in_map_list, core_ids=list(range(8)), **kw)



def emit_ln0kvq(b: B):
    nc, PE, VE, SC, SY, GP = b.nc, b.PE, b.VE, b.SC, b.SY, b.GP
    sW, sA, sO = b.sems["wload"], b.sems["aload"], b.sems["store"]
    sPe, sVe, sSc = b.sems["pe"], b.sems["ve"], b.sems["sc"]
    with ExitStack() as es:
        b.init_protocol(es)
        sumcb = b.sb(es, "sumcb", (128, 1), BF16)
        eyeb = b.sb(es, "eyeb", (128, 128), BF16)
        xc = [b.sb(es, "xc0", (128, 16, 512), F32)]
        xbf = [b.sb(es, "xbf0", (128, 16, 512), BF16)]
        x2 = b.sb(es, "x2", (128, 16, 512), BF16)
        xnb = [b.sb(es, "xnb0", (128, 16, 512), BF16)]
        tmpf = b.sb(es, "tmpf", (128, 512), F32)
        murow = b.sb(es, "murow", (1, 512), F32)
        e2row = b.sb(es, "e2row", (1, 512), F32)
        varrow = b.sb(es, "varrow", (1, 512), F32)
        ivrow = b.sb(es, "ivrow", (1, 512), F32)
        evacb = [b.sb(es, f"evacb{i}", (128, 512), BF16) for i in range(2)]
        sqtile = b.sb(es, "sqtile", (128, 512), BF16)
        vtb = [b.sb(es, f"vtb{i}", (128, 512), BF16) for i in range(2)]
        qnsl = b.sb(es, "qnsl", (1, 512), F32)
        kmax = [b.sb(es, f"kmax{g}", (1, 1), F32) for g in range(2)]
        kmt = b.sb(es, "kmt", (1, 1), F32)
        rowf = b.sb(es, "rowf", (1, T), F32)
        rowb = b.sb(es, "rowb", (1, T), BF16)
        sqbuf = b.sb(es, "sqbuf", (1, T), F32)
        kvw = [b.sb(es, f"kvw{i}", (128, 16, 128), BF16) for i in range(4)]
        qw = [b.sb(es, f"qw{i}", (128, 16, 128), BF16) for i in range(8)]

        ps_mu = b.ps(es, "ps_mu", (1, 512))
        ps_e2 = b.ps(es, "ps_e2", (1, 512))
        ps_mub = b.ps(es, "ps_mub", (128, 512))
        ps_ivb = b.ps(es, "ps_ivb", (128, 512))
        ps_p = [b.ps(es, f"ps_p{i}", (128, 512)) for i in range(2)]
        ps_tr = b.ps(es, "ps_tr", (128, 512), BF16)
        ps_nr = b.ps(es, "ps_nr", (1, 512))

        sW.inc(GP.dma_start(out=sumcb[:], in_=b.d["sumcol"][:, :]), 16)
        sW.inc(GP.dma_start(out=eyeb[:], in_=b.d["eye"][:, :]), 16)
        for ob in range(4):
            sW.inc(GP.dma_start(out=kvw[ob][:],
                                in_=b.d["wkv_t"][ob].rearrange("kb i o -> i kb o")), 16)
        for hh in range(8):
            sW.inc(GP.dma_start(out=qw[hh][:],
                                in_=b.d["wq_t"][hh].rearrange("kb i o -> i kb o")), 16)
        wready = sW.n
        VE.memset(kmax[0][:], 0.0)
        li = VE.memset(kmax[1][:], 0.0)
        sVe.seal(VE)

        for ch in range(8):
            sl = 512 * ch
            pb = 0
            if ch >= 1:
                SY.wait_ge(sPe.h, sPe.n)
                GP.wait_ge(sPe.h, sPe.n)
            la = sA.inc(SY.dma_start(
                out=xc[pb][:],
                in_=b.d["xf"][:, sl:sl + 512].rearrange("(t p) n -> p t n", p=128)), 16)
            lw = sW.inc(GP.dma_start(
                out=xbf[pb][:],
                in_=b.d["xf"][:, sl:sl + 512].rearrange("(t p) n -> p t n", p=128)), 16)
            SC.wait_ge(sW.h, lw)
            SC.wait_ge(sPe.h, sPe.n)
            for t in range(16):
                li = SC.activation(x2[:, t, :], xbf[pb][:, t, :], AF.Square)
            sSc.seal(SC)
            PE.wait_ge(sSc.h, sSc.n)
            PE.wait_ge(sVe.h, sVe.n)
            for t in range(16):
                PE.matmul(ps_mu[:], sumcb[:], xbf[pb][:, t, :], start=(t == 0), stop=(t == 15))
            for t in range(16):
                li = PE.matmul(ps_e2[:], sumcb[:], x2[:, t, :], start=(t == 0), stop=(t == 15))
            sPe.inc(li, 1)
            VE.wait_ge(sPe.h, sPe.n)
            VE.tensor_copy(murow[:], ps_mu[:])
            VE.tensor_copy(e2row[:], ps_e2[:])
            VE.drain()
            VE.tensor_mul(varrow[:], murow[:], murow[:])
            VE.drain()
            VE.tensor_sub(varrow[:], e2row[:], varrow[:])
            VE.drain()
            li = VE.tensor_scalar_add(varrow[:], varrow[:], EPS)
            sVe.seal(VE)
            SC.wait_ge(sVe.h, sVe.n)
            SC.activation(varrow[:], varrow[:], AF.Ln)
            SC.drain()
            li = SC.activation(ivrow[:], varrow[:], AF.Exp, scale=-0.5)
            sSc.seal(SC)
            PE.wait_ge(sSc.h, sSc.n)
            PE.matmul(ps_mub[:], b._ones_row_f32[:], murow[:], start=True, stop=True)
            li = PE.matmul(ps_ivb[:], b._ones_row_f32[:], ivrow[:], start=True, stop=True)
            sPe.inc(li, 1)
            VE.wait_ge(sPe.h, sPe.n)
            VE.wait_ge(sA.h, la)
            for t in range(16):
                VE.tensor_sub(tmpf[:], xc[pb][:, t, :], ps_mub[:])
                li = VE.tensor_mul(xnb[pb][:, t, :], tmpf[:], ps_ivb[:])
            sVe.seal(VE)
            SY.wait_ge(sVe.h, sVe.n)
            sO.inc(SY.dma_start(
                out=b.d["xnf"][:, sl:sl + 512].rearrange("(t p) n -> p t n", p=128),
                in_=xnb[pb][:]), 16)

            PE.wait_ge(sVe.h, sVe.n)
            PE.wait_ge(sW.h, wready)
            for ob in range(4):
                pp = ps_p[ob % 2]
                eb = evacb[ob % 2]
                PE.wait_ge(sO.h, sO.n)
                PE.wait_ge(sSc.h, sSc.n)
                for kb in range(16):
                    li = PE.matmul(pp[:], kvw[ob][:, kb, :],
                                   xnb[pb][:, kb, :], start=(kb == 0), stop=(kb == 15))
                sPe.inc(li, 1)
                SC.wait_ge(sPe.h, sPe.n)
                SC.wait_ge(sO.h, sO.n)
                li = SC.activation(eb[:], pp[:], AF.Copy)
                if ob < 2:
                    li = SC.activation(sqtile[:], pp[:], AF.Square)
                sSc.seal(SC)
                SY.wait_ge(sSc.h, sSc.n)
                if ob < 2:
                    sO.inc(SY.dma_start(
                        out=b.d["kT"][ob * 128:(ob + 1) * 128, sl:sl + 512],
                        in_=eb[:]), 16)
                    PE.wait_ge(sSc.h, sSc.n)
                    li = PE.matmul(ps_nr[:], b._ones_col_bf[:], sqtile[:], start=True, stop=True)
                    sPe.inc(li, 1)
                    VE.wait_ge(sPe.h, sPe.n)
                    VE.reduce_max(kmt[:], ps_nr[:], axis=AX)
                    VE.drain()
                    li = VE.tensor_max(kmax[ob][:], kmax[ob][:], kmt[:])
                    sVe.seal(VE)
                    PE.wait_ge(sVe.h, sVe.n)
                else:
                    vb = vtb[ob % 2]
                    PE.wait_ge(sSc.h, sSc.n)
                    for tt in range(4):
                        li = PE.transpose(ps_tr[:, tt * 128:(tt + 1) * 128],
                                          eb[:, tt * 128:(tt + 1) * 128], eyeb[:])
                    sPe.inc(li, 1)
                    VE.wait_ge(sPe.h, sPe.n)
                    VE.wait_ge(sO.h, sO.n)
                    li = VE.tensor_copy(vb[:], ps_tr[:])
                    sVe.seal(VE)
                    SY.wait_ge(sVe.h, sVe.n)
                    for tt in range(4):
                        sO.inc(SY.dma_start(
                            out=b.d["vTok"][sl + tt * 128:sl + (tt + 1) * 128,
                                            (ob - 2) * 128:(ob - 1) * 128],
                            in_=vb[:, tt * 128:(tt + 1) * 128]), 16)
                    PE.wait_ge(sVe.h, sVe.n)
            for hh in range(8):
                pp = ps_p[hh % 2]
                eb = evacb[hh % 2]
                PE.wait_ge(sO.h, sO.n)
                PE.wait_ge(sSc.h, sSc.n)
                PE.wait_ge(sVe.h, sVe.n)
                for kb in range(16):
                    li = PE.matmul(pp[:], qw[hh][:, kb, :],
                                   xnb[pb][:, kb, :], start=(kb == 0), stop=(kb == 15))
                sPe.inc(li, 1)
                SC.wait_ge(sPe.h, sPe.n)
                SC.wait_ge(sO.h, sO.n)
                SC.activation(eb[:], pp[:], AF.Copy, scale=RS128)
                li = SC.activation(sqtile[:], pp[:], AF.Square, scale=RS128)
                sSc.seal(SC)
                SY.wait_ge(sSc.h, sSc.n)
                sO.inc(SY.dma_start(
                    out=b.d["qsc"][hh * 128:(hh + 1) * 128, sl:sl + 512],
                    in_=eb[:]), 16)
                PE.wait_ge(sSc.h, sSc.n)
                li = PE.matmul(ps_nr[:], b._ones_col_bf[:], sqtile[:], start=True, stop=True)
                sPe.inc(li, 1)
                VE.wait_ge(sPe.h, sPe.n)
                VE.wait_ge(sO.h, sO.n)
                li = VE.tensor_copy(qnsl[:], ps_nr[:])
                sVe.seal(VE)
                SY.wait_ge(sVe.h, sVe.n)
                sO.inc(SY.dma_start(out=b.d["qn2"][0:1, hh * T + sl:hh * T + sl + 512],
                                    in_=qnsl[:]), 16)
                PE.wait_ge(sVe.h, sVe.n)

        SC.wait_ge(sVe.h, sVe.n)
        for g in range(2):
            li = SC.activation(kmax[g][:], kmax[g][:], AF.Sqrt)
        sSc.seal(SC)
        VE.wait_ge(sSc.h, sSc.n)
        for g in range(2):
            li = VE.tensor_scalar_mul(kmax[g][:], kmax[g][:], 1.0)
            VE.drain()
        sVe.seal(VE)
        qnrow = rowf
        for hh in range(8):
            gl = hh % 2
            SY.wait_ge(sO.h, sO.n)
            SY.wait_ge(sSc.h, sSc.n)
            la = sA.inc(SY.dma_start(out=qnrow[:], in_=b.d["qn2"][0:1, hh * T:(hh + 1) * T]), 16)
            sA.inc(SY.dma_start(out=sqbuf[:], in_=b.d["sqrow"][0:1, hh * T:(hh + 1) * T]), 16)
            SC.wait_ge(sA.h, la)
            SC.wait_ge(sVe.h, sVe.n)
            li = SC.activation(qnrow[:], qnrow[:], AF.Sqrt)
            sSc.seal(SC)
            VE.wait_ge(sSc.h, sSc.n)
            VE.tensor_scalar(qnrow[:], qnrow[:], kmax[gl][0:1, 0:1], None, OP.mult)
            VE.drain()
            VE.wait_ge(sA.h, sA.n)
            li = VE.tensor_add(qnrow[:], qnrow[:], sqbuf[:])
            sVe.seal(VE)
            SC.wait_ge(sVe.h, sVe.n)
            li = SC.activation(rowb[:], qnrow[:], AF.Copy, scale=-1.0)
            sSc.seal(SC)
            SY.wait_ge(sSc.h, sSc.n)
            sO.inc(SY.dma_start(out=b.d["rng_"][0:1, hh * T:(hh + 1) * T], in_=rowb[:]), 16)

        def a0(role):
            GP.wait_ge(sO.h, sO.n)
            b.sems["prot"].inc(GP.dma_start(
                out=b.d["xno"][:, :],
                in_=b.d["xnf"][:, role * TOK:(role + 1) * TOK]), 16)
        b.arm_pair("a0", a0)
        b.sems["prot"].wait(GP)
        SY.wait_ge(sO.h, sO.n)
        SY.wait_ge(b.sems["prot"].h, b.sems["prot"].n)


def emit_ffn0(b: B):
    nc, PE, VE, SC, SY, GP = b.nc, b.PE, b.VE, b.SC, b.SY, b.GP
    sW, sA, sO = b.sems["wload"], b.sems["aload"], b.sems["store"]
    sPe, sVe, sSc = b.sems["pe"], b.sems["ve"], b.sems["sc"]
    with ExitStack() as es:
        xq = b.sb(es, "xq", (128, 16, 512), BF16)
        acc = b.sb(es, "acc", (128, 16, 512), F32)
        hbuf = [b.sb(es, f"hb{i}", (128, 512), F32) for i in range(8)]
        hs = [b.sb(es, f"hs{i}", (128, 512), BF16) for i in range(8)]
        wgw = [b.sb(es, f"wgw{i}", (128, 16, 128), BF16) for i in range(2)]
        wuw = [b.sb(es, f"wuw{i}", (128, 16, 128), BF16) for i in range(2)]
        wdw = [b.sb(es, f"wdw{i}", (128, 16, 128), BF16) for i in range(8)]
        sg = b.sb(es, "sg", (128, 512), F32)
        af = [b.sb(es, f"af{i}", (128, 512), F32) for i in range(4)]
        cc = [b.sb(es, f"cc{i}", (128, 512), BF16) for i in range(6)]
        st = b.sb(es, "st", (128, 512), BF16)
        kp = [b.sb(es, f"kp{i}", (128, 512), BF16) for i in range(4)]
        g0t = b.sb(es, "g0t", (128, 512), F32)
        ps_g = b.ps(es, "ps_g", (128, 512))
        ps_u = b.ps(es, "ps_u", (128, 512))
        ps_w = [b.ps(es, f"ps_w{i}", (128, 512)) for i in range(6)]

        for q in range(4):
            ql = q * 512
            GP.wait_ge(sPe.h, sPe.n)
            lq = sW.inc(GP.dma_start(
                out=xq[:],
                in_=b.d["xno"][:, ql:ql + 512].rearrange("(kb p) n -> p kb n", p=128)), 16)
            for grp in range(8):
                cbs = [2 * grp + o for o in (0, 16, 32, 48)] + \
                      [2 * grp + 1 + o for o in (0, 16, 32, 48)]
                GP.wait_ge(sVe.h, sVe.n)
                GP.wait_ge(sPe.h, sPe.n)
                wdld = []
                for ci, cb in enumerate(cbs):
                    wdld.append(sW.inc(GP.dma_start(
                        out=wdw[ci][:],
                        in_=b.d["wd_t"][cb].rearrange("mb i o -> i mb o")), 16))
                for ci, cb in enumerate(cbs):
                    pb = ci % 2
                    GP.wait_ge(sPe.h, sPe.n)
                    lw = sW.inc(GP.dma_start(
                        out=wgw[pb][:],
                        in_=b.d["wg_t"][cb].rearrange("kb i o -> i kb o")), 16)
                    lw = sW.inc(GP.dma_start(
                        out=wuw[pb][:],
                        in_=b.d["wu_t"][cb].rearrange("kb i o -> i kb o")), 16)
                    PE.wait_ge(sW.h, lw)
                    PE.wait_ge(sVe.h, sVe.n)
                    PE.wait_ge(sSc.h, sSc.n)
                    for kb in range(16):
                        PE.matmul(ps_g[:], wgw[pb][:, kb, :], xq[:, kb, :],
                                  start=(kb == 0), stop=(kb == 15))
                    for kb in range(16):
                        li = PE.matmul(ps_u[:], wuw[pb][:, kb, :], xq[:, kb, :],
                                       start=(kb == 0), stop=(kb == 15))
                    sPe.inc(li, 1)
                    SC.wait_ge(sPe.h, sPe.n)
                    li = SC.activation(sg[:], ps_g[:], AF.Silu)
                    sSc.seal(SC)
                    VE.wait_ge(sSc.h, sSc.n)
                    li = VE.tensor_mul(hbuf[ci][:], sg[:], ps_u[:])
                    sVe.seal(VE)
                for qd in range(2):
                    hh4 = hbuf[qd * 4:qd * 4 + 4]
                    VE.wait_ge(sPe.h, sPe.n)
                    SC.wait_ge(sVe.h, sVe.n)
                    for j in range(4):
                        li = SC.activation(af[j][:], hh4[j][:], AF.Abs)
                    sSc.seal(SC)
                    VE.wait_ge(sSc.h, sSc.n)
                    pairs = [(0, 1), (0, 2), (0, 3), (1, 2), (1, 3), (2, 3)]
                    for ci2, (aa, bb) in enumerate(pairs):
                        VE.tensor_tensor(cc[ci2][:], af[aa][:], af[bb][:], OP.is_ge)
                    VE.tensor_add(st[:], cc[0][:], cc[1][:])
                    VE.tensor_add(st[:], st[:], cc[2][:])
                    VE.tensor_scalar(kp[0][:], st[:], 2.0, None, OP.is_ge)
                    VE.tensor_add(st[:], cc[3][:], cc[4][:])
                    VE.tensor_sub(st[:], st[:], cc[0][:])
                    VE.tensor_scalar(kp[1][:], st[:], 1.0, None, OP.is_ge)
                    VE.tensor_add(st[:], cc[1][:], cc[3][:])
                    VE.tensor_sub(st[:], cc[5][:], st[:])
                    VE.tensor_scalar(kp[2][:], st[:], 0.0, None, OP.is_ge)
                    VE.tensor_add(st[:], cc[2][:], cc[4][:])
                    VE.tensor_add(st[:], st[:], cc[5][:])
                    VE.tensor_scalar(kp[3][:], st[:], 1.0, None, OP.is_le)
                    for j in range(4):
                        li = VE.tensor_mul(hs[qd * 4 + j][:], hh4[j][:], kp[j][:])
                    sVe.seal(VE)
                PE.wait_ge(sW.h, wdld[-1])
                PE.wait_ge(sVe.h, sVe.n)
                for w0 in range(0, 16, 6):
                    ms = list(range(w0, min(w0 + 6, 16)))
                    if grp > 0 or w0 > 0:
                        PE.wait_ge(sVe.h, sVe.n)
                    for ci in range(8):
                        for mi, m in enumerate(ms):
                            li = PE.matmul(ps_w[mi][:], wdw[ci][:, m, :], hs[ci][:],
                                           start=(ci == 0), stop=(ci == 7))
                    sPe.inc(li, 1)
                    VE.wait_ge(sPe.h, sPe.n)
                    VE.wait_ge(sSc.h, sSc.n)
                    for mi, m in enumerate(ms):
                        if grp == 0:
                            li = VE.tensor_copy(acc[:, m, :], ps_w[mi][:])
                        else:
                            li = VE.tensor_add(acc[:, m, :], acc[:, m, :], ps_w[mi][:])
                    sVe.seal(VE)
            SC.wait_ge(sVe.h, sVe.n)
            for m in range(16):
                SC.wait_ge(sO.h, sO.n)
                li = SC.activation(g0t[:], acc[:, m, :], AF.Silu)
                sSc.seal(SC)
                SY.wait_ge(sSc.h, sSc.n)
                sO.inc(SY.dma_start(out=b.d["g0"][m * 128:(m + 1) * 128, ql:ql + 512],
                                    in_=g0t[:]), 16)
            SY.wait_ge(sO.h, sO.n)
        GP.wait_ge(sO.h, sO.n)


def emit_attn(b: B):
    nc, PE, VE, SC, SY, GP = b.nc, b.PE, b.VE, b.SC, b.SY, b.GP
    sW, sA, sO = b.sems["wload"], b.sems["aload"], b.sems["store"]
    sPe, sVe, sSc = b.sems["pe"], b.sems["ve"], b.sems["sc"]
    sPr = b.sems["prot"]
    with ExitStack() as es:
        b.init_protocol(es)
        aofull = b.sb(es, "aofull", (128, 16, TOK), BF16)
        with ExitStack() as es1:
            aosb = b.sb(es1, "aosb", (128, 8, T), BF16)
            ktsb = b.sb(es1, "ktsb", (128, 2, T), BF16)
            vtsb = b.sb(es1, "vtsb", (128, 32, 256), BF16)
            qsb = b.sb(es1, "qsb", (128, T), BF16)
            masksb = b.sb(es1, "masksb", (128, 4, 512), F32)
            alibsb = b.sb(es1, "alibsb", (128, 256), F32)
            rn = b.sb(es1, "rn", (1, T), BF16)
            att = [b.sb(es1, f"att{i}", (128, 512), BF16) for i in range(3)]
            denrow = b.sb(es1, "denrow", (1, 512), F32)
            avsb = b.sb(es1, "avsb", (128, 512), BF16)
            ps_s = [b.ps(es1, f"ps_s{i}", (128, 512)) for i in range(2)]
            ps_den = b.ps(es1, "ps_den", (1, 512))
            ps_av = b.ps(es1, "ps_av", (128, 512))
            ps_bc = b.ps(es1, "ps_bc", (128, 512))

            la = sA.inc(SY.dma_start(out=masksb[:], in_=b.d["maskc"].rearrange("v p n -> p v n")), 16)
            la = sA.inc(SY.dma_start(out=alibsb[:], in_=b.d["alibic"][:, :]), 16)
            lk = sW.inc(GP.dma_start(out=ktsb[:], in_=b.d["kT"].rearrange("(g p) n -> p g n", p=128)), 16)
            lk = sW.inc(GP.dma_start(out=vtsb[:], in_=b.d["vTok"].rearrange("(tb p) h -> p tb h", p=128)), 16)
            for hh in range(8):
                gl = hh % 2
                SY.wait_ge(sPe.h, sPe.n)
                lq = sA.inc(SY.dma_start(out=qsb[:], in_=b.d["qsc"][hh * 128:(hh + 1) * 128, :]), 16)
                lr = sA.inc(SY.dma_start(out=rn[:], in_=b.d["rng_"][0:1, hh * T:(hh + 1) * T]), 16)
                PE.wait_ge(sA.h, sA.n)
                PE.wait_ge(sW.h, sW.n)
                for qc in range(8):
                    nkb = 4 * (qc + 1)
                    for kb in range(nkb):
                        pb = kb % 2
                        ab = kb % 3
                        PE.wait_ge(sSc.h, sSc.n)
                        PE.matmul(ps_s[pb][:], ktsb[:, gl, kb * 128:(kb + 1) * 128],
                                  qsb[:, qc * 512:(qc + 1) * 512], start=True, stop=False)
                        li = PE.matmul(ps_s[pb][:], b._ones_row_bf[:],
                                       rn[0:1, qc * 512:(qc + 1) * 512], start=False, stop=True)
                        sPe.inc(li, 1)
                        diag = kb >= 4 * qc
                        if diag:
                            VE.wait_ge(sPe.h, sPe.n)
                            var = kb - 4 * qc
                            li = VE.tensor_add(ps_s[pb][:], ps_s[pb][:], masksb[:, var, :])
                            sVe.seal(VE)
                            SC.wait_ge(sVe.h, sVe.n)
                        SC.wait_ge(sPe.h, sPe.n)
                        li = SC.activation(att[ab][:], ps_s[pb][:], AF.Exp,
                                           bias=alibsb[:, hh * 32 + kb:hh * 32 + kb + 1])
                        sSc.seal(SC)
                        PE.wait_ge(sSc.h, sSc.n)
                        PE.matmul(ps_den[:], b._ones_col_bf[:], att[ab][:],
                                  start=(kb == 0), stop=(kb == nkb - 1))
                        li = PE.matmul(ps_av[:], vtsb[:, kb, gl * 128:(gl + 1) * 128],
                                       att[ab][:], start=(kb == 0), stop=(kb == nkb - 1))
                        sPe.inc(li, 1)
                    SC.wait_ge(sPe.h, sPe.n)
                    SC.activation(denrow[:], ps_den[:], AF.Ln)
                    SC.drain()
                    li = SC.activation(denrow[:], denrow[:], AF.Exp, scale=-1.0)
                    sSc.seal(SC)
                    PE.wait_ge(sSc.h, sSc.n)
                    li = PE.matmul(ps_bc[:], b._ones_row_f32[:], denrow[:], start=True, stop=True)
                    sPe.inc(li, 1)
                    SC.wait_ge(sVe.h, sVe.n)
                    li = SC.activation(avsb[:], ps_av[:], AF.Copy)
                    sSc.seal(SC)
                    VE.wait_ge(sPe.h, sPe.n)
                    VE.wait_ge(sSc.h, sSc.n)
                    li = VE.tensor_mul(aosb[:, hh, qc * 512:(qc + 1) * 512], avsb[:], ps_bc[:])
                    sVe.seal(VE)
                    PE.wait_ge(sVe.h, sVe.n)
            def a1(role):
                other_cols = slice((1 - role) * TOK, (2 - role) * TOK)
                GP.wait_ge(sVe.h, sVe.n)
                k = sPr.inc(GP.dma_start(
                    out=b.d["sh_ao"][role].rearrange("(hh p) n -> p hh n", p=128),
                    in_=aosb[:, :, other_cols]), 16)
                GP.wait_ge(sPr.h, k)
                sPr.inc(GP.dma_start(
                    out=b.d["sh_flag"][0:1, role:role + 1],
                    in_=b._nonce_sb[0:1, 0:1]), 16)
            b.arm_pair("a1", a1)

            def a2(role):
                own_cols = slice(role * TOK, (role + 1) * TOK)
                b.poll("a2", 0, role)
                sPr.inc(GP.dma_start(
                    out=aofull[:, 0:8, :],
                    in_=aosb[:, :, own_cols]), 16)
                sPr.inc(GP.dma_start(
                    out=aofull[:, 8:16, :],
                    in_=b.d["sh_ao"][1 - role].rearrange("(hh p) n -> p hh n", p=128)), 16)
                GP.wait_ge(sPr.h, sPr.n)
            b.arm_pair("a2", a2)
        b.barrier()
        with ExitStack() as es2:
            wosb = b.sb(es2, "wosb", (128, 16, 2048), BF16)
            bo0 = b.sb(es2, "bo0", (128, 16), F32)
            g0t = b.sb(es2, "g0t", (128, 512), F32)
            xot = b.sb(es2, "xot", (128, 512), F32)
            s_t = b.sb(es2, "s_t", (128, 512), F32)
            x1t = b.sb(es2, "x1t", (128, 512), F32)
            ps_o = [b.ps(es2, f"ps_o{i}", (128, 512)) for i in range(2)]
            lw = sW.inc(GP.dma_start(out=wosb[:], in_=b.d["wo0_t"].rearrange("kb i mb o -> i kb (mb o)")), 16)
            la = sA.inc(SY.dma_start(out=bo0[:], in_=b.d["bo0c"][:, :]), 16)
            GP.wait_ge(sPr.h, sPr.n)
            PE.wait_ge(sW.h, sW.n)
            PE.wait_ge(sA.h, sA.n)
            PE.wait_ge(sPr.h, sPr.n)
            for n in range(4):
                nl = n * 512
                for m in range(16):
                    pb = m % 2
                    PE.wait_ge(sVe.h, sVe.n)
                    for kb in range(16):
                        li = PE.matmul(ps_o[pb][:], wosb[:, kb, m * 128:(m + 1) * 128],
                                       aofull[:, kb, nl:nl + 512], start=(kb == 0), stop=(kb == 15))
                    sPe.inc(li, 1)
                    SY.wait_ge(sVe.h, sVe.n)
                    la = sA.inc(SY.dma_start(out=g0t[:], in_=b.d["g0"][m * 128:(m + 1) * 128, nl:nl + 512]), 16)
                    la = sA.inc(SY.dma_start(out=xot[:], in_=b.d["xo"][m * 128:(m + 1) * 128, nl:nl + 512]), 16)
                    VE.wait_ge(sPe.h, sPe.n)
                    VE.wait_ge(sA.h, la)
                    VE.wait_ge(sO.h, sO.n)
                    VE.tensor_scalar(s_t[:], ps_o[pb][:], bo0[:, m:m + 1], None, OP.add)
                    VE.tensor_mul(x1t[:], s_t[:], g0t[:])
                    li = VE.tensor_add(x1t[:], x1t[:], xot[:])
                    sVe.seal(VE)
                    SY.wait_ge(sVe.h, sVe.n)
                    sO.inc(SY.dma_start(out=b.d["x1"][m * 128:(m + 1) * 128, nl:nl + 512],
                                        in_=x1t[:]), 16)
            SY.wait_ge(sO.h, sO.n)
        GP.wait_ge(sO.h, sO.n)


def emit_ln1mix(b: B):
    nc, PE, VE, SC, SY, GP = b.nc, b.PE, b.VE, b.SC, b.SY, b.GP
    sW, sA, sO = b.sems["wload"], b.sems["aload"], b.sems["store"]
    sPe, sVe, sSc = b.sems["pe"], b.sems["ve"], b.sems["sc"]
    sPr = b.sems["prot"]
    with ExitStack() as es:
        b.init_protocol(es)
        sumcb = b.sb(es, "sumcb", (128, 1), BF16)
        cols = b.sb(es, "cols", (128, 8, 16), F32)
        bcol = b.sb(es, "bcol", (128, 16), BF16)
        mcol = b.sb(es, "mcol", (128, 1), F32)
        sW.inc(GP.dma_start(out=sumcb[:], in_=b.d["sumcol"][:, :]), 16)
        for i, nm in enumerate(("bkc", "bvc", "brc", "uc", "tmkc", "tmvc", "tmrc")):
            sA.inc(SY.dma_start(out=cols[:, i, :], in_=b.d[nm][:, :]), 16)
        sA.inc(SY.dma_start(out=mcol[:], in_=b.d["maskcol"][:, :]), 16)
        with ExitStack() as es1:
            xc = b.sb(es1, "xc", (128, 16, 512), F32)
            x2 = b.sb(es1, "x2", (128, 16, 512), BF16)
            xbf = b.sb(es1, "xbf", (128, 16, 512), BF16)
            xnb = b.sb(es1, "xnb", (128, 16, 512), BF16)
            tmpf = b.sb(es1, "tmpf", (128, 512), F32)
            murow = b.sb(es1, "murow", (1, 512), F32)
            e2row = b.sb(es1, "e2row", (1, 512), F32)
            varrow = b.sb(es1, "varrow", (1, 512), F32)
            ivrow = b.sb(es1, "ivrow", (1, 512), F32)
            ps_mu = b.ps(es1, "ps_mu", (1, 512))
            ps_e2 = b.ps(es1, "ps_e2", (1, 512))
            ps_mub = b.ps(es1, "ps_mub", (128, 512))
            ps_ivb = b.ps(es1, "ps_ivb", (128, 512))
            for ch in range(4):
                sl = 512 * ch
                if ch >= 1:
                    SY.wait_ge(sVe.h, sVe.n)
                    GP.wait_ge(sPe.h, sPe.n)
                la = sA.inc(SY.dma_start(
                    out=xc[:], in_=b.d["x1"][:, sl:sl + 512].rearrange("(t p) n -> p t n", p=128)), 16)
                lw = sW.inc(GP.dma_start(
                    out=xbf[:], in_=b.d["x1"][:, sl:sl + 512].rearrange("(t p) n -> p t n", p=128)), 16)
                SC.wait_ge(sW.h, lw)
                SC.wait_ge(sPe.h, sPe.n)
                for t in range(16):
                    li = SC.activation(x2[:, t, :], xbf[:, t, :], AF.Square)
                sSc.seal(SC)
                PE.wait_ge(sSc.h, sSc.n)
                PE.wait_ge(sW.h, sW.n)
                PE.wait_ge(sVe.h, sVe.n)
                for t in range(16):
                    PE.matmul(ps_mu[:], sumcb[:], xbf[:, t, :], start=(t == 0), stop=(t == 15))
                for t in range(16):
                    li = PE.matmul(ps_e2[:], sumcb[:], x2[:, t, :], start=(t == 0), stop=(t == 15))
                sPe.inc(li, 1)
                VE.wait_ge(sPe.h, sPe.n)
                VE.tensor_copy(murow[:], ps_mu[:])
                VE.tensor_copy(e2row[:], ps_e2[:])
                VE.drain()
                VE.tensor_mul(varrow[:], murow[:], murow[:])
                VE.drain()
                VE.tensor_sub(varrow[:], e2row[:], varrow[:])
                VE.drain()
                li = VE.tensor_scalar_add(varrow[:], varrow[:], EPS)
                sVe.seal(VE)
                SC.wait_ge(sVe.h, sVe.n)
                SC.activation(varrow[:], varrow[:], AF.Ln)
                SC.drain()
                li = SC.activation(ivrow[:], varrow[:], AF.Exp, scale=-0.5)
                sSc.seal(SC)
                PE.wait_ge(sSc.h, sSc.n)
                PE.matmul(ps_mub[:], b._ones_row_f32[:], murow[:], start=True, stop=True)
                li = PE.matmul(ps_ivb[:], b._ones_row_f32[:], ivrow[:], start=True, stop=True)
                sPe.inc(li, 1)
                VE.wait_ge(sPe.h, sPe.n)
                VE.wait_ge(sA.h, la)
                VE.wait_ge(sO.h, sO.n)
                for t in range(16):
                    VE.tensor_sub(tmpf[:], xc[:, t, :], ps_mub[:])
                    li = VE.tensor_mul(xnb[:, t, :], tmpf[:], ps_ivb[:])
                sVe.seal(VE)
                SY.wait_ge(sVe.h, sVe.n)
                sO.inc(SY.dma_start(
                    out=b.d["xn1"][:, sl:sl + 512].rearrange("(t p) n -> p t n", p=128),
                    in_=xnb[:]), 16)
        def a3(role):
            GP.wait_ge(sO.h, sO.n)
            k = sPr.inc(GP.dma_start(
                out=b.d["sh_bcol"][role],
                in_=b.d["xn1"][:, TOK - 1:TOK].rearrange("(t p) a -> p (t a)", p=128),
                allow_slow_non_contiguous=True), 16)
            GP.wait_ge(sPr.h, k)
            k = sPr.inc(GP.dma_start(
                out=b.d["sh_flag"][1:2, role:role + 1],
                in_=b._nonce_sb[1:2, 0:1]), 16)
            b.poll("a3", 1, role)
            k = sPr.inc(GP.dma_start(out=bcol[:], in_=b.d["sh_bcol"][1 - role]), 16)
            GP.wait_ge(sPr.h, k)
        b.arm_pair("a3", a3)

        b.barrier()
        with ExitStack() as es2:
            wp = b.sb(es2, "wp", (128, 16, 2048), BF16)
            xn1c = b.sb(es2, "xn1c", (128, 16, 512), BF16)
            xsc = b.sb(es2, "xsc", (128, 16, 512), BF16)
            mixc = b.sb(es2, "mixc", (128, 16, 512), BF16)
            evac = [b.sb(es2, f"pevac{i}", (128, 512), BF16) for i in range(2)]
            bku = b.sb(es2, "bku", (128, 1), F32)
            ps_p = [b.ps(es2, f"ps_p{i}", (128, 512)) for i in range(2)]
            VE.wait_ge(sPr.h, sPr.n)
            for pi, (wnm, onm, tmi, bi) in enumerate((
                    ("wk_t", "kt1", 4, 0), ("wv_t", "v1", 5, 1), ("wr_t", "r1", 6, 2))):
                GP.wait_ge(sPe.h, sPe.n)
                lw = sW.inc(GP.dma_start(out=wp[:], in_=b.d[wnm].rearrange("kb i mb o -> i kb (mb o)")), 16)
                for n in range(4):
                    nl = n * 512
                    SY.wait_ge(sVe.h, sVe.n)
                    la = sA.inc(SY.dma_start(
                        out=xn1c[:],
                        in_=b.d["xn1"][:, nl:nl + 512].rearrange("(t p) n -> p t n", p=128)), 16)
                    if n == 0:
                        la = sA.inc(SY.dma_start(
                            out=xsc[:, :, 1:512],
                            in_=b.d["xn1"][:, 0:511].rearrange("(t p) n -> p t n", p=128)), 16)
                    else:
                        la = sA.inc(SY.dma_start(
                            out=xsc[:],
                            in_=b.d["xn1"][:, nl - 1:nl + 511].rearrange("(t p) n -> p t n", p=128)), 16)
                    VE.wait_ge(sA.h, la)
                    if n == 0:
                        for t in range(16):
                            VE.tensor_scalar(xsc[:, t, 0:1], bcol[:, t:t + 1], mcol[:, 0:1], None, OP.mult)
                    for t in range(16):
                        VE.tensor_sub(mixc[:, t, :], xn1c[:, t, :], xsc[:, t, :])
                        li = VE.scalar_tensor_tensor(
                            mixc[:, t, :], mixc[:, t, :], cols[:, tmi, t:t + 1],
                            xsc[:, t, :], OP.mult, OP.add)
                    sVe.seal(VE)
                    if pi == 2:
                        SC.wait_ge(sVe.h, sVe.n)
                        for t in range(16):
                            li = SC.activation(mixc[:, t, :], mixc[:, t, :], AF.Silu)
                        sSc.seal(SC)
                        PE.wait_ge(sSc.h, sSc.n)
                    PE.wait_ge(sVe.h, sVe.n)
                    PE.wait_ge(sW.h, lw)
                    for m in range(16):
                        pb = m % 2
                        PE.wait_ge(sSc.h, sSc.n)
                        PE.wait_ge(sVe.h, sVe.n)
                        for kb in range(16):
                            li = PE.matmul(ps_p[pb][:], wp[:, kb, m * 128:(m + 1) * 128],
                                           mixc[:, kb, :], start=(kb == 0), stop=(kb == 15))
                        sPe.inc(li, 1)
                        eb = evac[pb]
                        if pi == 0:
                            VE.tensor_tensor(bku[:], cols[:, 0, m:m + 1], cols[:, 3, m:m + 1], OP.add)
                        else:
                            VE.tensor_scalar(bku[:], cols[:, bi, m:m + 1], 0.0, None, OP.add)
                        VE.drain()
                        VE.wait_ge(sPe.h, sPe.n)
                        VE.wait_ge(sO.h, sO.n)
                        li = VE.tensor_scalar(eb[:], ps_p[pb][:], bku[:, 0:1], None, OP.add)
                        sVe.seal(VE)
                        SY.wait_ge(sVe.h, sVe.n)
                        sO.inc(SY.dma_start(out=b.d[onm][m * 128:(m + 1) * 128, nl:nl + 512],
                                            in_=eb[:]), 16)
        SY.wait_ge(sO.h, sO.n)
        GP.wait_ge(sO.h, sO.n)


def emit_wkv(b: B):
    nc, PE, VE, SC, SY, GP = b.nc, b.PE, b.VE, b.SC, b.SY, b.GP
    sW, sA, sO = b.sems["wload"], b.sems["aload"], b.sems["store"]
    sPe, sVe, sSc = b.sems["pe"], b.sems["ve"], b.sems["sc"]
    sPr = b.sems["prot"]
    with ExitStack() as es:
        b.init_protocol(es)
        kt = b.sb(es, "kt", (128, TOK), BF16)
        vt = b.sb(es, "vt", (128, TOK), BF16)
        rt = b.sb(es, "rt", (128, TOK), BF16)
        tau = b.sb(es, "tau", (128, TOK), F32)
        e1 = b.sb(es, "e1", (128, TOK), F32)
        e2 = b.sb(es, "e2", (128, TOK), F32)
        e2v = b.sb(es, "e2v", (128, TOK), F32)
        ba = b.sb(es, "ba", (128, TOK), F32)
        bb = b.sb(es, "bb", (128, TOK), F32)
        al = b.sb(es, "al", (128, TOK), F32)
        zz = b.sb(es, "zz", (128, TOK), F32)
        rwk = b.sb(es, "rwk", (128, TOK), BF16)
        wcol = b.sb(es, "wcol", (128, 16), F32)
        mcol = b.sb(es, "mcol", (128, 1), F32)
        pkt = b.sb(es, "pkt", (128, 2), F32)
        rcv = b.sb(es, "rcv", (128, 2), F32)
        a0c = b.sb(es, "a0c", (128, 2), F32)
        sA.inc(SY.dma_start(out=wcol[:], in_=b.d["wc"][:, :]), 16)
        sA.inc(SY.dma_start(out=mcol[:], in_=b.d["maskcol"][:, :]), 16)
        VE.wait_ge(sA.h, sA.n)
        li = VE.memset(zz[:], 0.0)
        sVe.seal(VE)
        for dt in range(16):
            rs = slice(dt * 128, (dt + 1) * 128)
            SY.wait_ge(sVe.h, sVe.n)
            la = sA.inc(SY.dma_start(out=kt[:], in_=b.d["kt1"][rs, :]), 16)
            la = sA.inc(SY.dma_start(out=vt[:], in_=b.d["v1"][rs, :]), 16)
            la = sA.inc(SY.dma_start(out=rt[:], in_=b.d["r1"][rs, :]), 16)
            VE.wait_ge(sA.h, la)
            VE.tensor_scalar(tau[:], kt[:], wcol[:, dt:dt + 1], None, OP.max)
            VE.drain()
            li = VE.tensor_scalar(e1[:], tau[:], wcol[:, dt:dt + 1], None, OP.add)
            sVe.seal(VE)
            SC.wait_ge(sVe.h, sVe.n)
            li = SC.activation(e1[:], e1[:], AF.Exp, scale=-1.0)
            sSc.seal(SC)
            VE.wait_ge(sSc.h, sSc.n)
            li = VE.tensor_sub(e2[:], kt[:], tau[:])
            sVe.seal(VE)
            SC.wait_ge(sVe.h, sVe.n)
            li = SC.activation(e2[:], e2[:], AF.Exp)
            sSc.seal(SC)
            VE.wait_ge(sSc.h, sSc.n)
            VE.tensor_mul(e2v[:], e2[:], vt[:])
            VE.tensor_tensor_scan(ba[:], e1[:], e2v[:], 0.0, OP.mult, OP.add)
            VE.tensor_tensor_scan(bb[:], e1[:], e2[:], 0.0, OP.mult, OP.add)
            VE.tensor_tensor_scan(al[:], e1[:], zz[:], 1.0, OP.mult, OP.add)
            VE.tensor_copy(pkt[:, 0:1], ba[:, TOK - 1:TOK])
            li = VE.tensor_copy(pkt[:, 1:2], bb[:, TOK - 1:TOK])
            sVe.seal(VE)

            def a4(role, _dt=dt):
                GP.wait_ge(sVe.h, sVe.n)
                k = sPr.inc(GP.dma_start(
                    out=b.d["sh_carry"][role][:, 2 * _dt:2 * _dt + 2], in_=pkt[:]), 16)
                GP.wait_ge(sPr.h, k)
                k = sPr.inc(GP.dma_start(
                    out=b.d["sh_flag"][2 + _dt:3 + _dt, role:role + 1],
                    in_=b._nonce_sb[2 + _dt:3 + _dt, 0:1]), 16)
                b.poll(f"a4_{_dt}", 2 + _dt, role)
                k = sPr.inc(GP.dma_start(
                    out=rcv[:], in_=b.d["sh_carry"][1 - role][:, 2 * _dt:2 * _dt + 2]), 16)
                GP.wait_ge(sPr.h, k)
            b.arm_pair(f"a4_{dt}", a4)
            VE.wait_ge(sPr.h, sPr.n)
            VE.tensor_scalar(a0c[:], rcv[:], mcol[:, 0:1], None, OP.mult)
            VE.drain()
            VE.scalar_tensor_tensor(ba[:], al[:], a0c[:, 0:1], ba[:], OP.mult, OP.add)
            li = VE.scalar_tensor_tensor(bb[:], al[:], a0c[:, 1:2], bb[:], OP.mult, OP.add)
            sVe.seal(VE)
            SC.wait_ge(sVe.h, sVe.n)
            SC.activation(bb[:], bb[:], AF.Ln)
            SC.drain()
            li = SC.activation(bb[:], bb[:], AF.Exp, scale=-1.0)
            sSc.seal(SC)
            VE.wait_ge(sSc.h, sSc.n)
            VE.tensor_mul(ba[:], ba[:], bb[:])
            li = VE.tensor_mul(rwk[:], ba[:], rt[:])
            sVe.seal(VE)
            SY.wait_ge(sVe.h, sVe.n)
            sO.inc(SY.dma_start(out=b.d["rwkv"][rs, :], in_=rwk[:]), 16)
        SY.wait_ge(sO.h, sO.n)
        GP.wait_ge(sO.h, sO.n)


def emit_moe(b: B):
    nc, PE, VE, SC, SY, GP = b.nc, b.PE, b.VE, b.SC, b.SY, b.GP
    sW, sA, sO = b.sems["wload"], b.sems["aload"], b.sems["store"]
    sPe, sVe, sSc = b.sems["pe"], b.sems["ve"], b.sems["sc"]
    with ExitStack() as es:
        with ExitStack() as es1:
            wp = b.sb(es1, "wp", (128, 16, 2048), BF16)
            rwc = b.sb(es1, "rwc", (128, 16, 512), BF16)
            bo1 = b.sb(es1, "bo1", (128, 16), F32)
            s1t = b.sb(es1, "s1t", (128, 512), F32)
            ps_p = [b.ps(es1, f"ps_p{i}", (128, 512)) for i in range(2)]
            lw = sW.inc(GP.dma_start(out=wp[:], in_=b.d["wo1_t"].rearrange("kb i mb o -> i kb (mb o)")), 16)
            la = sA.inc(SY.dma_start(out=bo1[:], in_=b.d["bo1c"][:, :]), 16)
            for n in range(4):
                nl = n * 512
                SY.wait_ge(sPe.h, sPe.n)
                lr = sA.inc(SY.dma_start(
                    out=rwc[:], in_=b.d["rwkv"][:, nl:nl + 512].rearrange("(t p) n -> p t n", p=128)), 16)
                PE.wait_ge(sA.h, sA.n)
                PE.wait_ge(sW.h, lw)
                for m in range(16):
                    pb = m % 2
                    PE.wait_ge(sVe.h, sVe.n)
                    for kb in range(16):
                        li = PE.matmul(ps_p[pb][:], wp[:, kb, m * 128:(m + 1) * 128],
                                       rwc[:, kb, :], start=(kb == 0), stop=(kb == 15))
                    sPe.inc(li, 1)
                    VE.wait_ge(sPe.h, sPe.n)
                    VE.wait_ge(sO.h, sO.n)
                    li = VE.tensor_scalar(s1t[:], ps_p[pb][:], bo1[:, m:m + 1], None, OP.add)
                    sVe.seal(VE)
                    SY.wait_ge(sVe.h, sVe.n)
                    sO.inc(SY.dma_start(out=b.d["s1"][m * 128:(m + 1) * 128, nl:nl + 512],
                                        in_=s1t[:]), 16)
        b.barrier()
        with ExitStack() as es2:
            wgr = b.sb(es2, "wgr", (128, 16, 8), BF16)
            eyef = b.sb(es2, "eyef", (128, 128), F32)
            xnc = b.sb(es2, "xnc", (128, 16, 512), BF16)
            lgf = b.sb(es2, "lgf", (8, 512), F32)
            lgt = b.sb(es2, "lgt", (128, 8), F32)
            m1 = b.sb(es2, "m1", (128, 1), F32)
            nm1 = b.sb(es2, "nm1", (128, 1), F32)
            e8 = b.sb(es2, "e8", (128, 8), F32)
            eqm = b.sb(es2, "eqm", (128, 8), F32)
            msk8 = b.sb(es2, "msk8", (128, 8), F32)
            m2 = b.sb(es2, "m2", (128, 1), F32)
            kp8 = b.sb(es2, "kp8", (128, 8), F32)
            w8 = b.sb(es2, "w8", (128, 8), F32)
            den8 = b.sb(es2, "den8", (128, 1), F32)
            c8 = b.sb(es2, "c8", (128, 8), F32)
            c8p = b.sb(es2, "c8p", (128, 32), F32)
            negc = b.sb(es2, "negc", (128, 1), F32)
            c8t = b.sb(es2, "c8t", (8, TOK), F32)
            ps_l = b.ps(es2, "ps_l", (8, 512))
            ps_t = b.ps(es2, "ps_t", (128, 128))
            lw = sW.inc(GP.dma_start(out=wgr[:], in_=b.d["wgr_t"].rearrange("kb i e -> i kb e")), 16)
            la = sA.inc(SY.dma_start(out=eyef[:], in_=b.d["eye"][:, :]), 16)
            VE.memset(c8p[:], 0.0)
            li = VE.memset(negc[:], NEG)
            sVe.seal(VE)
            for n in range(4):
                nl = n * 512
                SY.wait_ge(sPe.h, sPe.n)
                lx = sA.inc(SY.dma_start(
                    out=xnc[:], in_=b.d["xn1"][:, nl:nl + 512].rearrange("(t p) n -> p t n", p=128)), 16)
                PE.wait_ge(sA.h, sA.n)
                PE.wait_ge(sW.h, lw)
                PE.wait_ge(sVe.h, sVe.n)
                for kb in range(16):
                    li = PE.matmul(ps_l[:], wgr[:, kb, :], xnc[:, kb, :],
                                   start=(kb == 0), stop=(kb == 15))
                sPe.inc(li, 1)
                VE.wait_ge(sPe.h, sPe.n)
                li = VE.tensor_copy(lgf[:], ps_l[:])
                sVe.seal(VE)
                for j in range(4):
                    tok0 = nl + j * 128
                    PE.wait_ge(sVe.h, sVe.n)
                    li = PE.transpose(ps_t[:, 0:8], lgf[0:8, j * 128:(j + 1) * 128], eyef[0:8, 0:8])
                    sPe.inc(li, 1)
                    VE.wait_ge(sPe.h, sPe.n)
                    VE.tensor_copy(lgt[:], ps_t[:, 0:8])
                    VE.drain()
                    VE.reduce_max(m1[:], lgt[:], axis=AX)
                    VE.drain()
                    li = VE.tensor_scalar(nm1[:], m1[:], -1.0, None, OP.mult)
                    sVe.seal(VE)
                    SC.wait_ge(sVe.h, sVe.n)
                    li = SC.activation(e8[:], lgt[:], AF.Exp, bias=nm1[:, 0:1])
                    sSc.seal(SC)
                    VE.wait_ge(sSc.h, sSc.n)
                    VE.tensor_scalar(eqm[:], lgt[:], m1[:, 0:1], None, OP.is_equal)
                    VE.drain()
                    VE.scalar_tensor_tensor(msk8[:], eqm[:], negc[:, 0:1], lgt[:], OP.mult, OP.add)
                    VE.drain()
                    VE.reduce_max(m2[:], msk8[:], axis=AX)
                    VE.drain()
                    VE.tensor_scalar(kp8[:], lgt[:], m2[:, 0:1], None, OP.is_ge)
                    VE.drain()
                    VE.tensor_mul(w8[:], e8[:], kp8[:])
                    VE.drain()
                    li = VE.reduce_sum(den8[:], w8[:], axis=AX)
                    sVe.seal(VE)
                    SC.wait_ge(sVe.h, sVe.n)
                    SC.activation(den8[:], den8[:], AF.Ln)
                    SC.drain()
                    li = SC.activation(den8[:], den8[:], AF.Exp, scale=-1.0)
                    sSc.seal(SC)
                    VE.wait_ge(sSc.h, sSc.n)
                    VE.tensor_scalar(c8[:], w8[:], den8[:, 0:1], None, OP.mult)
                    VE.drain()
                    li = VE.tensor_copy(c8p[:, 0:8], c8[:])
                    sVe.seal(VE)
                    PE.wait_ge(sVe.h, sVe.n)
                    li = PE.transpose(ps_t[0:32, 0:128], c8p[:], eyef[:])
                    sPe.inc(li, 1)
                    VE.wait_ge(sPe.h, sPe.n)
                    li = VE.tensor_copy(c8t[0:8, tok0:tok0 + 128], ps_t[0:8, 0:128])
                    sVe.seal(VE)
            SY.wait_ge(sVe.h, sVe.n)
            sO.inc(SY.dma_start(out=b.d["crow"][:, :], in_=c8t[:]), 16)
            SY.wait_ge(sO.h, sO.n)
        b.barrier()
        with ExitStack() as es3:
            xnc = [b.sb(es3, f"xnc{i}", (128, 16, 512), BF16) for i in range(2)]
            w1a = [b.sb(es3, f"w1a{i}", (128, 16, 128), BF16) for i in range(2)]
            w1b = [b.sb(es3, f"w1b{i}", (128, 16, 128), BF16) for i in range(2)]
            w2sb = b.sb(es3, "w2sb", (128, 11, 2048), BF16)
            hseb = b.sb(es3, "hseb", (128, 11, 1024), BF16)
            acc = b.sb(es3, "acc", (128, 16, 1024), F32)
            crowb = b.sb(es3, "crowb", (1, 1024), F32)
            sg = b.sb(es3, "sg", (128, 512), F32)
            hpp = b.sb(es3, "hpp", (128, 512), F32)
            g1t = b.sb(es3, "g1t", (128, 512), F32)
            x1t = b.sb(es3, "x1t", (128, 512), F32)
            s1t = b.sb(es3, "s1t3", (128, 512), F32)
            outt = b.sb(es3, "outt", (128, 512), F32)
            ps_a = b.ps(es3, "ps_a", (128, 512))
            ps_b = b.ps(es3, "ps_b", (128, 512))
            ps_c = [b.ps(es3, f"ps_c{i}", (128, 512)) for i in range(2)]
            ps_w = [b.ps(es3, f"ps_w{i}", (128, 512)) for i in range(4)]
            for hf in range(2):
                hl = hf * 1024
                for e in range(NE):
                    SY.wait_ge(sPe.h, sPe.n)
                    lc = sA.inc(SY.dma_start(out=crowb[:], in_=b.d["crow"][e:e + 1, hl:hl + 1024]), 16)
                    GP.wait_ge(sPe.h, sPe.n)
                    lx = [0, 0]
                    for n2 in range(2):
                        lx[n2] = sW.inc(GP.dma_start(
                            out=xnc[n2][:],
                            in_=b.d["xn1"][:, hl + n2 * 512:hl + (n2 + 1) * 512]
                                 .rearrange("(t p) n -> p t n", p=128)), 16)
                    lw2 = sW.inc(GP.dma_start(
                        out=w2sb[:], in_=b.d["w2_t"][e].rearrange("kb i mb o -> i kb (mb o)")), 16)
                    for mp in range(11):
                        pb = mp % 2
                        GP.wait_ge(sPe.h, sPe.n)
                        la1 = sW.inc(GP.dma_start(
                            out=w1a[pb][:], in_=b.d["w1_t"][e, :, mp].rearrange("kb i o -> i kb o")), 16)
                        la2 = sW.inc(GP.dma_start(
                            out=w1b[pb][:], in_=b.d["w1_t"][e, :, mp + 11].rearrange("kb i o -> i kb o")), 16)
                        for n2 in range(2):
                            PE.wait_ge(sW.h, la2)
                            PE.wait_ge(sVe.h, sVe.n)
                            PE.wait_ge(sSc.h, sSc.n)
                            for kb in range(16):
                                PE.matmul(ps_a[:], w1a[pb][:, kb, :], xnc[n2][:, kb, :],
                                          start=(kb == 0), stop=(kb == 15))
                            for kb in range(16):
                                li = PE.matmul(ps_b[:], w1b[pb][:, kb, :], xnc[n2][:, kb, :],
                                               start=(kb == 0), stop=(kb == 15))
                            if mp == 0 and n2 == 0:
                                PE.wait_ge(sA.h, lc)
                            if mp == 0:
                                li = PE.matmul(ps_c[n2][:], b._ones_row_f32[:],
                                               crowb[0:1, n2 * 512:(n2 + 1) * 512],
                                               start=True, stop=True)
                            sPe.inc(li, 1)
                            SC.wait_ge(sPe.h, sPe.n)
                            li = SC.activation(sg[:], ps_a[:], AF.Silu)
                            sSc.seal(SC)
                            VE.wait_ge(sSc.h, sSc.n)
                            VE.tensor_mul(hpp[:], sg[:], ps_b[:])
                            li = VE.tensor_mul(hseb[:, mp, n2 * 512:(n2 + 1) * 512], hpp[:], ps_c[n2][:])
                            sVe.seal(VE)
                    PE.wait_ge(sW.h, lw2)
                    PE.wait_ge(sVe.h, sVe.n)
                    tiles = [(m, n2) for m in range(16) for n2 in range(2)]
                    for w0 in range(0, 32, 4):
                        wts = tiles[w0:w0 + 4]
                        PE.wait_ge(sVe.h, sVe.n)
                        for kb in range(11):
                            for wi, (m, n2) in enumerate(wts):
                                li = PE.matmul(ps_w[wi][:], w2sb[:, kb, m * 128:(m + 1) * 128],
                                               hseb[:, kb, n2 * 512:(n2 + 1) * 512],
                                               start=(kb == 0), stop=(kb == 10))
                        sPe.inc(li, 1)
                        VE.wait_ge(sPe.h, sPe.n)
                        if e == 0:
                            VE.wait_ge(sSc.h, sSc.n)
                        for wi, (m, n2) in enumerate(wts):
                            if e == 0:
                                li = VE.tensor_copy(acc[:, m, n2 * 512:(n2 + 1) * 512], ps_w[wi][:])
                            else:
                                li = VE.tensor_add(acc[:, m, n2 * 512:(n2 + 1) * 512],
                                                   acc[:, m, n2 * 512:(n2 + 1) * 512], ps_w[wi][:])
                        sVe.seal(VE)
                for m in range(16):
                    for n2 in range(2):
                        nl = hl + n2 * 512
                        SC.wait_ge(sVe.h, sVe.n)
                        SC.wait_ge(sO.h, sO.n)
                        li = SC.activation(g1t[:], acc[:, m, n2 * 512:(n2 + 1) * 512], AF.Silu)
                        sSc.seal(SC)
                        SY.wait_ge(sO.h, sO.n)
                        la = sA.inc(SY.dma_start(out=x1t[:], in_=b.d["x1"][m * 128:(m + 1) * 128, nl:nl + 512]), 16)
                        la = sA.inc(SY.dma_start(out=s1t[:], in_=b.d["s1"][m * 128:(m + 1) * 128, nl:nl + 512]), 16)
                        VE.wait_ge(sA.h, la)
                        VE.wait_ge(sSc.h, sSc.n)
                        VE.wait_ge(sO.h, sO.n)
                        VE.tensor_mul(outt[:], s1t[:], g1t[:])
                        li = VE.tensor_add(outt[:], outt[:], x1t[:])
                        sVe.seal(VE)
                        SY.wait_ge(sVe.h, sVe.n)
                        sO.inc(SY.dma_start(out=b.d["outT"][m * 128:(m + 1) * 128, nl:nl + 512],
                                            in_=outt[:]), 16)
        SY.wait_ge(sO.h, sO.n)
        GP.wait_ge(sO.h, sO.n)


def _emit_common_consts(b, es):
    b._ones_row_f32 = b.sb(es, "ones_row_f32", (1, 128), F32)
    b._ones_col_bf = b.sb(es, "ones_col_bf", (128, 1), BF16)
    b._ones_row_bf = b.sb(es, "ones_row_bf", (1, 128), BF16)
    b.GP.memset(b._ones_row_f32[:], 1.0)
    b.GP.memset(b._ones_row_bf[:], 1.0)
    li = b.GP.memset(b._ones_col_bf[:], 1.0)
    sGp = b.sems["gp"]
    sGp.inc(li, 1)
    for eng in (b.PE, b.VE, b.SC):
        eng.wait_ge(sGp.h, sGp.n)



def build_launch(stages, dev=False):
    b = B(stages, dev=dev)
    with ExitStack() as ges:
        b._ges = ges
        _emit_common_consts(b, ges)
        for st in stages:
            EMITTERS[st](b)
    return b


EMITTERS = {
    "ln0kvq": emit_ln0kvq,
    "ffn0": emit_ffn0,
    "attn": emit_attn,
    "ln1mix": emit_ln1mix,
    "wkv": emit_wkv,
    "moe": emit_moe,
}


def kernel(**inputs):
    per_core = _stage_all(inputs)
    arrs = [dict(pc) for pc in per_core]
    for stages in LAUNCH_PLAN:
        b = build_launch(stages)
        needed = set()
        for st in stages:
            needed |= set(STAGE_IO[st][0])
            needed |= set(STAGE_IO[st][1])
        in_maps = []
        for c in range(8):
            in_maps.append({name: arrs[c][name] for name in b.input_names})
        res = b.run(in_maps)
        for c in range(8):
            for k, v in res.results[c].items():
                arrs[c][k] = v
    out = np.empty((NB, T, D), np.float32)
    for c in range(8):
        role, batch = c % 2, c // 2
        out[batch, role * TOK:(role + 1) * TOK, :] = np.asarray(arrs[c]["outT"], np.float32).T
    return out


# version 07
# speedup vs baseline: 1.1035x; 1.1035x over previous
import math
from contextlib import ExitStack

import numpy as np
import ml_dtypes

import concourse.bass as bass
import concourse.mybir as mybir
from concourse.bass_utils import run_bass_kernel_spmd

F32 = mybir.dt.float32
BF16 = mybir.dt.bfloat16
I32 = mybir.dt.int32
AX = mybir.AxisListType.X
OP = mybir.AluOpType
AF = mybir.ActivationFunctionType

D = 2048
T = 4096
TOK = 2048
NB = 4
H = 16
HD = 128
KVH = 4
I_FFN = 8192
FFN1 = 1408
NE = 8
EPS = 1e-5
RS128 = 1.0 / math.sqrt(128.0)
NEG = -1e30

SLOPES = [2.0 ** (-(h + 1) / 2.0) for h in range(H)]


def _heads(role):
    return [h for h in range(H) if h % 4 in (2 * role, 2 * role + 1)]



def _tiles2(w, kb, mb):
    return np.ascontiguousarray(
        w.reshape(kb, 128, mb, 128).transpose(0, 2, 1, 3))


def _colblock(v):
    return np.ascontiguousarray(v.reshape(16, 128).T)


def _stage_core(inputs, core):
    role = core % 2
    batch = core // 2
    mine = _heads(role)
    theirs = _heads(1 - role)
    groups = (2 * role, 2 * role + 1)

    d = {}
    xb = np.asarray(inputs["L0_attn_x"][batch], np.float32)
    xT = np.ascontiguousarray(xb.T)
    d["xf"] = xT
    d["xo"] = np.ascontiguousarray(xT[:, role * TOK:(role + 1) * TOK])

    wq = np.asarray(inputs["L0_attn_Wq"], np.float32)
    d["wq_t"] = np.stack([_tiles2(wq[:, h * 128:(h + 1) * 128], 16, 1)[:, 0]
                          for h in mine])
    wkv = np.asarray(inputs["L0_attn_Wkv"], np.float32)
    cols = [wkv[:, g * 128:(g + 1) * 128] for g in groups] + \
           [wkv[:, 512 + g * 128:512 + (g + 1) * 128] for g in groups]
    d["wkv_t"] = np.stack([_tiles2(c, 16, 1)[:, 0] for c in cols])

    wo = np.asarray(inputs["L0_attn_Wo"], np.float32)
    perm = []
    for h in mine + theirs:
        perm.extend(range(h * 128, (h + 1) * 128))
    d["wo0_t"] = np.ascontiguousarray(_tiles2(wo[perm, :], 16, 16).transpose(0, 2, 1, 3))
    d["bo0c"] = _colblock(np.asarray(inputs["L0_attn_bo"], np.float32))

    wg = np.asarray(inputs["L0_ffn_w_gate"], np.float32)
    wu = np.asarray(inputs["L0_ffn_w_up"], np.float32)
    wd = np.asarray(inputs["L0_ffn_w_down"], np.float32)
    wg_p = np.ascontiguousarray(wg.reshape(D, I_FFN // 4, 4).transpose(0, 2, 1).reshape(D, I_FFN))
    wu_p = np.ascontiguousarray(wu.reshape(D, I_FFN // 4, 4).transpose(0, 2, 1).reshape(D, I_FFN))
    wd_p = np.ascontiguousarray(wd.reshape(I_FFN // 4, 4, D).transpose(1, 0, 2).reshape(I_FFN, D))
    d["wg_t"] = np.ascontiguousarray(_tiles2(wg_p, 16, 64).transpose(1, 0, 2, 3))
    d["wu_t"] = np.ascontiguousarray(_tiles2(wu_p, 16, 64).transpose(1, 0, 2, 3))
    d["wd_t"] = _tiles2(wd_p, 64, 16)

    for nm, key in (("wk_t", "L1_attn_Wk"), ("wv_t", "L1_attn_Wv"),
                    ("wr_t", "L1_attn_Wr"), ("wo1_t", "L1_attn_Wo")):
        w = np.ascontiguousarray(np.asarray(inputs[key], np.float32).T)
        d[nm] = np.ascontiguousarray(_tiles2(w, 16, 16).transpose(0, 2, 1, 3))
    for nm, key in (("bkc", "L1_attn_bk"), ("bvc", "L1_attn_bv"),
                    ("brc", "L1_attn_br"), ("bo1c", "L1_attn_bo"),
                    ("uc", "L1_attn_u"), ("wc", "L1_attn_w"),
                    ("tmkc", "L1_attn_tmk"), ("tmvc", "L1_attn_tmv"),
                    ("tmrc", "L1_attn_tmr")):
        d[nm] = _colblock(np.asarray(inputs[key], np.float32))

    w1 = np.asarray(inputs["L1_ffn_w1"], np.float32)
    w2 = np.asarray(inputs["L1_ffn_w2"], np.float32)
    d["w1_t"] = np.stack([_tiles2(w1[e], 16, 22) for e in range(NE)])
    d["w2_t"] = np.stack([np.ascontiguousarray(_tiles2(w2[e], 11, 16).transpose(0, 2, 1, 3))
                          for e in range(NE)])
    wgr = np.asarray(inputs["L1_ffn_wg"], np.float32)
    d["wgr_t"] = np.ascontiguousarray(wgr.T.reshape(16, 128, 8))

    masks = np.zeros((4, 128, 512), np.float32)
    for si, s in enumerate((0, 128, 256, 384)):
        i = np.arange(128)[:, None]
        j = np.arange(512)[None, :]
        masks[si] = np.where(j >= i + s, 0.0, NEG)
    d["maskc"] = masks
    alib = np.zeros((128, 256), np.float32)
    for hh, h in enumerate(mine):
        for kb in range(32):
            alib[:, hh * 32 + kb] = SLOPES[h] * (kb * 128 + np.arange(128))
    d["alibic"] = alib
    sqr = np.zeros((1, 8 * T), np.float32)
    for hh, h in enumerate(mine):
        sqr[0, hh * T:(hh + 1) * T] = SLOPES[h] * np.arange(T)
    d["sqrow"] = sqr
    d["eye"] = np.eye(128, dtype=np.float32)
    d["sumcol"] = np.full((128, 1), 1.0 / D, np.float32)
    d["onecol"] = np.ones((128, 1), np.float32)
    d["maskcol"] = np.full((128, 1), float(role), np.float32)
    return d


def _stage_all(inputs):
    per_core = [_stage_core(inputs, c) for c in range(8)]
    rng = np.random.default_rng()
    for k in range(4):
        nonce = rng.integers(1 << 16, (1 << 31) - 1, size=(18, 1)).astype(np.int32)
        per_core[2 * k]["nonce"] = nonce
        per_core[2 * k + 1]["nonce"] = nonce
    return per_core


CATALOG = {
    "xf": ((D, T), F32), "xo": ((D, TOK), F32),
    "wq_t": ((8, 16, 128, 128), F32), "wkv_t": ((4, 16, 128, 128), F32),
    "wo0_t": ((16, 128, 16, 128), F32), "bo0c": ((128, 16), F32),
    "wg_t": ((64, 16, 128, 128), F32), "wu_t": ((64, 16, 128, 128), F32),
    "wd_t": ((64, 16, 128, 128), F32),
    "wk_t": ((16, 128, 16, 128), F32), "wv_t": ((16, 128, 16, 128), F32),
    "wr_t": ((16, 128, 16, 128), F32), "wo1_t": ((16, 128, 16, 128), F32),
    "bkc": ((128, 16), F32), "bvc": ((128, 16), F32), "brc": ((128, 16), F32),
    "bo1c": ((128, 16), F32), "uc": ((128, 16), F32), "wc": ((128, 16), F32),
    "tmkc": ((128, 16), F32), "tmvc": ((128, 16), F32), "tmrc": ((128, 16), F32),
    "w1_t": ((8, 16, 22, 128, 128), F32), "w2_t": ((8, 11, 128, 16, 128), F32),
    "wgr_t": ((16, 128, 8), F32),
    "maskc": ((4, 128, 512), F32), "alibic": ((128, 256), F32),
    "sqrow": ((1, 8 * T), F32), "eye": ((128, 128), F32),
    "sumcol": ((128, 1), F32), "onecol": ((128, 1), F32),
    "maskcol": ((128, 1), F32), "nonce": ((18, 1), I32),
    "xnf": ((D, T), BF16), "xno": ((D, TOK), BF16),
    "kT": ((256, T), BF16), "vTok": ((T, 256), BF16),
    "qsc": ((1024, T), BF16), "rng_": ((1, 8 * T), BF16),
    "qn2": ((1, 8 * T), F32),
    "g0": ((D, TOK), F32), "x1": ((D, TOK), F32),
    "xn1": ((D, TOK), BF16),
    "kt1": ((D, TOK), BF16), "v1": ((D, TOK), BF16), "r1": ((D, TOK), BF16),
    "rwkv": ((D, TOK), BF16), "s1": ((D, TOK), F32), "crow": ((8, TOK), F32),
    "outT": ((D, TOK), F32),
}

SHARED = {
    "sh_ao": ((2, 1024, TOK), BF16),
    "sh_bcol": ((2, 128, 16), BF16),
    "sh_carry": ((2, 128, 32), F32),
    "sh_flag": ((18, 2), I32),
}

STAGE_IO = {
    "ln0kvq": (["xf", "wkv_t", "wq_t", "sumcol", "onecol", "eye", "sqrow", "nonce"],
               ["xnf", "xno", "kT", "vTok", "qsc", "rng_", "qn2"]),
    "ffn0": (["xno", "wg_t", "wu_t", "wd_t"], ["g0"]),
    "attn": (["kT", "vTok", "qsc", "rng_", "maskc", "alibic", "onecol",
              "wo0_t", "bo0c", "xo", "g0", "nonce"], ["x1"]),
    "ln1mix": (["x1", "sumcol", "onecol", "wk_t", "wv_t", "wr_t", "bkc", "bvc", "brc",
                "uc", "tmkc", "tmvc", "tmrc", "maskcol", "nonce"],
               ["xn1", "kt1", "v1", "r1"]),
    "wkv": (["kt1", "v1", "r1", "wc", "maskcol", "nonce"], ["rwkv"]),
    "moe": (["rwkv", "wo1_t", "bo1c", "xn1", "x1", "wgr_t", "w1_t", "w2_t",
             "onecol", "eye"], ["s1", "outT", "crow"]),
}

LAUNCH_PLAN = [["ln0kvq", "ffn0", "attn", "ln1mix", "wkv", "moe"]]


class S:

    def __init__(self, nc, es, name):
        self.h = es.enter_context(nc.semaphore(name))
        self.n = 0

    def inc(self, instr, k=1):
        instr.then_inc(self.h, k)
        self.n += k
        return self.n

    def wait(self, eng, at=None):
        eng.wait_ge(self.h, self.n if at is None else at)

    def seal(self, eng, k=1):
        d = eng.drain()
        d.then_inc(self.h, k)
        self.n += k
        return self.n


class B:

    def __init__(self, stages, dev=False):
        self.nc = bass.Bass("TRN2", target_bir_lowering=False, debug=False,
                            detect_race_conditions=False)
        self.es = ExitStack()
        self.stages = stages
        self.dev = dev
        self.d = {}
        self.sems = {}
        self._armcnt = 0
        reads = set()
        writes = set()
        for st in stages:
            r, w = STAGE_IO[st]
            reads |= set(r)
            writes |= set(w)
        produced_before = set()
        consumed_after = set()
        order = [s for gl in LAUNCH_PLAN for s in gl]
        my_idx = max(order.index(s) for s in stages)
        for st in order:
            if order.index(st) < min(order.index(s) for s in stages):
                produced_before |= set(STAGE_IO[st][1])
            if order.index(st) > my_idx:
                consumed_after |= set(STAGE_IO[st][0])
        self.input_names = set()
        self.output_names = set()
        for name, (shape, dt) in CATALOG.items():
            if name not in reads and name not in writes:
                continue
            if name in writes:
                kind = "ExternalOutput" if (name == "outT" or name in consumed_after or dev) else "Internal"
                if kind == "ExternalOutput":
                    self.output_names.add(name)
            else:
                kind = "ExternalInput"
                self.input_names.add(name)
            self.d[name] = self.nc.dram_tensor(name, list(shape), dt, kind=kind).ap()
        for name, (shape, dt) in SHARED.items():
            self.d[name] = self.nc.dram_tensor(name, list(shape), dt, kind="Internal",
                                               addr_space="Shared").ap()
        self.blk = self.es.enter_context(self.nc.Block())
        for nm in ("wload", "aload", "store", "pe", "ve", "sc", "gp", "prot", "poll"):
            self.sems[nm] = S(self.nc, self.es, "s_" + nm)

    @property
    def PE(self):
        return self.nc.tensor

    @property
    def VE(self):
        return self.nc.vector

    @property
    def SC(self):
        return self.nc.scalar

    @property
    def SY(self):
        return self.nc.sync

    @property
    def GP(self):
        return self.nc.gpsimd

    def sb(self, es, name, shape, dt):
        self._uid = getattr(self, "_uid", 0) + 1
        return es.enter_context(self.nc.sbuf_tensor(f"{name}_u{self._uid}", list(shape), dt))

    def ps(self, es, name, shape, dt=F32):
        self._uid = getattr(self, "_uid", 0) + 1
        return es.enter_context(self.nc.psum_tensor(f"{name}_u{self._uid}", list(shape), dt))

    def barrier(self):
        for eng in (self.SY, self.GP):
            for nm in ("pe", "ve", "sc"):
                eng.wait_ge(self.sems[nm].h, self.sems[nm].n)
        self.SY.wait_ge(self.sems["store"].h, self.sems["store"].n)
        self.GP.wait_ge(self.sems["store"].h, self.sems["store"].n)

    def arm_pair(self, tag, fn):
        self._armcnt += 1
        nc = self.nc
        gp = self.GP
        engt = [gp.engine]
        base = f"role{self._armcnt}_{tag}"
        for arm in nc.switch_core_id(engines=[gp], n=128):
            nc.br(f"{base}_r{arm.logical % 2}", engines=engt)
        snap = {k: v.n for k, v in self.sems.items()}
        for role in (0, 1):
            for k, v in self.sems.items():
                v.n = snap[k]
            with nc.body(f"{base}_r{role}", valid_engines=engt):
                fn(role)
                nc.br(f"{base}_join", engines=engt)
            if role == 0:
                after0 = {k: v.n for k, v in self.sems.items()}
        after1 = {k: v.n for k, v in self.sems.items()}
        assert after0 == after1, f"arm {tag}: sem counts diverge {after0} vs {after1}"
        nc.switch_bb(f"{base}_join")

    def poll(self, tag, event, role):
        nc = self.nc
        gp = self.GP
        engt = [gp.engine]
        other = 1 - role
        lbl = f"poll{self._armcnt}_{tag}_{role}"
        fl = self._flag_sb
        nn = self._nonce_sb
        cnt = self._poll_cnt
        flag_reg = self._flag_reg
        nonce_reg = self._nonce_reg
        gp.reg_load(nonce_reg, nn[event:event + 1, 0:1])
        nc.br(f"{lbl}_loop", engines=engt)
        with nc.body(f"{lbl}_loop", valid_engines=engt):
            gp.dma_start(out=fl[0:1, 0:1], in_=self.d["sh_flag"][event:event + 1, other:other + 1]
                         ).then_inc(self.sems["poll"].h, 16)
            gp.reg_add(cnt, cnt, 16)
            gp.wait_ge(self.sems["poll"].h, cnt)
            gp.reg_load(flag_reg, fl[0:1, 0:1])
            nc.br_cmp(flag_reg, nonce_reg, on_true=f"{lbl}_done", on_false=f"{lbl}_loop",
                      comp_op="IS_EQ", engines=engt)
        nc.switch_bb(f"{lbl}_done")

    def init_protocol(self, es):
        if getattr(self, "_prot_ready", False):
            return
        self._prot_ready = True
        es = self._ges
        nc = self.nc
        self._flag_sb = self.sb(es, "flag_sb", (1, 1), I32)
        self._nonce_sb = self.sb(es, "nonce_sb", (18, 1), I32)
        self._poll_cnt = es.enter_context(self.GP.register("pollcnt"))
        self._flag_reg = es.enter_context(self.GP.register("flagv"))
        self._nonce_reg = es.enter_context(self.GP.register("noncev"))
        sa = self.sems["aload"]
        sa.inc(self.SY.dma_start(out=self._nonce_sb[:], in_=self.d["nonce"][:, :]), 16)
        self.GP.reg_mov(self._poll_cnt, 0)
        sa.wait(self.GP)

    def run(self, in_map_list, **kw):
        self.es.close()
        return run_bass_kernel_spmd(self.nc, in_map_list, core_ids=list(range(8)), **kw)



def emit_ln0kvq(b: B):
    nc, PE, VE, SC, SY, GP = b.nc, b.PE, b.VE, b.SC, b.SY, b.GP
    sW, sA, sO = b.sems["wload"], b.sems["aload"], b.sems["store"]
    sPe, sVe, sSc = b.sems["pe"], b.sems["ve"], b.sems["sc"]
    with ExitStack() as es:
        b.init_protocol(es)
        sumcb = b.sb(es, "sumcb", (128, 1), BF16)
        eyeb = b.sb(es, "eyeb", (128, 128), BF16)
        xc = [b.sb(es, "xc0", (128, 16, 512), F32)]
        xbf = [b.sb(es, "xbf0", (128, 16, 512), BF16)]
        x2 = b.sb(es, "x2", (128, 16, 512), BF16)
        xnb = [b.sb(es, "xnb0", (128, 16, 512), BF16)]
        tmpf = b.sb(es, "tmpf", (128, 512), F32)
        murow = b.sb(es, "murow", (1, 512), F32)
        e2row = b.sb(es, "e2row", (1, 512), F32)
        varrow = b.sb(es, "varrow", (1, 512), F32)
        ivrow = b.sb(es, "ivrow", (1, 512), F32)
        evacb = [b.sb(es, f"evacb{i}", (128, 512), BF16) for i in range(2)]
        sqt2 = [b.sb(es, f"sqt{i}", (128, 512), BF16) for i in range(2)]
        vtb = [b.sb(es, f"vtb{i}", (128, 512), BF16) for i in range(2)]
        qnsl = b.sb(es, "qnsl", (1, 512), F32)
        kmax = [b.sb(es, f"kmax{g}", (1, 1), F32) for g in range(2)]
        kmt = b.sb(es, "kmt", (1, 1), F32)
        rowf = b.sb(es, "rowf", (1, T), F32)
        rowb = b.sb(es, "rowb", (1, T), BF16)
        sqbuf = b.sb(es, "sqbuf", (1, T), F32)
        kvw = [b.sb(es, f"kvw{i}", (128, 16, 128), BF16) for i in range(4)]
        qw = [b.sb(es, f"qw{i}", (128, 16, 128), BF16) for i in range(8)]

        ps_mu = b.ps(es, "ps_mu", (1, 512))
        ps_e2 = b.ps(es, "ps_e2", (1, 512))
        ps_mub = b.ps(es, "ps_mub", (128, 512))
        ps_ivb = b.ps(es, "ps_ivb", (128, 512))
        ps_p = [b.ps(es, f"ps_p{i}", (128, 512)) for i in range(2)]
        _pstr = b.ps(es, "ps_tr", (128, 512), BF16)
        ps_tr2 = [_pstr, _pstr]
        ps_nr = b.ps(es, "ps_nr", (1, 512))
        prj_evac = [0, 0]
        prj_store = [0, 0]
        sq_use = [0, 0]
        eb_use = [0, 0]
        nr_evac = 0
        nr_done = 0
        tr_evac = [0, 0]
        vt_store = [0, 0]
        qn_store = 0

        sW.inc(GP.dma_start(out=sumcb[:], in_=b.d["sumcol"][:, :]), 16)
        sW.inc(GP.dma_start(out=eyeb[:], in_=b.d["eye"][:, :]), 16)
        for ob in range(4):
            sW.inc(GP.dma_start(out=kvw[ob][:],
                                in_=b.d["wkv_t"][ob].rearrange("kb i o -> i kb o")), 16)
        for hh in range(8):
            sW.inc(GP.dma_start(out=qw[hh][:],
                                in_=b.d["wq_t"][hh].rearrange("kb i o -> i kb o")), 16)
        wready = sW.n
        VE.memset(kmax[0][:], 0.0)
        li = VE.memset(kmax[1][:], 0.0)
        sVe.seal(VE)

        for ch in range(8):
            sl = 512 * ch
            pb = 0
            if ch >= 1:
                SY.wait_ge(sPe.h, sPe.n)
                GP.wait_ge(sPe.h, sPe.n)
            la = sA.inc(SY.dma_start(
                out=xc[pb][:],
                in_=b.d["xf"][:, sl:sl + 512].rearrange("(t p) n -> p t n", p=128)), 16)
            lw = sW.inc(GP.dma_start(
                out=xbf[pb][:],
                in_=b.d["xf"][:, sl:sl + 512].rearrange("(t p) n -> p t n", p=128)), 16)
            SC.wait_ge(sW.h, lw)
            SC.wait_ge(sPe.h, sPe.n)
            for t in range(16):
                li = SC.activation(x2[:, t, :], xbf[pb][:, t, :], AF.Square)
            sSc.seal(SC)
            PE.wait_ge(sSc.h, sSc.n)
            PE.wait_ge(sVe.h, sVe.n)
            for t in range(16):
                PE.matmul(ps_mu[:], sumcb[:], xbf[pb][:, t, :], start=(t == 0), stop=(t == 15))
            for t in range(16):
                li = PE.matmul(ps_e2[:], sumcb[:], x2[:, t, :], start=(t == 0), stop=(t == 15))
            sPe.inc(li, 1)
            VE.wait_ge(sPe.h, sPe.n)
            VE.tensor_copy(murow[:], ps_mu[:])
            VE.tensor_copy(e2row[:], ps_e2[:])
            VE.drain()
            VE.tensor_mul(varrow[:], murow[:], murow[:])
            VE.drain()
            VE.tensor_sub(varrow[:], e2row[:], varrow[:])
            VE.drain()
            li = VE.tensor_scalar_add(varrow[:], varrow[:], EPS)
            sVe.seal(VE)
            SC.wait_ge(sVe.h, sVe.n)
            SC.activation(varrow[:], varrow[:], AF.Ln)
            SC.drain()
            li = SC.activation(ivrow[:], varrow[:], AF.Exp, scale=-0.5)
            sSc.seal(SC)
            PE.wait_ge(sSc.h, sSc.n)
            PE.matmul(ps_mub[:], b._ones_row_f32[:], murow[:], start=True, stop=True)
            li = PE.matmul(ps_ivb[:], b._ones_row_f32[:], ivrow[:], start=True, stop=True)
            sPe.inc(li, 1)
            VE.wait_ge(sPe.h, sPe.n)
            VE.wait_ge(sA.h, la)
            for t in range(16):
                VE.tensor_sub(tmpf[:], xc[pb][:, t, :], ps_mub[:])
                li = VE.tensor_mul(xnb[pb][:, t, :], tmpf[:], ps_ivb[:])
            sVe.seal(VE)
            SY.wait_ge(sVe.h, sVe.n)
            sO.inc(SY.dma_start(
                out=b.d["xnf"][:, sl:sl + 512].rearrange("(t p) n -> p t n", p=128),
                in_=xnb[pb][:]), 16)

            PE.wait_ge(sVe.h, sVe.n)
            PE.wait_ge(sW.h, wready)
            for ob in range(4):
                pp = ps_p[ob % 2]
                eb = evacb[ob % 2]
                sq = sqt2[ob % 2]
                PE.wait_ge(sSc.h, prj_evac[ob % 2])
                for kb in range(16):
                    li = PE.matmul(pp[:], kvw[ob][:, kb, :],
                                   xnb[pb][:, kb, :], start=(kb == 0), stop=(kb == 15))
                mm_done = sPe.inc(li, 1)
                SC.wait_ge(sPe.h, mm_done)
                SC.wait_ge(sO.h, prj_store[ob % 2])
                SC.wait_ge(sPe.h, max(sq_use[ob % 2], eb_use[ob % 2]))
                li = SC.activation(eb[:], pp[:], AF.Copy)
                if ob < 2:
                    li = SC.activation(sq[:], pp[:], AF.Square)
                prj_evac[ob % 2] = sSc.seal(SC)
                SY.wait_ge(sSc.h, prj_evac[ob % 2])
                if ob < 2:
                    prj_store[ob % 2] = sO.inc(SY.dma_start(
                        out=b.d["kT"][ob * 128:(ob + 1) * 128, sl:sl + 512],
                        in_=eb[:]), 16)
                    PE.wait_ge(sSc.h, prj_evac[ob % 2])
                    PE.wait_ge(sVe.h, nr_evac)
                    li = PE.matmul(ps_nr[:], b._ones_col_bf[:], sq[:], start=True, stop=True)
                    nr_done = sPe.inc(li, 1)
                    sq_use[ob % 2] = nr_done
                    VE.wait_ge(sPe.h, nr_done)
                    VE.reduce_max(kmt[:], ps_nr[:], axis=AX)
                    VE.drain()
                    li = VE.tensor_max(kmax[ob][:], kmax[ob][:], kmt[:])
                    nr_evac = sVe.seal(VE)
                else:
                    vb = vtb[ob % 2]
                    PE.wait_ge(sSc.h, prj_evac[ob % 2])
                    PE.wait_ge(sVe.h, max(tr_evac))
                    for tt in range(4):
                        li = PE.transpose(ps_tr2[ob % 2][:, tt * 128:(tt + 1) * 128],
                                          eb[:, tt * 128:(tt + 1) * 128], eyeb[:])
                    tr_done = sPe.inc(li, 1)
                    eb_use[ob % 2] = tr_done
                    VE.wait_ge(sPe.h, tr_done)
                    VE.wait_ge(sO.h, vt_store[ob % 2])
                    li = VE.tensor_copy(vb[:], ps_tr2[ob % 2][:])
                    tr_evac[ob % 2] = sVe.seal(VE)
                    SY.wait_ge(sVe.h, tr_evac[ob % 2])
                    for tt in range(4):
                        vt_store[ob % 2] = sO.inc(SY.dma_start(
                            out=b.d["vTok"][sl + tt * 128:sl + (tt + 1) * 128,
                                            (ob - 2) * 128:(ob - 1) * 128],
                            in_=vb[:, tt * 128:(tt + 1) * 128]), 16)
            for hh in range(8):
                pp = ps_p[hh % 2]
                eb = evacb[hh % 2]
                sq = sqt2[hh % 2]
                PE.wait_ge(sSc.h, prj_evac[hh % 2])
                for kb in range(16):
                    li = PE.matmul(pp[:], qw[hh][:, kb, :],
                                   xnb[pb][:, kb, :], start=(kb == 0), stop=(kb == 15))
                mm_done = sPe.inc(li, 1)
                SC.wait_ge(sPe.h, mm_done)
                SC.wait_ge(sO.h, prj_store[hh % 2])
                SC.wait_ge(sPe.h, sq_use[hh % 2])
                SC.activation(eb[:], pp[:], AF.Copy, scale=RS128)
                li = SC.activation(sq[:], pp[:], AF.Square, scale=RS128)
                prj_evac[hh % 2] = sSc.seal(SC)
                SY.wait_ge(sSc.h, prj_evac[hh % 2])
                prj_store[hh % 2] = sO.inc(SY.dma_start(
                    out=b.d["qsc"][hh * 128:(hh + 1) * 128, sl:sl + 512],
                    in_=eb[:]), 16)
                PE.wait_ge(sSc.h, prj_evac[hh % 2])
                PE.wait_ge(sVe.h, nr_evac)
                li = PE.matmul(ps_nr[:], b._ones_col_bf[:], sq[:], start=True, stop=True)
                nr_done = sPe.inc(li, 1)
                sq_use[hh % 2] = nr_done
                VE.wait_ge(sPe.h, nr_done)
                VE.wait_ge(sO.h, qn_store)
                li = VE.tensor_copy(qnsl[:], ps_nr[:])
                nr_evac = sVe.seal(VE)
                SY.wait_ge(sVe.h, nr_evac)
                qn_store = sO.inc(SY.dma_start(out=b.d["qn2"][0:1, hh * T + sl:hh * T + sl + 512],
                                               in_=qnsl[:]), 16)
        SC.wait_ge(sVe.h, sVe.n)
        for g in range(2):
            li = SC.activation(kmax[g][:], kmax[g][:], AF.Sqrt)
        sSc.seal(SC)
        VE.wait_ge(sSc.h, sSc.n)
        for g in range(2):
            li = VE.tensor_scalar_mul(kmax[g][:], kmax[g][:], 1.0)
            VE.drain()
        sVe.seal(VE)
        qnrow = rowf
        for hh in range(8):
            gl = hh % 2
            SY.wait_ge(sO.h, sO.n)
            SY.wait_ge(sSc.h, sSc.n)
            la = sA.inc(SY.dma_start(out=qnrow[:], in_=b.d["qn2"][0:1, hh * T:(hh + 1) * T]), 16)
            sA.inc(SY.dma_start(out=sqbuf[:], in_=b.d["sqrow"][0:1, hh * T:(hh + 1) * T]), 16)
            SC.wait_ge(sA.h, la)
            SC.wait_ge(sVe.h, sVe.n)
            li = SC.activation(qnrow[:], qnrow[:], AF.Sqrt)
            sSc.seal(SC)
            VE.wait_ge(sSc.h, sSc.n)
            VE.tensor_scalar(qnrow[:], qnrow[:], kmax[gl][0:1, 0:1], None, OP.mult)
            VE.drain()
            VE.wait_ge(sA.h, sA.n)
            li = VE.tensor_add(qnrow[:], qnrow[:], sqbuf[:])
            sVe.seal(VE)
            SC.wait_ge(sVe.h, sVe.n)
            li = SC.activation(rowb[:], qnrow[:], AF.Copy, scale=-1.0)
            sSc.seal(SC)
            SY.wait_ge(sSc.h, sSc.n)
            sO.inc(SY.dma_start(out=b.d["rng_"][0:1, hh * T:(hh + 1) * T], in_=rowb[:]), 16)

        def a0(role):
            GP.wait_ge(sO.h, sO.n)
            b.sems["prot"].inc(GP.dma_start(
                out=b.d["xno"][:, :],
                in_=b.d["xnf"][:, role * TOK:(role + 1) * TOK]), 16)
        b.arm_pair("a0", a0)
        b.sems["prot"].wait(GP)
        SY.wait_ge(sO.h, sO.n)
        SY.wait_ge(b.sems["prot"].h, b.sems["prot"].n)


def emit_ffn0(b: B):
    nc, PE, VE, SC, SY, GP = b.nc, b.PE, b.VE, b.SC, b.SY, b.GP
    sW, sA, sO = b.sems["wload"], b.sems["aload"], b.sems["store"]
    sPe, sVe, sSc = b.sems["pe"], b.sems["ve"], b.sems["sc"]
    with ExitStack() as es:
        xq = [b.sb(es, f"xq{i}", (128, 16, 512), BF16) for i in range(2)]
        acc = b.sb(es, "acc", (128, 16, 512), F32)
        hbuf = [b.sb(es, f"hb{i}", (128, 512), F32) for i in range(8)]
        hs = [b.sb(es, f"hs{i}", (128, 512), BF16) for i in range(8)]
        wgw = [b.sb(es, f"wgw{i}", (128, 16, 128), BF16) for i in range(2)]
        wuw = [b.sb(es, f"wuw{i}", (128, 16, 128), BF16) for i in range(2)]
        wdw = [b.sb(es, f"wdw{i}", (128, 16, 128), BF16) for i in range(8)]
        sg = [b.sb(es, f"sg{i}", (128, 512), F32) for i in range(2)]
        af = [b.sb(es, f"af{i}", (128, 512), F32) for i in range(4)]
        cc = [b.sb(es, f"cc{i}", (128, 512), BF16) for i in range(6)]
        st = b.sb(es, "st", (128, 512), BF16)
        kp = [b.sb(es, f"kp{i}", (128, 512), BF16) for i in range(4)]
        g0t = [b.sb(es, f"g0t{i}", (128, 512), F32) for i in range(2)]
        ps_g = [b.ps(es, f"ps_g{i}", (128, 512)) for i in range(2)]
        ps_u = [b.ps(es, f"ps_u{i}", (128, 512)) for i in range(2)]
        ps_w = [b.ps(es, f"ps_w{i}", (128, 512)) for i in range(4)]

        xq_use = [0, 0]
        slab_use = [0, 0]
        gu_evac = [0, 0]
        wd_use = [0] * 8
        hs_use = 0
        win_evac = [0, 0]
        g0_store = [0, 0]
        acc_free = 0

        for q in range(4):
            ql = q * 512
            qb = q % 2
            GP.wait_ge(sPe.h, xq_use[qb])
            lq = sW.inc(GP.dma_start(
                out=xq[qb][:],
                in_=b.d["xno"][:, ql:ql + 512].rearrange("(kb p) n -> p kb n", p=128)), 16)
            for grp in range(8):
                cbs = [2 * grp + o for o in (0, 16, 32, 48)] + \
                      [2 * grp + 1 + o for o in (0, 16, 32, 48)]
                wdld = []
                for ci, cb in enumerate(cbs):
                    GP.wait_ge(sPe.h, wd_use[ci])
                    wdld.append(sW.inc(GP.dma_start(
                        out=wdw[ci][:],
                        in_=b.d["wd_t"][cb].rearrange("mb i o -> i mb o")), 16))
                mm_done = [0] * 8
                for ci, cb in enumerate(cbs):
                    pb = ci % 2
                    GP.wait_ge(sPe.h, slab_use[pb])
                    sW.inc(GP.dma_start(
                        out=wgw[pb][:],
                        in_=b.d["wg_t"][cb].rearrange("kb i o -> i kb o")), 16)
                    lw = sW.inc(GP.dma_start(
                        out=wuw[pb][:],
                        in_=b.d["wu_t"][cb].rearrange("kb i o -> i kb o")), 16)
                    PE.wait_ge(sW.h, lw)
                    if ci == 0:
                        PE.wait_ge(sW.h, lq)
                    PE.wait_ge(sVe.h, gu_evac[pb])
                    for kb in range(16):
                        PE.matmul(ps_g[pb][:], wgw[pb][:, kb, :], xq[qb][:, kb, :],
                                  start=(kb == 0), stop=(kb == 15))
                    for kb in range(16):
                        li = PE.matmul(ps_u[pb][:], wuw[pb][:, kb, :], xq[qb][:, kb, :],
                                       start=(kb == 0), stop=(kb == 15))
                    mm_done[ci] = sPe.inc(li, 1)
                    slab_use[pb] = mm_done[ci]
                    SC.wait_ge(sPe.h, mm_done[ci])
                    li = SC.activation(sg[pb][:], ps_g[pb][:], AF.Silu)
                    sc_done = sSc.seal(SC)
                    VE.wait_ge(sSc.h, sc_done)
                    VE.wait_ge(sPe.h, hs_use)
                    li = VE.tensor_mul(hbuf[ci][:], sg[pb][:], ps_u[pb][:])
                    gu_evac[pb] = sVe.seal(VE)
                xq_use[qb] = mm_done[7]
                for qd in range(2):
                    hh4 = hbuf[qd * 4:qd * 4 + 4]
                    SC.wait_ge(sVe.h, sVe.n)
                    for j in range(4):
                        li = SC.activation(af[j][:], hh4[j][:], AF.Abs)
                    sSc.seal(SC)
                    VE.wait_ge(sSc.h, sSc.n)
                    VE.wait_ge(sPe.h, hs_use)
                    pairs = [(0, 1), (0, 2), (0, 3), (1, 2), (1, 3), (2, 3)]
                    for ci2, (aa, bb) in enumerate(pairs):
                        VE.tensor_tensor(cc[ci2][:], af[aa][:], af[bb][:], OP.is_ge)
                    VE.drain()
                    VE.tensor_add(st[:], cc[0][:], cc[1][:])
                    VE.drain()
                    VE.tensor_add(st[:], st[:], cc[2][:])
                    VE.drain()
                    VE.tensor_scalar(kp[0][:], st[:], 2.0, None, OP.is_ge)
                    VE.tensor_add(st[:], cc[3][:], cc[4][:])
                    VE.drain()
                    VE.tensor_sub(st[:], st[:], cc[0][:])
                    VE.drain()
                    VE.tensor_scalar(kp[1][:], st[:], 1.0, None, OP.is_ge)
                    VE.tensor_add(st[:], cc[1][:], cc[3][:])
                    VE.drain()
                    VE.tensor_sub(st[:], cc[5][:], st[:])
                    VE.drain()
                    VE.tensor_scalar(kp[2][:], st[:], 0.0, None, OP.is_ge)
                    VE.tensor_add(st[:], cc[2][:], cc[4][:])
                    VE.drain()
                    VE.tensor_add(st[:], st[:], cc[5][:])
                    VE.drain()
                    VE.tensor_scalar(kp[3][:], st[:], 1.0, None, OP.is_le)
                    VE.drain()
                    for j in range(4):
                        li = VE.tensor_mul(hs[qd * 4 + j][:], hh4[j][:], kp[j][:])
                    sVe.seal(VE)
                hs_ready = sVe.n
                PE.wait_ge(sW.h, wdld[-1])
                PE.wait_ge(sVe.h, hs_ready)
                for wi0, w0 in enumerate(range(0, 16, 2)):
                    ms = [w0, w0 + 1]
                    bs = (wi0 % 2) * 2
                    PE.wait_ge(sVe.h, win_evac[wi0 % 2])
                    for ci in range(8):
                        for mi, m in enumerate(ms):
                            li = PE.matmul(ps_w[bs + mi][:], wdw[ci][:, m, :], hs[ci][:],
                                           start=(ci == 0), stop=(ci == 7))
                    wd_done = sPe.inc(li, 1)
                    VE.wait_ge(sPe.h, wd_done)
                    if grp == 0:
                        VE.wait_ge(sSc.h, acc_free)
                    for mi, m in enumerate(ms):
                        if grp == 0:
                            li = VE.tensor_copy(acc[:, m, :], ps_w[bs + mi][:])
                        else:
                            li = VE.tensor_add(acc[:, m, :], acc[:, m, :], ps_w[bs + mi][:])
                    win_evac[wi0 % 2] = sVe.seal(VE)
                for ci in range(8):
                    wd_use[ci] = sPe.n
                hs_use = sPe.n
            SC.wait_ge(sVe.h, sVe.n)
            for m in range(16):
                gb = m % 2
                SC.wait_ge(sO.h, g0_store[gb])
                li = SC.activation(g0t[gb][:], acc[:, m, :], AF.Silu)
                sc_done = sSc.seal(SC)
                SY.wait_ge(sSc.h, sc_done)
                g0_store[gb] = sO.inc(SY.dma_start(
                    out=b.d["g0"][m * 128:(m + 1) * 128, ql:ql + 512], in_=g0t[gb][:]), 16)
            acc_free = sSc.n
        SY.wait_ge(sO.h, sO.n)
        GP.wait_ge(sO.h, sO.n)


def emit_attn(b: B):
    nc, PE, VE, SC, SY, GP = b.nc, b.PE, b.VE, b.SC, b.SY, b.GP
    sW, sA, sO = b.sems["wload"], b.sems["aload"], b.sems["store"]
    sPe, sVe, sSc = b.sems["pe"], b.sems["ve"], b.sems["sc"]
    sPr = b.sems["prot"]
    with ExitStack() as es:
        b.init_protocol(es)
        aofull = b.sb(es, "aofull", (128, 16, TOK), BF16)
        with ExitStack() as es1:
            aosb = b.sb(es1, "aosb", (128, 8, T), BF16)
            ktsb = b.sb(es1, "ktsb", (128, 2, T), BF16)
            vtsb = b.sb(es1, "vtsb", (128, 32, 256), BF16)
            qsb = b.sb(es1, "qsb", (128, T), BF16)
            masksb = b.sb(es1, "masksb", (128, 4, 512), F32)
            alibsb = b.sb(es1, "alibsb", (128, 256), F32)
            rn = b.sb(es1, "rn", (1, T), BF16)
            att = [b.sb(es1, f"att{i}", (128, 512), BF16) for i in range(3)]
            denrow = b.sb(es1, "denrow", (1, 512), F32)
            avsb = b.sb(es1, "avsb", (128, 512), BF16)
            ps_s = [b.ps(es1, f"ps_s{i}", (128, 512)) for i in range(2)]
            ps_den = b.ps(es1, "ps_den", (1, 512))
            ps_av = b.ps(es1, "ps_av", (128, 512))
            ps_bc = b.ps(es1, "ps_bc", (128, 512))

            la = sA.inc(SY.dma_start(out=masksb[:], in_=b.d["maskc"].rearrange("v p n -> p v n")), 16)
            la = sA.inc(SY.dma_start(out=alibsb[:], in_=b.d["alibic"][:, :]), 16)
            SC.wait_ge(sA.h, la)
            VE.wait_ge(sA.h, la)
            lk = sW.inc(GP.dma_start(out=ktsb[:], in_=b.d["kT"].rearrange("(g p) n -> p g n", p=128)), 16)
            lk = sW.inc(GP.dma_start(out=vtsb[:], in_=b.d["vTok"].rearrange("(tb p) h -> p tb h", p=128)), 16)
            qsb2 = [qsb, b.sb(es1, "qsb_b", (128, T), BF16)]
            rn2 = [rn, b.sb(es1, "rn_b", (1, T), BF16)]
            q_use = [0, 0]
            exp_done = {}
            denav_done = {}
            mask_done = {}
            fin_evac = 0
            av_free = 0
            for hh in range(8):
                gl = hh % 2
                hb = hh % 2
                SY.wait_ge(sPe.h, q_use[hb])
                lq = sA.inc(SY.dma_start(out=qsb2[hb][:], in_=b.d["qsc"][hh * 128:(hh + 1) * 128, :]), 16)
                lr = sA.inc(SY.dma_start(out=rn2[hb][:], in_=b.d["rng_"][0:1, hh * T:(hh + 1) * T]), 16)
                first_pe = True
                for qc in range(8):
                    nkb = 4 * (qc + 1)

                    def scores(kb):
                        pb = kb % 2
                        if kb >= 2:
                            PE.wait_ge(sSc.h, exp_done[kb - 2])
                        if first_pe:
                            PE.wait_ge(sA.h, lr)
                            PE.wait_ge(sW.h, lk)
                        PE.matmul(ps_s[pb][:], ktsb[:, gl, kb * 128:(kb + 1) * 128],
                                  qsb2[hb][:, qc * 512:(qc + 1) * 512], start=True, stop=False)
                        li = PE.matmul(ps_s[pb][:], b._ones_row_bf[:],
                                       rn2[hb][0:1, qc * 512:(qc + 1) * 512], start=False, stop=True)
                        sreq = sPe.inc(li, 1)
                        if kb >= 4 * qc:
                            VE.wait_ge(sPe.h, sreq)
                            li = VE.tensor_add(ps_s[pb][:], ps_s[pb][:], masksb[:, kb - 4 * qc, :])
                            mask_done[kb] = sVe.seal(VE)
                        else:
                            mask_done[kb] = None
                        return sreq

                    def do_exp(kb, sreq):
                        if kb >= 3 and (kb - 3) in denav_done:
                            SC.wait_ge(sPe.h, denav_done[kb - 3])
                        if mask_done[kb] is not None:
                            SC.wait_ge(sVe.h, mask_done[kb])
                        else:
                            SC.wait_ge(sPe.h, sreq)
                        li = SC.activation(att[kb % 3][:], ps_s[kb % 2][:], AF.Exp,
                                           bias=alibsb[:, hh * 32 + kb:hh * 32 + kb + 1])
                        exp_done[kb] = sSc.seal(SC)

                    def denav(kb):
                        PE.wait_ge(sSc.h, exp_done[kb])
                        if kb == 0:
                            PE.wait_ge(sSc.h, av_free)
                            PE.wait_ge(sVe.h, fin_evac)
                        PE.matmul(ps_den[:], b._ones_col_bf[:], att[kb % 3][:],
                                  start=(kb == 0), stop=(kb == nkb - 1))
                        li = PE.matmul(ps_av[:], vtsb[:, kb, gl * 128:(gl + 1) * 128],
                                       att[kb % 3][:], start=(kb == 0), stop=(kb == nkb - 1))
                        denav_done[kb] = sPe.inc(li, 1)

                    exp_done.clear(); denav_done.clear(); mask_done.clear()
                    sreq_prev = scores(0)
                    first_pe = False
                    do_exp(0, sreq_prev)
                    for kb in range(1, nkb):
                        sreq = scores(kb)
                        denav(kb - 1)
                        do_exp(kb, sreq)
                    denav(nkb - 1)
                    q_use[hb] = sPe.n
                    SC.wait_ge(sPe.h, denav_done[nkb - 1])
                    SC.wait_ge(sVe.h, fin_evac)
                    SC.activation(denrow[:], ps_den[:], AF.Ln)
                    SC.drain()
                    li = SC.activation(denrow[:], denrow[:], AF.Exp, scale=-1.0)
                    rd_done = sSc.seal(SC)
                    PE.wait_ge(sSc.h, rd_done)
                    PE.wait_ge(sVe.h, fin_evac)
                    li = PE.matmul(ps_bc[:], b._ones_row_f32[:], denrow[:], start=True, stop=True)
                    bc_done = sPe.inc(li, 1)
                    li = SC.activation(avsb[:], ps_av[:], AF.Copy)
                    av_done = sSc.seal(SC)
                    av_free = av_done
                    VE.wait_ge(sPe.h, bc_done)
                    VE.wait_ge(sSc.h, av_done)
                    li = VE.tensor_mul(aosb[:, hh, qc * 512:(qc + 1) * 512], avsb[:], ps_bc[:])
                    fin_evac = sVe.seal(VE)
            def a1(role):
                other_cols = slice((1 - role) * TOK, (2 - role) * TOK)
                GP.wait_ge(sVe.h, sVe.n)
                k = sPr.inc(GP.dma_start(
                    out=b.d["sh_ao"][role].rearrange("(hh p) n -> p hh n", p=128),
                    in_=aosb[:, :, other_cols]), 16)
                GP.wait_ge(sPr.h, k)
                sPr.inc(GP.dma_start(
                    out=b.d["sh_flag"][0:1, role:role + 1],
                    in_=b._nonce_sb[0:1, 0:1]), 16)
            b.arm_pair("a1", a1)

            def a2(role):
                own_cols = slice(role * TOK, (role + 1) * TOK)
                b.poll("a2", 0, role)
                sPr.inc(GP.dma_start(
                    out=aofull[:, 0:8, :],
                    in_=aosb[:, :, own_cols]), 16)
                sPr.inc(GP.dma_start(
                    out=aofull[:, 8:16, :],
                    in_=b.d["sh_ao"][1 - role].rearrange("(hh p) n -> p hh n", p=128)), 16)
                GP.wait_ge(sPr.h, sPr.n)
            b.arm_pair("a2", a2)
        b.barrier()
        with ExitStack() as es2:
            wosb = b.sb(es2, "wosb", (128, 16, 2048), BF16)
            bo0 = b.sb(es2, "bo0", (128, 16), F32)
            g0t = b.sb(es2, "g0t", (128, 512), F32)
            xot = b.sb(es2, "xot", (128, 512), F32)
            s_t = b.sb(es2, "s_t", (128, 512), F32)
            x1t = b.sb(es2, "x1t", (128, 512), F32)
            ps_o = [b.ps(es2, f"ps_o{i}", (128, 512)) for i in range(2)]
            lw = sW.inc(GP.dma_start(out=wosb[:], in_=b.d["wo0_t"].rearrange("kb i mb o -> i kb (mb o)")), 16)
            la = sA.inc(SY.dma_start(out=bo0[:], in_=b.d["bo0c"][:, :]), 16)
            GP.wait_ge(sPr.h, sPr.n)
            PE.wait_ge(sW.h, sW.n)
            PE.wait_ge(sA.h, sA.n)
            PE.wait_ge(sPr.h, sPr.n)
            for n in range(4):
                nl = n * 512
                for m in range(16):
                    pb = m % 2
                    PE.wait_ge(sVe.h, sVe.n)
                    for kb in range(16):
                        li = PE.matmul(ps_o[pb][:], wosb[:, kb, m * 128:(m + 1) * 128],
                                       aofull[:, kb, nl:nl + 512], start=(kb == 0), stop=(kb == 15))
                    sPe.inc(li, 1)
                    SY.wait_ge(sVe.h, sVe.n)
                    la = sA.inc(SY.dma_start(out=g0t[:], in_=b.d["g0"][m * 128:(m + 1) * 128, nl:nl + 512]), 16)
                    la = sA.inc(SY.dma_start(out=xot[:], in_=b.d["xo"][m * 128:(m + 1) * 128, nl:nl + 512]), 16)
                    VE.wait_ge(sPe.h, sPe.n)
                    VE.wait_ge(sA.h, la)
                    VE.wait_ge(sO.h, sO.n)
                    VE.tensor_scalar(s_t[:], ps_o[pb][:], bo0[:, m:m + 1], None, OP.add)
                    VE.tensor_mul(x1t[:], s_t[:], g0t[:])
                    li = VE.tensor_add(x1t[:], x1t[:], xot[:])
                    sVe.seal(VE)
                    SY.wait_ge(sVe.h, sVe.n)
                    sO.inc(SY.dma_start(out=b.d["x1"][m * 128:(m + 1) * 128, nl:nl + 512],
                                        in_=x1t[:]), 16)
            SY.wait_ge(sO.h, sO.n)
        GP.wait_ge(sO.h, sO.n)


def emit_ln1mix(b: B):
    nc, PE, VE, SC, SY, GP = b.nc, b.PE, b.VE, b.SC, b.SY, b.GP
    sW, sA, sO = b.sems["wload"], b.sems["aload"], b.sems["store"]
    sPe, sVe, sSc = b.sems["pe"], b.sems["ve"], b.sems["sc"]
    sPr = b.sems["prot"]
    with ExitStack() as es:
        b.init_protocol(es)
        sumcb = b.sb(es, "sumcb", (128, 1), BF16)
        cols = b.sb(es, "cols", (128, 8, 16), F32)
        bcol = b.sb(es, "bcol", (128, 16), BF16)
        mcol = b.sb(es, "mcol", (128, 1), F32)
        sW.inc(GP.dma_start(out=sumcb[:], in_=b.d["sumcol"][:, :]), 16)
        for i, nm in enumerate(("bkc", "bvc", "brc", "uc", "tmkc", "tmvc", "tmrc")):
            sA.inc(SY.dma_start(out=cols[:, i, :], in_=b.d[nm][:, :]), 16)
        sA.inc(SY.dma_start(out=mcol[:], in_=b.d["maskcol"][:, :]), 16)
        with ExitStack() as es1:
            xc = b.sb(es1, "xc", (128, 16, 512), F32)
            x2 = b.sb(es1, "x2", (128, 16, 512), BF16)
            xbf = b.sb(es1, "xbf", (128, 16, 512), BF16)
            xnb = b.sb(es1, "xnb", (128, 16, 512), BF16)
            tmpf = b.sb(es1, "tmpf", (128, 512), F32)
            murow = b.sb(es1, "murow", (1, 512), F32)
            e2row = b.sb(es1, "e2row", (1, 512), F32)
            varrow = b.sb(es1, "varrow", (1, 512), F32)
            ivrow = b.sb(es1, "ivrow", (1, 512), F32)
            ps_mu = b.ps(es1, "ps_mu", (1, 512))
            ps_e2 = b.ps(es1, "ps_e2", (1, 512))
            ps_mub = b.ps(es1, "ps_mub", (128, 512))
            ps_ivb = b.ps(es1, "ps_ivb", (128, 512))
            for ch in range(4):
                sl = 512 * ch
                if ch >= 1:
                    SY.wait_ge(sVe.h, sVe.n)
                    GP.wait_ge(sPe.h, sPe.n)
                la = sA.inc(SY.dma_start(
                    out=xc[:], in_=b.d["x1"][:, sl:sl + 512].rearrange("(t p) n -> p t n", p=128)), 16)
                lw = sW.inc(GP.dma_start(
                    out=xbf[:], in_=b.d["x1"][:, sl:sl + 512].rearrange("(t p) n -> p t n", p=128)), 16)
                SC.wait_ge(sW.h, lw)
                SC.wait_ge(sPe.h, sPe.n)
                for t in range(16):
                    li = SC.activation(x2[:, t, :], xbf[:, t, :], AF.Square)
                sSc.seal(SC)
                PE.wait_ge(sSc.h, sSc.n)
                PE.wait_ge(sW.h, sW.n)
                PE.wait_ge(sVe.h, sVe.n)
                for t in range(16):
                    PE.matmul(ps_mu[:], sumcb[:], xbf[:, t, :], start=(t == 0), stop=(t == 15))
                for t in range(16):
                    li = PE.matmul(ps_e2[:], sumcb[:], x2[:, t, :], start=(t == 0), stop=(t == 15))
                sPe.inc(li, 1)
                VE.wait_ge(sPe.h, sPe.n)
                VE.tensor_copy(murow[:], ps_mu[:])
                VE.tensor_copy(e2row[:], ps_e2[:])
                VE.drain()
                VE.tensor_mul(varrow[:], murow[:], murow[:])
                VE.drain()
                VE.tensor_sub(varrow[:], e2row[:], varrow[:])
                VE.drain()
                li = VE.tensor_scalar_add(varrow[:], varrow[:], EPS)
                sVe.seal(VE)
                SC.wait_ge(sVe.h, sVe.n)
                SC.activation(varrow[:], varrow[:], AF.Ln)
                SC.drain()
                li = SC.activation(ivrow[:], varrow[:], AF.Exp, scale=-0.5)
                sSc.seal(SC)
                PE.wait_ge(sSc.h, sSc.n)
                PE.matmul(ps_mub[:], b._ones_row_f32[:], murow[:], start=True, stop=True)
                li = PE.matmul(ps_ivb[:], b._ones_row_f32[:], ivrow[:], start=True, stop=True)
                sPe.inc(li, 1)
                VE.wait_ge(sPe.h, sPe.n)
                VE.wait_ge(sA.h, la)
                VE.wait_ge(sO.h, sO.n)
                for t in range(16):
                    VE.tensor_sub(tmpf[:], xc[:, t, :], ps_mub[:])
                    li = VE.tensor_mul(xnb[:, t, :], tmpf[:], ps_ivb[:])
                sVe.seal(VE)
                SY.wait_ge(sVe.h, sVe.n)
                sO.inc(SY.dma_start(
                    out=b.d["xn1"][:, sl:sl + 512].rearrange("(t p) n -> p t n", p=128),
                    in_=xnb[:]), 16)
        def a3(role):
            GP.wait_ge(sO.h, sO.n)
            k = sPr.inc(GP.dma_start(
                out=b.d["sh_bcol"][role],
                in_=b.d["xn1"][:, TOK - 1:TOK].rearrange("(t p) a -> p (t a)", p=128),
                allow_slow_non_contiguous=True), 16)
            GP.wait_ge(sPr.h, k)
            k = sPr.inc(GP.dma_start(
                out=b.d["sh_flag"][1:2, role:role + 1],
                in_=b._nonce_sb[1:2, 0:1]), 16)
            b.poll("a3", 1, role)
            k = sPr.inc(GP.dma_start(out=bcol[:], in_=b.d["sh_bcol"][1 - role]), 16)
            GP.wait_ge(sPr.h, k)
        b.arm_pair("a3", a3)

        b.barrier()
        with ExitStack() as es2:
            wp = b.sb(es2, "wp", (128, 16, 2048), BF16)
            xn1c = b.sb(es2, "xn1c", (128, 16, 512), BF16)
            xsc = b.sb(es2, "xsc", (128, 16, 512), BF16)
            mixc = b.sb(es2, "mixc", (128, 16, 512), BF16)
            evac = [b.sb(es2, f"pevac{i}", (128, 512), BF16) for i in range(2)]
            bku2 = [b.sb(es2, f"bku{i}", (128, 1), F32) for i in range(2)]
            ps_p = [b.ps(es2, f"ps_p{i}", (128, 512)) for i in range(2)]
            prj_evac = [0, 0]
            prj_store = [0, 0]
            VE.wait_ge(sPr.h, sPr.n)
            for pi, (wnm, onm, tmi, bi) in enumerate((
                    ("wk_t", "kt1", 4, 0), ("wv_t", "v1", 5, 1), ("wr_t", "r1", 6, 2))):
                GP.wait_ge(sPe.h, sPe.n)
                lw = sW.inc(GP.dma_start(out=wp[:], in_=b.d[wnm].rearrange("kb i mb o -> i kb (mb o)")), 16)
                for n in range(4):
                    nl = n * 512
                    SY.wait_ge(sVe.h, sVe.n)
                    la = sA.inc(SY.dma_start(
                        out=xn1c[:],
                        in_=b.d["xn1"][:, nl:nl + 512].rearrange("(t p) n -> p t n", p=128)), 16)
                    if n == 0:
                        la = sA.inc(SY.dma_start(
                            out=xsc[:, :, 1:512],
                            in_=b.d["xn1"][:, 0:511].rearrange("(t p) n -> p t n", p=128)), 16)
                    else:
                        la = sA.inc(SY.dma_start(
                            out=xsc[:],
                            in_=b.d["xn1"][:, nl - 1:nl + 511].rearrange("(t p) n -> p t n", p=128)), 16)
                    VE.wait_ge(sA.h, la)
                    if n == 0:
                        for t in range(16):
                            VE.tensor_scalar(xsc[:, t, 0:1], bcol[:, t:t + 1], mcol[:, 0:1], None, OP.mult)
                    for t in range(16):
                        VE.tensor_sub(mixc[:, t, :], xn1c[:, t, :], xsc[:, t, :])
                        li = VE.scalar_tensor_tensor(
                            mixc[:, t, :], mixc[:, t, :], cols[:, tmi, t:t + 1],
                            xsc[:, t, :], OP.mult, OP.add)
                    sVe.seal(VE)
                    if pi == 2:
                        SC.wait_ge(sVe.h, sVe.n)
                        for t in range(16):
                            li = SC.activation(mixc[:, t, :], mixc[:, t, :], AF.Silu)
                        sSc.seal(SC)
                        PE.wait_ge(sSc.h, sSc.n)
                    PE.wait_ge(sVe.h, sVe.n)
                    PE.wait_ge(sW.h, lw)
                    for m in range(16):
                        pb = m % 2
                        PE.wait_ge(sVe.h, prj_evac[pb])
                        for kb in range(16):
                            li = PE.matmul(ps_p[pb][:], wp[:, kb, m * 128:(m + 1) * 128],
                                           mixc[:, kb, :], start=(kb == 0), stop=(kb == 15))
                        mm_done = sPe.inc(li, 1)
                        eb = evac[pb]
                        if pi == 0:
                            VE.tensor_tensor(bku2[pb][:], cols[:, 0, m:m + 1], cols[:, 3, m:m + 1], OP.add)
                        else:
                            VE.tensor_scalar(bku2[pb][:], cols[:, bi, m:m + 1], 0.0, None, OP.add)
                        VE.drain()
                        VE.wait_ge(sPe.h, mm_done)
                        VE.wait_ge(sO.h, prj_store[pb])
                        li = VE.tensor_scalar(eb[:], ps_p[pb][:], bku2[pb][:, 0:1], None, OP.add)
                        prj_evac[pb] = sVe.seal(VE)
                        SY.wait_ge(sVe.h, prj_evac[pb])
                        prj_store[pb] = sO.inc(SY.dma_start(
                            out=b.d[onm][m * 128:(m + 1) * 128, nl:nl + 512], in_=eb[:]), 16)
        SY.wait_ge(sO.h, sO.n)
        GP.wait_ge(sO.h, sO.n)


def emit_wkv(b: B):
    nc, PE, VE, SC, SY, GP = b.nc, b.PE, b.VE, b.SC, b.SY, b.GP
    sW, sA, sO = b.sems["wload"], b.sems["aload"], b.sems["store"]
    sPe, sVe, sSc = b.sems["pe"], b.sems["ve"], b.sems["sc"]
    sPr = b.sems["prot"]
    with ExitStack() as es:
        b.init_protocol(es)
        kt = b.sb(es, "kt", (128, TOK), BF16)
        vt = b.sb(es, "vt", (128, TOK), BF16)
        rt = b.sb(es, "rt", (128, TOK), BF16)
        tau = b.sb(es, "tau", (128, TOK), F32)
        e1 = b.sb(es, "e1", (128, TOK), F32)
        e2 = b.sb(es, "e2", (128, TOK), F32)
        e2v = b.sb(es, "e2v", (128, TOK), F32)
        ba = b.sb(es, "ba", (128, TOK), F32)
        bb = b.sb(es, "bb", (128, TOK), F32)
        al = b.sb(es, "al", (128, TOK), F32)
        zz = b.sb(es, "zz", (128, TOK), F32)
        rwk = b.sb(es, "rwk", (128, TOK), BF16)
        wcol = b.sb(es, "wcol", (128, 16), F32)
        mcol = b.sb(es, "mcol", (128, 1), F32)
        pkt = b.sb(es, "pkt", (128, 2), F32)
        rcv = b.sb(es, "rcv", (128, 2), F32)
        a0c = b.sb(es, "a0c", (128, 2), F32)
        sA.inc(SY.dma_start(out=wcol[:], in_=b.d["wc"][:, :]), 16)
        sA.inc(SY.dma_start(out=mcol[:], in_=b.d["maskcol"][:, :]), 16)
        VE.wait_ge(sA.h, sA.n)
        li = VE.memset(zz[:], 0.0)
        sVe.seal(VE)
        for dt in range(16):
            rs = slice(dt * 128, (dt + 1) * 128)
            SY.wait_ge(sVe.h, sVe.n)
            la = sA.inc(SY.dma_start(out=kt[:], in_=b.d["kt1"][rs, :]), 16)
            la = sA.inc(SY.dma_start(out=vt[:], in_=b.d["v1"][rs, :]), 16)
            la = sA.inc(SY.dma_start(out=rt[:], in_=b.d["r1"][rs, :]), 16)
            VE.wait_ge(sA.h, la)
            VE.tensor_scalar(tau[:], kt[:], wcol[:, dt:dt + 1], None, OP.max)
            VE.drain()
            li = VE.tensor_scalar(e1[:], tau[:], wcol[:, dt:dt + 1], None, OP.add)
            sVe.seal(VE)
            SC.wait_ge(sVe.h, sVe.n)
            li = SC.activation(e1[:], e1[:], AF.Exp, scale=-1.0)
            sSc.seal(SC)
            VE.wait_ge(sSc.h, sSc.n)
            li = VE.tensor_sub(e2[:], kt[:], tau[:])
            sVe.seal(VE)
            SC.wait_ge(sVe.h, sVe.n)
            li = SC.activation(e2[:], e2[:], AF.Exp)
            sSc.seal(SC)
            VE.wait_ge(sSc.h, sSc.n)
            VE.tensor_mul(e2v[:], e2[:], vt[:])
            VE.tensor_tensor_scan(ba[:], e1[:], e2v[:], 0.0, OP.mult, OP.add)
            VE.tensor_tensor_scan(bb[:], e1[:], e2[:], 0.0, OP.mult, OP.add)
            VE.tensor_tensor_scan(al[:], e1[:], zz[:], 1.0, OP.mult, OP.add)
            VE.tensor_copy(pkt[:, 0:1], ba[:, TOK - 1:TOK])
            li = VE.tensor_copy(pkt[:, 1:2], bb[:, TOK - 1:TOK])
            sVe.seal(VE)

            def a4(role, _dt=dt):
                GP.wait_ge(sVe.h, sVe.n)
                k = sPr.inc(GP.dma_start(
                    out=b.d["sh_carry"][role][:, 2 * _dt:2 * _dt + 2], in_=pkt[:]), 16)
                GP.wait_ge(sPr.h, k)
                k = sPr.inc(GP.dma_start(
                    out=b.d["sh_flag"][2 + _dt:3 + _dt, role:role + 1],
                    in_=b._nonce_sb[2 + _dt:3 + _dt, 0:1]), 16)
                b.poll(f"a4_{_dt}", 2 + _dt, role)
                k = sPr.inc(GP.dma_start(
                    out=rcv[:], in_=b.d["sh_carry"][1 - role][:, 2 * _dt:2 * _dt + 2]), 16)
                GP.wait_ge(sPr.h, k)
            b.arm_pair(f"a4_{dt}", a4)
            VE.wait_ge(sPr.h, sPr.n)
            VE.tensor_scalar(a0c[:], rcv[:], mcol[:, 0:1], None, OP.mult)
            VE.drain()
            VE.scalar_tensor_tensor(ba[:], al[:], a0c[:, 0:1], ba[:], OP.mult, OP.add)
            li = VE.scalar_tensor_tensor(bb[:], al[:], a0c[:, 1:2], bb[:], OP.mult, OP.add)
            sVe.seal(VE)
            SC.wait_ge(sVe.h, sVe.n)
            SC.activation(bb[:], bb[:], AF.Ln)
            SC.drain()
            li = SC.activation(bb[:], bb[:], AF.Exp, scale=-1.0)
            sSc.seal(SC)
            VE.wait_ge(sSc.h, sSc.n)
            VE.tensor_mul(ba[:], ba[:], bb[:])
            li = VE.tensor_mul(rwk[:], ba[:], rt[:])
            sVe.seal(VE)
            SY.wait_ge(sVe.h, sVe.n)
            sO.inc(SY.dma_start(out=b.d["rwkv"][rs, :], in_=rwk[:]), 16)
        SY.wait_ge(sO.h, sO.n)
        GP.wait_ge(sO.h, sO.n)


def emit_moe(b: B):
    nc, PE, VE, SC, SY, GP = b.nc, b.PE, b.VE, b.SC, b.SY, b.GP
    sW, sA, sO = b.sems["wload"], b.sems["aload"], b.sems["store"]
    sPe, sVe, sSc = b.sems["pe"], b.sems["ve"], b.sems["sc"]
    with ExitStack() as es:
        with ExitStack() as es1:
            wp = b.sb(es1, "wp", (128, 16, 2048), BF16)
            rwc = b.sb(es1, "rwc", (128, 16, 512), BF16)
            bo1 = b.sb(es1, "bo1", (128, 16), F32)
            s1t = b.sb(es1, "s1t", (128, 512), F32)
            ps_p = [b.ps(es1, f"ps_p{i}", (128, 512)) for i in range(2)]
            lw = sW.inc(GP.dma_start(out=wp[:], in_=b.d["wo1_t"].rearrange("kb i mb o -> i kb (mb o)")), 16)
            la = sA.inc(SY.dma_start(out=bo1[:], in_=b.d["bo1c"][:, :]), 16)
            s1tb = [s1t, b.sb(es1, "s1t_b", (128, 512), F32)]
            rwcb = [rwc, b.sb(es1, "rwc_b", (128, 16, 512), BF16)]
            p1_evac = [0, 0]
            p1_store = [0, 0]
            rw_use = [0, 0]
            for n in range(4):
                nl = n * 512
                nb = n % 2
                SY.wait_ge(sPe.h, rw_use[nb])
                lr = sA.inc(SY.dma_start(
                    out=rwcb[nb][:], in_=b.d["rwkv"][:, nl:nl + 512].rearrange("(t p) n -> p t n", p=128)), 16)
                PE.wait_ge(sA.h, lr)
                PE.wait_ge(sW.h, lw)
                for m in range(16):
                    pb = m % 2
                    PE.wait_ge(sVe.h, p1_evac[pb])
                    for kb in range(16):
                        li = PE.matmul(ps_p[pb][:], wp[:, kb, m * 128:(m + 1) * 128],
                                       rwcb[nb][:, kb, :], start=(kb == 0), stop=(kb == 15))
                    mm_done = sPe.inc(li, 1)
                    VE.wait_ge(sPe.h, mm_done)
                    VE.wait_ge(sO.h, p1_store[pb])
                    li = VE.tensor_scalar(s1tb[pb][:], ps_p[pb][:], bo1[:, m:m + 1], None, OP.add)
                    p1_evac[pb] = sVe.seal(VE)
                    SY.wait_ge(sVe.h, p1_evac[pb])
                    p1_store[pb] = sO.inc(SY.dma_start(
                        out=b.d["s1"][m * 128:(m + 1) * 128, nl:nl + 512], in_=s1tb[pb][:]), 16)
                rw_use[nb] = sPe.n
        b.barrier()
        with ExitStack() as es2:
            wgr = b.sb(es2, "wgr", (128, 16, 8), BF16)
            eyef = b.sb(es2, "eyef", (128, 128), F32)
            xnc = b.sb(es2, "xnc", (128, 16, 512), BF16)
            lgf = b.sb(es2, "lgf", (8, 512), F32)
            lgt = b.sb(es2, "lgt", (128, 8), F32)
            m1 = b.sb(es2, "m1", (128, 1), F32)
            nm1 = b.sb(es2, "nm1", (128, 1), F32)
            e8 = b.sb(es2, "e8", (128, 8), F32)
            eqm = b.sb(es2, "eqm", (128, 8), F32)
            msk8 = b.sb(es2, "msk8", (128, 8), F32)
            m2 = b.sb(es2, "m2", (128, 1), F32)
            kp8 = b.sb(es2, "kp8", (128, 8), F32)
            w8 = b.sb(es2, "w8", (128, 8), F32)
            den8 = b.sb(es2, "den8", (128, 1), F32)
            c8 = b.sb(es2, "c8", (128, 8), F32)
            c8p = b.sb(es2, "c8p", (128, 32), F32)
            negc = b.sb(es2, "negc", (128, 1), F32)
            c8t = b.sb(es2, "c8t", (8, TOK), F32)
            ps_l = b.ps(es2, "ps_l", (8, 512))
            ps_t = b.ps(es2, "ps_t", (128, 128))
            lw = sW.inc(GP.dma_start(out=wgr[:], in_=b.d["wgr_t"].rearrange("kb i e -> i kb e")), 16)
            la = sA.inc(SY.dma_start(out=eyef[:], in_=b.d["eye"][:, :]), 16)
            VE.memset(c8p[:], 0.0)
            li = VE.memset(negc[:], NEG)
            sVe.seal(VE)
            for n in range(4):
                nl = n * 512
                SY.wait_ge(sPe.h, sPe.n)
                lx = sA.inc(SY.dma_start(
                    out=xnc[:], in_=b.d["xn1"][:, nl:nl + 512].rearrange("(t p) n -> p t n", p=128)), 16)
                PE.wait_ge(sA.h, sA.n)
                PE.wait_ge(sW.h, lw)
                PE.wait_ge(sVe.h, sVe.n)
                for kb in range(16):
                    li = PE.matmul(ps_l[:], wgr[:, kb, :], xnc[:, kb, :],
                                   start=(kb == 0), stop=(kb == 15))
                sPe.inc(li, 1)
                VE.wait_ge(sPe.h, sPe.n)
                li = VE.tensor_copy(lgf[:], ps_l[:])
                sVe.seal(VE)
                for j in range(4):
                    tok0 = nl + j * 128
                    PE.wait_ge(sVe.h, sVe.n)
                    li = PE.transpose(ps_t[:, 0:8], lgf[0:8, j * 128:(j + 1) * 128], eyef[0:8, 0:8])
                    sPe.inc(li, 1)
                    VE.wait_ge(sPe.h, sPe.n)
                    VE.tensor_copy(lgt[:], ps_t[:, 0:8])
                    VE.drain()
                    VE.reduce_max(m1[:], lgt[:], axis=AX)
                    VE.drain()
                    li = VE.tensor_scalar(nm1[:], m1[:], -1.0, None, OP.mult)
                    sVe.seal(VE)
                    SC.wait_ge(sVe.h, sVe.n)
                    li = SC.activation(e8[:], lgt[:], AF.Exp, bias=nm1[:, 0:1])
                    sSc.seal(SC)
                    VE.wait_ge(sSc.h, sSc.n)
                    VE.tensor_scalar(eqm[:], lgt[:], m1[:, 0:1], None, OP.is_equal)
                    VE.drain()
                    VE.scalar_tensor_tensor(msk8[:], eqm[:], negc[:, 0:1], lgt[:], OP.mult, OP.add)
                    VE.drain()
                    VE.reduce_max(m2[:], msk8[:], axis=AX)
                    VE.drain()
                    VE.tensor_scalar(kp8[:], lgt[:], m2[:, 0:1], None, OP.is_ge)
                    VE.drain()
                    VE.tensor_mul(w8[:], e8[:], kp8[:])
                    VE.drain()
                    li = VE.reduce_sum(den8[:], w8[:], axis=AX)
                    sVe.seal(VE)
                    SC.wait_ge(sVe.h, sVe.n)
                    SC.activation(den8[:], den8[:], AF.Ln)
                    SC.drain()
                    li = SC.activation(den8[:], den8[:], AF.Exp, scale=-1.0)
                    sSc.seal(SC)
                    VE.wait_ge(sSc.h, sSc.n)
                    VE.tensor_scalar(c8[:], w8[:], den8[:, 0:1], None, OP.mult)
                    VE.drain()
                    li = VE.tensor_copy(c8p[:, 0:8], c8[:])
                    sVe.seal(VE)
                    PE.wait_ge(sVe.h, sVe.n)
                    li = PE.transpose(ps_t[0:32, 0:128], c8p[:], eyef[:])
                    sPe.inc(li, 1)
                    VE.wait_ge(sPe.h, sPe.n)
                    li = VE.tensor_copy(c8t[0:8, tok0:tok0 + 128], ps_t[0:8, 0:128])
                    sVe.seal(VE)
            SY.wait_ge(sVe.h, sVe.n)
            sO.inc(SY.dma_start(out=b.d["crow"][:, :], in_=c8t[:]), 16)
            SY.wait_ge(sO.h, sO.n)
        b.barrier()
        with ExitStack() as es3:
            xnc = [b.sb(es3, f"xnc{i}", (128, 16, 512), BF16) for i in range(2)]
            w1a = [b.sb(es3, f"w1a{i}", (128, 16, 128), BF16) for i in range(2)]
            w1b = [b.sb(es3, f"w1b{i}", (128, 16, 128), BF16) for i in range(2)]
            w2sb = b.sb(es3, "w2sb", (128, 11, 2048), BF16)
            hseb = b.sb(es3, "hseb", (128, 11, 1024), BF16)
            acc = b.sb(es3, "acc", (128, 16, 1024), F32)
            crowb = b.sb(es3, "crowb", (1, 1024), F32)
            sg = [b.sb(es3, f"sg{i}", (128, 512), F32) for i in range(2)]
            hpp = [b.sb(es3, f"hpp{i}", (128, 512), F32) for i in range(2)]
            g1t = [b.sb(es3, f"g1t{i}", (128, 512), F32) for i in range(2)]
            x1t = [b.sb(es3, f"x1t{i}", (128, 512), F32) for i in range(2)]
            s1t = [b.sb(es3, f"s1t3{i}", (128, 512), F32) for i in range(2)]
            outt = [b.sb(es3, f"outt{i}", (128, 512), F32) for i in range(2)]
            ps_a = [b.ps(es3, f"ps_a{i}", (128, 512)) for i in range(2)]
            ps_b = [b.ps(es3, f"ps_b{i}", (128, 512)) for i in range(2)]
            ps_c = [b.ps(es3, f"ps_c{i}", (128, 512)) for i in range(2)]
            ps_w = [b.ps(es3, f"ps_w{i}", (128, 512)) for i in range(2)]
            slab_use = [0, 0]
            gu_evac = [0, 0]
            xnc_use = [0, 0]
            psc_free = 0
            win_evac = [0, 0]
            crow_use = 0
            hs_free = 0
            w2_use = 0
            acc_free = 0
            comb_done = [0, 0]
            xs_done = [0, 0]

            def load_w1(e, mp):
                pb = mp % 2
                GP.wait_ge(sPe.h, slab_use[pb])
                sW.inc(GP.dma_start(
                    out=w1a[pb][:], in_=b.d["w1_t"][e, :, mp].rearrange("kb i o -> i kb o")), 16)
                return sW.inc(GP.dma_start(
                    out=w1b[pb][:], in_=b.d["w1_t"][e, :, mp + 11].rearrange("kb i o -> i kb o")), 16)

            for hf in range(2):
                hl = hf * 1024
                for e in range(NE):
                    SY.wait_ge(sPe.h, crow_use)
                    lc = sA.inc(SY.dma_start(out=crowb[:], in_=b.d["crow"][e:e + 1, hl:hl + 1024]), 16)
                    lx = [0, 0]
                    for n2 in range(2):
                        GP.wait_ge(sPe.h, xnc_use[n2])
                        lx[n2] = sW.inc(GP.dma_start(
                            out=xnc[n2][:],
                            in_=b.d["xn1"][:, hl + n2 * 512:hl + (n2 + 1) * 512]
                                 .rearrange("(t p) n -> p t n", p=128)), 16)
                    w1ld = [0] * 11
                    w1ld[0] = load_w1(e, 0)
                    GP.wait_ge(sPe.h, w2_use)
                    lw2 = sW.inc(GP.dma_start(
                        out=w2sb[:], in_=b.d["w2_t"][e].rearrange("kb i mb o -> i kb (mb o)")), 16)
                    for mp in range(11):
                        pb = mp % 2
                        if mp + 1 < 11:
                            w1ld[mp + 1] = load_w1(e, mp + 1)
                        for n2 in range(2):
                            pa = (2 * mp + n2) % 2
                            PE.wait_ge(sW.h, max(w1ld[mp], lx[n2]))
                            PE.wait_ge(sVe.h, gu_evac[pa])
                            for kb in range(16):
                                PE.matmul(ps_a[pa][:], w1a[pb][:, kb, :], xnc[n2][:, kb, :],
                                          start=(kb == 0), stop=(kb == 15))
                            for kb in range(16):
                                li = PE.matmul(ps_b[pa][:], w1b[pb][:, kb, :], xnc[n2][:, kb, :],
                                               start=(kb == 0), stop=(kb == 15))
                            if mp == 0:
                                if n2 == 0:
                                    PE.wait_ge(sA.h, lc)
                                    PE.wait_ge(sVe.h, psc_free)
                                li = PE.matmul(ps_c[n2][:], b._ones_row_f32[:],
                                               crowb[0:1, n2 * 512:(n2 + 1) * 512],
                                               start=True, stop=True)
                            u_done = sPe.inc(li, 1)
                            if n2 == 1:
                                slab_use[pb] = u_done
                            xnc_use[n2] = max(xnc_use[n2], u_done)
                            SC.wait_ge(sPe.h, u_done)
                            li = SC.activation(sg[pa][:], ps_a[pa][:], AF.Silu)
                            sc_done = sSc.seal(SC)
                            VE.wait_ge(sSc.h, sc_done)
                            if mp == 0 and n2 == 0:
                                VE.wait_ge(sPe.h, hs_free)
                            VE.tensor_mul(hpp[pa][:], sg[pa][:], ps_b[pa][:])
                            VE.drain()
                            li = VE.tensor_mul(hseb[:, mp, n2 * 512:(n2 + 1) * 512],
                                               hpp[pa][:], ps_c[n2][:])
                            gu_evac[pa] = sVe.seal(VE)
                    psc_free = sVe.n
                    hs_ready = sVe.n
                    PE.wait_ge(sW.h, lw2)
                    PE.wait_ge(sVe.h, hs_ready)
                    tiles = [(m, n2) for m in range(16) for n2 in range(2)]
                    for ui, (m, n2) in enumerate(tiles):
                        bs = ui % 2
                        PE.wait_ge(sVe.h, win_evac[bs])
                        for kb in range(11):
                            li = PE.matmul(ps_w[bs][:], w2sb[:, kb, m * 128:(m + 1) * 128],
                                           hseb[:, kb, n2 * 512:(n2 + 1) * 512],
                                           start=(kb == 0), stop=(kb == 10))
                        wd_done = sPe.inc(li, 1)
                        VE.wait_ge(sPe.h, wd_done)
                        if e == 0:
                            VE.wait_ge(sSc.h, acc_free)
                        if e == 0:
                            li = VE.tensor_copy(acc[:, m, n2 * 512:(n2 + 1) * 512], ps_w[bs][:])
                        else:
                            li = VE.tensor_add(acc[:, m, n2 * 512:(n2 + 1) * 512],
                                               acc[:, m, n2 * 512:(n2 + 1) * 512], ps_w[bs][:])
                        win_evac[bs] = sVe.seal(VE)
                    crow_use = sPe.n
                    hs_free = sPe.n
                    w2_use = sPe.n
                SC.wait_ge(sVe.h, sVe.n)
                for m in range(16):
                    for n2 in range(2):
                        nl = hl + n2 * 512
                        gb = (2 * m + n2) % 2
                        SC.wait_ge(sVe.h, xs_done[gb])
                        li = SC.activation(g1t[gb][:], acc[:, m, n2 * 512:(n2 + 1) * 512], AF.Silu)
                        sc_done = sSc.seal(SC)
                        SY.wait_ge(sVe.h, xs_done[gb])
                        la = sA.inc(SY.dma_start(out=x1t[gb][:], in_=b.d["x1"][m * 128:(m + 1) * 128, nl:nl + 512]), 16)
                        la = sA.inc(SY.dma_start(out=s1t[gb][:], in_=b.d["s1"][m * 128:(m + 1) * 128, nl:nl + 512]), 16)
                        VE.wait_ge(sA.h, la)
                        VE.wait_ge(sSc.h, sc_done)
                        VE.wait_ge(sO.h, comb_done[gb])
                        VE.tensor_mul(outt[gb][:], s1t[gb][:], g1t[gb][:])
                        li = VE.tensor_add(outt[gb][:], outt[gb][:], x1t[gb][:])
                        xs_done[gb] = sVe.seal(VE)
                        SY.wait_ge(sVe.h, xs_done[gb])
                        comb_done[gb] = sO.inc(SY.dma_start(
                            out=b.d["outT"][m * 128:(m + 1) * 128, nl:nl + 512],
                            in_=outt[gb][:]), 16)
                acc_free = sSc.n
        SY.wait_ge(sO.h, sO.n)
        GP.wait_ge(sO.h, sO.n)


def _emit_common_consts(b, es):
    b._ones_row_f32 = b.sb(es, "ones_row_f32", (1, 128), F32)
    b._ones_col_bf = b.sb(es, "ones_col_bf", (128, 1), BF16)
    b._ones_row_bf = b.sb(es, "ones_row_bf", (1, 128), BF16)
    b.GP.memset(b._ones_row_f32[:], 1.0)
    b.GP.memset(b._ones_row_bf[:], 1.0)
    li = b.GP.memset(b._ones_col_bf[:], 1.0)
    sGp = b.sems["gp"]
    sGp.inc(li, 1)
    for eng in (b.PE, b.VE, b.SC):
        eng.wait_ge(sGp.h, sGp.n)



def build_launch(stages, dev=False):
    b = B(stages, dev=dev)
    with ExitStack() as ges:
        b._ges = ges
        _emit_common_consts(b, ges)
        for i, st in enumerate(stages):
            if i:
                b.barrier()
            EMITTERS[st](b)
    return b


EMITTERS = {
    "ln0kvq": emit_ln0kvq,
    "ffn0": emit_ffn0,
    "attn": emit_attn,
    "ln1mix": emit_ln1mix,
    "wkv": emit_wkv,
    "moe": emit_moe,
}


def kernel(**inputs):
    per_core = _stage_all(inputs)
    arrs = [dict(pc) for pc in per_core]
    for stages in LAUNCH_PLAN:
        b = build_launch(stages)
        needed = set()
        for st in stages:
            needed |= set(STAGE_IO[st][0])
            needed |= set(STAGE_IO[st][1])
        in_maps = []
        for c in range(8):
            in_maps.append({name: arrs[c][name] for name in b.input_names})
        res = b.run(in_maps)
        for c in range(8):
            for k, v in res.results[c].items():
                arrs[c][k] = v
    out = np.empty((NB, T, D), np.float32)
    for c in range(8):
        role, batch = c % 2, c // 2
        out[batch, role * TOK:(role + 1) * TOK, :] = np.asarray(arrs[c]["outT"], np.float32).T
    return out


# version 09 (final)
# speedup vs baseline: 1.5571x; 1.4110x over previous
import math
from contextlib import ExitStack

import numpy as np
import ml_dtypes

import concourse.bass as bass
import concourse.mybir as mybir
from concourse.bass_utils import run_bass_kernel_spmd

F32 = mybir.dt.float32
BF16 = mybir.dt.bfloat16
I32 = mybir.dt.int32
AX = mybir.AxisListType.X
OP = mybir.AluOpType
AF = mybir.ActivationFunctionType

D = 2048
T = 4096
TOK = 2048
NB = 4
H = 16
HD = 128
KVH = 4
I_FFN = 8192
FFN1 = 1408
NE = 8
EPS = 1e-5
RS128 = 1.0 / math.sqrt(128.0)
NEG = -1e30

SLOPES = [2.0 ** (-(h + 1) / 2.0) for h in range(H)]


def _heads(role):
    return [h for h in range(H) if h % 4 in (2 * role, 2 * role + 1)]



def _tiles2(w, kb, mb):
    return np.ascontiguousarray(
        w.reshape(kb, 128, mb, 128).transpose(0, 2, 1, 3))


def _colblock(v):
    return np.ascontiguousarray(v.reshape(16, 128).T)


def _stage_core(inputs, core):
    role = core % 2
    batch = core // 2
    mine = _heads(role)
    theirs = _heads(1 - role)
    groups = (2 * role, 2 * role + 1)

    d = {}
    xb = np.asarray(inputs["L0_attn_x"][batch], np.float32)
    xT = np.ascontiguousarray(xb.T)
    d["xf"] = xT
    d["xo"] = np.ascontiguousarray(xT[:, role * TOK:(role + 1) * TOK])

    wq = np.asarray(inputs["L0_attn_Wq"], np.float32)
    d["wq_t"] = np.stack([_tiles2(wq[:, h * 128:(h + 1) * 128], 16, 1)[:, 0]
                          for h in mine])
    wkv = np.asarray(inputs["L0_attn_Wkv"], np.float32)
    cols = [wkv[:, g * 128:(g + 1) * 128] for g in groups] + \
           [wkv[:, 512 + g * 128:512 + (g + 1) * 128] for g in groups]
    d["wkv_t"] = np.stack([_tiles2(c, 16, 1)[:, 0] for c in cols])

    wo = np.asarray(inputs["L0_attn_Wo"], np.float32)
    perm = []
    for h in mine + theirs:
        perm.extend(range(h * 128, (h + 1) * 128))
    d["wo0_t"] = np.ascontiguousarray(_tiles2(wo[perm, :], 16, 16).transpose(0, 2, 1, 3))
    d["bo0c"] = _colblock(np.asarray(inputs["L0_attn_bo"], np.float32))

    wg = np.asarray(inputs["L0_ffn_w_gate"], np.float32)
    wu = np.asarray(inputs["L0_ffn_w_up"], np.float32)
    wd = np.asarray(inputs["L0_ffn_w_down"], np.float32)
    wg_p = np.ascontiguousarray(wg.reshape(D, I_FFN // 4, 4).transpose(0, 2, 1).reshape(D, I_FFN))
    wu_p = np.ascontiguousarray(wu.reshape(D, I_FFN // 4, 4).transpose(0, 2, 1).reshape(D, I_FFN))
    wd_p = np.ascontiguousarray(wd.reshape(I_FFN // 4, 4, D).transpose(1, 0, 2).reshape(I_FFN, D))
    d["wg_t"] = np.ascontiguousarray(_tiles2(wg_p, 16, 64).transpose(1, 0, 2, 3))
    d["wu_t"] = np.ascontiguousarray(_tiles2(wu_p, 16, 64).transpose(1, 0, 2, 3))
    d["wd_t"] = _tiles2(wd_p, 64, 16)

    for nm, key in (("wk_t", "L1_attn_Wk"), ("wv_t", "L1_attn_Wv"),
                    ("wr_t", "L1_attn_Wr"), ("wo1_t", "L1_attn_Wo")):
        w = np.ascontiguousarray(np.asarray(inputs[key], np.float32).T)
        d[nm] = np.ascontiguousarray(_tiles2(w, 16, 16).transpose(0, 2, 1, 3))
    for nm, key in (("bkc", "L1_attn_bk"), ("bvc", "L1_attn_bv"),
                    ("brc", "L1_attn_br"), ("bo1c", "L1_attn_bo"),
                    ("uc", "L1_attn_u"), ("wc", "L1_attn_w"),
                    ("tmkc", "L1_attn_tmk"), ("tmvc", "L1_attn_tmv"),
                    ("tmrc", "L1_attn_tmr")):
        d[nm] = _colblock(np.asarray(inputs[key], np.float32))

    w1 = np.asarray(inputs["L1_ffn_w1"], np.float32)
    w2 = np.asarray(inputs["L1_ffn_w2"], np.float32)
    d["w1_t"] = np.stack([_tiles2(w1[e], 16, 22) for e in range(NE)])
    d["w2_t"] = np.stack([np.ascontiguousarray(_tiles2(w2[e], 11, 16).transpose(0, 2, 1, 3))
                          for e in range(NE)])
    wgr = np.asarray(inputs["L1_ffn_wg"], np.float32)
    d["wgr_t"] = np.ascontiguousarray(wgr.T.reshape(16, 128, 8))

    masks = np.zeros((4, 128, 512), np.float32)
    for si, s in enumerate((0, 128, 256, 384)):
        i = np.arange(128)[:, None]
        j = np.arange(512)[None, :]
        masks[si] = np.where(j >= i + s, 0.0, NEG)
    d["maskc"] = masks
    alib = np.zeros((128, 256), np.float32)
    for hh, h in enumerate(mine):
        for kb in range(32):
            alib[:, hh * 32 + kb] = SLOPES[h] * (kb * 128 + np.arange(128))
    d["alibic"] = alib
    sqr = np.zeros((1, 8 * T), np.float32)
    for hh, h in enumerate(mine):
        sqr[0, hh * T:(hh + 1) * T] = SLOPES[h] * np.arange(T)
    d["sqrow"] = sqr
    d["eye"] = np.eye(128, dtype=np.float32)
    d["sumcol"] = np.full((128, 1), 1.0 / D, np.float32)
    d["onecol"] = np.ones((128, 1), np.float32)
    d["maskcol"] = np.full((128, 1), float(role), np.float32)
    return d


def _stage_all(inputs):
    per_core = [_stage_core(inputs, c) for c in range(8)]
    rng = np.random.default_rng()
    for k in range(4):
        nonce = rng.integers(1 << 16, (1 << 31) - 1, size=(1, 18)).astype(np.int32)
        per_core[2 * k]["nonce"] = nonce
        per_core[2 * k + 1]["nonce"] = nonce
    return per_core


CATALOG = {
    "xf": ((D, T), F32), "xo": ((D, TOK), F32),
    "wq_t": ((8, 16, 128, 128), F32), "wkv_t": ((4, 16, 128, 128), F32),
    "wo0_t": ((16, 128, 16, 128), F32), "bo0c": ((128, 16), F32),
    "wg_t": ((64, 16, 128, 128), F32), "wu_t": ((64, 16, 128, 128), F32),
    "wd_t": ((64, 16, 128, 128), F32),
    "wk_t": ((16, 128, 16, 128), F32), "wv_t": ((16, 128, 16, 128), F32),
    "wr_t": ((16, 128, 16, 128), F32), "wo1_t": ((16, 128, 16, 128), F32),
    "bkc": ((128, 16), F32), "bvc": ((128, 16), F32), "brc": ((128, 16), F32),
    "bo1c": ((128, 16), F32), "uc": ((128, 16), F32), "wc": ((128, 16), F32),
    "tmkc": ((128, 16), F32), "tmvc": ((128, 16), F32), "tmrc": ((128, 16), F32),
    "w1_t": ((8, 16, 22, 128, 128), F32), "w2_t": ((8, 11, 128, 16, 128), F32),
    "wgr_t": ((16, 128, 8), F32),
    "maskc": ((4, 128, 512), F32), "alibic": ((128, 256), F32),
    "sqrow": ((1, 8 * T), F32), "eye": ((128, 128), F32),
    "sumcol": ((128, 1), F32), "onecol": ((128, 1), F32),
    "maskcol": ((128, 1), F32), "nonce": ((1, 18), I32),
    "xnf": ((D, T), BF16), "xno": ((D, TOK), BF16),
    "kT": ((256, T), BF16), "vTok": ((T, 256), BF16),
    "qsc": ((1024, T), BF16), "rng_": ((1, 8 * T), BF16),
    "qn2": ((1, 8 * T), F32),
    "g0": ((D, TOK), F32), "x1": ((D, TOK), F32),
    "xn1": ((D, TOK), BF16),
    "kt1": ((D, TOK), BF16), "v1": ((D, TOK), BF16), "r1": ((D, TOK), BF16),
    "rwkv": ((D, TOK), BF16), "s1": ((D, TOK), F32), "crow": ((8, TOK), F32),
    "outT": ((D, TOK), F32),
}

SHARED = {
    "sh_ao": ((2, 1024, TOK), BF16),
    "sh_bcol": ((2, 128, 16), BF16),
    "sh_carry": ((2, 128, 32), F32),
    "sh_flag": ((18, 2, 16), I32),
}

STAGE_IO = {
    "ln0kvq": (["xf", "wkv_t", "wq_t", "sumcol", "onecol", "eye", "sqrow", "nonce"],
               ["xnf", "xno", "kT", "vTok", "qsc", "rng_", "qn2"]),
    "ffn0": (["xno", "wg_t", "wu_t", "wd_t"], ["g0"]),
    "attn": (["kT", "vTok", "qsc", "rng_", "maskc", "alibic", "onecol",
              "wo0_t", "bo0c", "xo", "g0", "nonce"], ["x1"]),
    "ln1mix": (["x1", "sumcol", "onecol", "wk_t", "wv_t", "wr_t", "bkc", "bvc", "brc",
                "uc", "tmkc", "tmvc", "tmrc", "maskcol", "nonce"],
               ["xn1", "kt1", "v1", "r1"]),
    "wkv": (["kt1", "v1", "r1", "wc", "maskcol", "nonce"], ["rwkv"]),
    "moe": (["rwkv", "wo1_t", "bo1c", "xn1", "x1", "wgr_t", "w1_t", "w2_t",
             "onecol", "eye"], ["s1", "outT", "crow"]),
}

LAUNCH_PLAN = [["ln0kvq", "ffn0", "attn", "ln1mix", "wkv", "moe"]]


class S:

    def __init__(self, nc, es, name):
        self.h = es.enter_context(nc.semaphore(name))
        self.n = 0

    def inc(self, instr, k=1):
        instr.then_inc(self.h, k)
        self.n += k
        return self.n

    def wait(self, eng, at=None):
        eng.wait_ge(self.h, self.n if at is None else at)

    def seal(self, eng, k=1):
        d = eng.drain()
        d.then_inc(self.h, k)
        self.n += k
        return self.n


class B:

    def __init__(self, stages, dev=False):
        self.nc = bass.Bass("TRN2", target_bir_lowering=False, debug=False,
                            detect_race_conditions=False)
        self.es = ExitStack()
        self.stages = stages
        self.dev = dev
        self.d = {}
        self.sems = {}
        self._armcnt = 0
        reads = set()
        writes = set()
        for st in stages:
            r, w = STAGE_IO[st]
            reads |= set(r)
            writes |= set(w)
        produced_before = set()
        consumed_after = set()
        order = [s for gl in LAUNCH_PLAN for s in gl]
        my_idx = max(order.index(s) for s in stages)
        for st in order:
            if order.index(st) < min(order.index(s) for s in stages):
                produced_before |= set(STAGE_IO[st][1])
            if order.index(st) > my_idx:
                consumed_after |= set(STAGE_IO[st][0])
        self.input_names = set()
        self.output_names = set()
        for name, (shape, dt) in CATALOG.items():
            if name not in reads and name not in writes:
                continue
            if name in writes:
                kind = "ExternalOutput" if (name == "outT" or name in consumed_after or dev) else "Internal"
                if kind == "ExternalOutput":
                    self.output_names.add(name)
            else:
                kind = "ExternalInput"
                self.input_names.add(name)
            self.d[name] = self.nc.dram_tensor(name, list(shape), dt, kind=kind).ap()
        for name, (shape, dt) in SHARED.items():
            self.d[name] = self.nc.dram_tensor(name, list(shape), dt, kind="Internal",
                                               addr_space="Shared").ap()
        self.blk = self.es.enter_context(self.nc.Block())
        for nm in ("wload", "aload", "store", "pe", "ve", "sc", "gp", "prot", "poll"):
            self.sems[nm] = S(self.nc, self.es, "s_" + nm)

    @property
    def PE(self):
        return self.nc.tensor

    @property
    def VE(self):
        return self.nc.vector

    @property
    def SC(self):
        return self.nc.scalar

    @property
    def SY(self):
        return self.nc.sync

    @property
    def GP(self):
        return self.nc.gpsimd

    def sb(self, es, name, shape, dt):
        self._uid = getattr(self, "_uid", 0) + 1
        return es.enter_context(self.nc.sbuf_tensor(f"{name}_u{self._uid}", list(shape), dt))

    def ps(self, es, name, shape, dt=F32):
        self._uid = getattr(self, "_uid", 0) + 1
        return es.enter_context(self.nc.psum_tensor(f"{name}_u{self._uid}", list(shape), dt))

    def barrier(self):
        for eng in (self.SY, self.GP):
            for nm in ("pe", "ve", "sc"):
                eng.wait_ge(self.sems[nm].h, self.sems[nm].n)
        self.SY.wait_ge(self.sems["store"].h, self.sems["store"].n)
        self.GP.wait_ge(self.sems["store"].h, self.sems["store"].n)

    def arm_pair(self, tag, fn):
        self._armcnt += 1
        nc = self.nc
        gp = self.GP
        engt = [gp.engine]
        base = f"role{self._armcnt}_{tag}"
        for arm in nc.switch_core_id(engines=[gp], n=128):
            nc.br(f"{base}_r{arm.logical % 2}", engines=engt)
        snap = {k: v.n for k, v in self.sems.items()}
        for role in (0, 1):
            for k, v in self.sems.items():
                v.n = snap[k]
            with nc.body(f"{base}_r{role}", valid_engines=engt):
                fn(role)
                nc.br(f"{base}_join", engines=engt)
            if role == 0:
                after0 = {k: v.n for k, v in self.sems.items()}
        after1 = {k: v.n for k, v in self.sems.items()}
        assert after0 == after1, f"arm {tag}: sem counts diverge {after0} vs {after1}"
        nc.switch_bb(f"{base}_join")

    def poll(self, tag, event, role):
        nc = self.nc
        gp = self.GP
        engt = [gp.engine]
        other = 1 - role
        lbl = f"poll{self._armcnt}_{tag}_{role}"
        fl = self._flag_sb
        nn = self._nonce_sb
        cnt = self._poll_cnt
        flag_reg = self._flag_reg
        nonce_reg = self._nonce_reg
        gp.reg_load(nonce_reg, nn[0:1, event:event + 1])
        nc.br(f"{lbl}_loop", engines=engt)
        with nc.body(f"{lbl}_loop", valid_engines=engt):
            gp.dma_start(out=fl[0:1, 0:1], in_=self.d["sh_flag"][event, other:other + 1, 0:1]
                         ).then_inc(self.sems["poll"].h, 16)
            gp.reg_add(cnt, cnt, 16)
            gp.wait_ge(self.sems["poll"].h, cnt)
            gp.reg_load(flag_reg, fl[0:1, 0:1])
            nc.br_cmp(flag_reg, nonce_reg, on_true=f"{lbl}_done", on_false=f"{lbl}_loop",
                      comp_op="IS_EQ", engines=engt)
        nc.switch_bb(f"{lbl}_done")

    def init_protocol(self, es):
        if getattr(self, "_prot_ready", False):
            return
        self._prot_ready = True
        es = self._ges
        nc = self.nc
        self._flag_sb = self.sb(es, "flag_sb", (1, 1), I32)
        self._nonce_sb = self.sb(es, "nonce_sb", (1, 18), I32)
        self._poll_cnt = es.enter_context(self.GP.register("pollcnt"))
        self._flag_reg = es.enter_context(self.GP.register("flagv"))
        self._nonce_reg = es.enter_context(self.GP.register("noncev"))
        sa = self.sems["aload"]
        sa.inc(self.SY.dma_start(out=self._nonce_sb[:], in_=self.d["nonce"][:, :]), 16)
        self.GP.reg_mov(self._poll_cnt, 0)
        sa.wait(self.GP)

    def run(self, in_map_list, **kw):
        self.es.close()
        return run_bass_kernel_spmd(self.nc, in_map_list, core_ids=list(range(8)), **kw)



def emit_ln0kvq(b: B):
    nc, PE, VE, SC, SY, GP = b.nc, b.PE, b.VE, b.SC, b.SY, b.GP
    sW, sA, sO = b.sems["wload"], b.sems["aload"], b.sems["store"]
    sPe, sVe, sSc = b.sems["pe"], b.sems["ve"], b.sems["sc"]
    with ExitStack() as es:
        b.init_protocol(es)
        sumcb = b.sb(es, "sumcb", (128, 1), BF16)
        eyeb = b.sb(es, "eyeb", (128, 128), BF16)
        xc = [b.sb(es, "xc0", (128, 16, 512), F32)]
        xbf = [b.sb(es, "xbf0", (128, 16, 512), BF16)]
        x2 = b.sb(es, "x2", (128, 16, 512), BF16)
        xnb = [b.sb(es, "xnb0", (128, 16, 512), BF16)]
        tmpf = b.sb(es, "tmpf", (128, 512), F32)
        murow = b.sb(es, "murow", (1, 512), F32)
        e2row = b.sb(es, "e2row", (1, 512), F32)
        varrow = b.sb(es, "varrow", (1, 512), F32)
        ivrow = b.sb(es, "ivrow", (1, 512), F32)
        evacb = [b.sb(es, f"evacb{i}", (128, 512), BF16) for i in range(2)]
        sqt2 = [b.sb(es, f"sqt{i}", (128, 512), BF16) for i in range(2)]
        vtb = [b.sb(es, f"vtb{i}", (128, 512), BF16) for i in range(2)]
        qnsl = b.sb(es, "qnsl", (1, 512), F32)
        kmax = [b.sb(es, f"kmax{g}", (1, 1), F32) for g in range(2)]
        kmt = b.sb(es, "kmt", (1, 1), F32)
        rowf = b.sb(es, "rowf", (1, T), F32)
        rowb = b.sb(es, "rowb", (1, T), BF16)
        sqbuf = b.sb(es, "sqbuf", (1, T), F32)
        kvw = [b.sb(es, f"kvw{i}", (128, 16, 128), BF16) for i in range(4)]
        qw = [b.sb(es, f"qw{i}", (128, 16, 128), BF16) for i in range(8)]

        ps_mu = b.ps(es, "ps_mu", (1, 512))
        ps_e2 = b.ps(es, "ps_e2", (1, 512))
        ps_mub = b.ps(es, "ps_mub", (128, 512))
        ps_ivb = b.ps(es, "ps_ivb", (128, 512))
        ps_p = [b.ps(es, f"ps_p{i}", (128, 512)) for i in range(2)]
        _pstr = b.ps(es, "ps_tr", (128, 512), BF16)
        ps_tr2 = [_pstr, _pstr]
        ps_nr = b.ps(es, "ps_nr", (1, 512))
        prj_evac = [0, 0]
        prj_store = [0, 0]
        sq_use = [0, 0]
        eb_use = [0, 0]
        nr_evac = 0
        nr_done = 0
        tr_evac = [0, 0]
        vt_store = [0, 0]
        qn_store = 0

        sW.inc(GP.dma_start(out=sumcb[:], in_=b.d["sumcol"][:, :]), 16)
        sW.inc(GP.dma_start(out=eyeb[:], in_=b.d["eye"][:, :]), 16)
        for ob in range(4):
            sW.inc(GP.dma_start(out=kvw[ob][:],
                                in_=b.d["wkv_t"][ob].rearrange("kb i o -> i kb o")), 16)
        for hh in range(8):
            sW.inc(GP.dma_start(out=qw[hh][:],
                                in_=b.d["wq_t"][hh].rearrange("kb i o -> i kb o")), 16)
        wready = sW.n
        VE.memset(kmax[0][:], 0.0)
        li = VE.memset(kmax[1][:], 0.0)
        sVe.seal(VE)

        for ch in range(8):
            sl = 512 * ch
            pb = 0
            if ch >= 1:
                SY.wait_ge(sPe.h, sPe.n)
                GP.wait_ge(sPe.h, sPe.n)
            la = sA.inc(SY.dma_start(
                out=xc[pb][:],
                in_=b.d["xf"][:, sl:sl + 512].rearrange("(t p) n -> p t n", p=128)), 16)
            lw = sW.inc(GP.dma_start(
                out=xbf[pb][:],
                in_=b.d["xf"][:, sl:sl + 512].rearrange("(t p) n -> p t n", p=128)), 16)
            SC.wait_ge(sW.h, lw)
            SC.wait_ge(sPe.h, sPe.n)
            for t in range(16):
                li = SC.activation(x2[:, t, :], xbf[pb][:, t, :], AF.Square)
            sSc.seal(SC)
            PE.wait_ge(sSc.h, sSc.n)
            PE.wait_ge(sVe.h, sVe.n)
            for t in range(16):
                PE.matmul(ps_mu[:], sumcb[:], xbf[pb][:, t, :], start=(t == 0), stop=(t == 15))
            for t in range(16):
                li = PE.matmul(ps_e2[:], sumcb[:], x2[:, t, :], start=(t == 0), stop=(t == 15))
            sPe.inc(li, 1)
            VE.wait_ge(sPe.h, sPe.n)
            VE.tensor_copy(murow[:], ps_mu[:])
            VE.tensor_copy(e2row[:], ps_e2[:])
            VE.drain()
            VE.tensor_mul(varrow[:], murow[:], murow[:])
            VE.drain()
            VE.tensor_sub(varrow[:], e2row[:], varrow[:])
            VE.drain()
            li = VE.tensor_scalar_add(varrow[:], varrow[:], EPS)
            sVe.seal(VE)
            SC.wait_ge(sVe.h, sVe.n)
            SC.activation(varrow[:], varrow[:], AF.Ln)
            SC.drain()
            li = SC.activation(ivrow[:], varrow[:], AF.Exp, scale=-0.5)
            sSc.seal(SC)
            PE.wait_ge(sSc.h, sSc.n)
            PE.matmul(ps_mub[:], b._ones_row_f32[:], murow[:], start=True, stop=True)
            li = PE.matmul(ps_ivb[:], b._ones_row_f32[:], ivrow[:], start=True, stop=True)
            sPe.inc(li, 1)
            VE.wait_ge(sPe.h, sPe.n)
            VE.wait_ge(sA.h, la)
            for t in range(16):
                VE.tensor_sub(tmpf[:], xc[pb][:, t, :], ps_mub[:])
                li = VE.tensor_mul(xnb[pb][:, t, :], tmpf[:], ps_ivb[:])
            sVe.seal(VE)
            SY.wait_ge(sVe.h, sVe.n)
            sO.inc(SY.dma_start(
                out=b.d["xnf"][:, sl:sl + 512].rearrange("(t p) n -> p t n", p=128),
                in_=xnb[pb][:]), 16)

            PE.wait_ge(sVe.h, sVe.n)
            PE.wait_ge(sW.h, wready)
            for ob in range(4):
                pp = ps_p[ob % 2]
                eb = evacb[ob % 2]
                sq = sqt2[ob % 2]
                PE.wait_ge(sSc.h, prj_evac[ob % 2])
                for kb in range(16):
                    li = PE.matmul(pp[:], kvw[ob][:, kb, :],
                                   xnb[pb][:, kb, :], start=(kb == 0), stop=(kb == 15))
                mm_done = sPe.inc(li, 1)
                SC.wait_ge(sPe.h, mm_done)
                SC.wait_ge(sO.h, prj_store[ob % 2])
                SC.wait_ge(sPe.h, max(sq_use[ob % 2], eb_use[ob % 2]))
                li = SC.activation(eb[:], pp[:], AF.Copy)
                if ob < 2:
                    li = SC.activation(sq[:], pp[:], AF.Square)
                prj_evac[ob % 2] = sSc.seal(SC)
                SY.wait_ge(sSc.h, prj_evac[ob % 2])
                if ob < 2:
                    prj_store[ob % 2] = sO.inc(SY.dma_start(
                        out=b.d["kT"][ob * 128:(ob + 1) * 128, sl:sl + 512],
                        in_=eb[:]), 16)
                    PE.wait_ge(sSc.h, prj_evac[ob % 2])
                    PE.wait_ge(sVe.h, nr_evac)
                    li = PE.matmul(ps_nr[:], b._ones_col_bf[:], sq[:], start=True, stop=True)
                    nr_done = sPe.inc(li, 1)
                    sq_use[ob % 2] = nr_done
                    VE.wait_ge(sPe.h, nr_done)
                    VE.reduce_max(kmt[:], ps_nr[:], axis=AX)
                    VE.drain()
                    li = VE.tensor_max(kmax[ob][:], kmax[ob][:], kmt[:])
                    nr_evac = sVe.seal(VE)
                else:
                    vb = vtb[ob % 2]
                    PE.wait_ge(sSc.h, prj_evac[ob % 2])
                    PE.wait_ge(sVe.h, max(tr_evac))
                    for tt in range(4):
                        li = PE.transpose(ps_tr2[ob % 2][:, tt * 128:(tt + 1) * 128],
                                          eb[:, tt * 128:(tt + 1) * 128], eyeb[:])
                    tr_done = sPe.inc(li, 1)
                    eb_use[ob % 2] = tr_done
                    VE.wait_ge(sPe.h, tr_done)
                    VE.wait_ge(sO.h, vt_store[ob % 2])
                    li = VE.tensor_copy(vb[:], ps_tr2[ob % 2][:])
                    tr_evac[ob % 2] = sVe.seal(VE)
                    SY.wait_ge(sVe.h, tr_evac[ob % 2])
                    for tt in range(4):
                        vt_store[ob % 2] = sO.inc(SY.dma_start(
                            out=b.d["vTok"][sl + tt * 128:sl + (tt + 1) * 128,
                                            (ob - 2) * 128:(ob - 1) * 128],
                            in_=vb[:, tt * 128:(tt + 1) * 128]), 16)
            for hh in range(8):
                pp = ps_p[hh % 2]
                eb = evacb[hh % 2]
                sq = sqt2[hh % 2]
                PE.wait_ge(sSc.h, prj_evac[hh % 2])
                for kb in range(16):
                    li = PE.matmul(pp[:], qw[hh][:, kb, :],
                                   xnb[pb][:, kb, :], start=(kb == 0), stop=(kb == 15))
                mm_done = sPe.inc(li, 1)
                SC.wait_ge(sPe.h, mm_done)
                SC.wait_ge(sO.h, prj_store[hh % 2])
                SC.wait_ge(sPe.h, sq_use[hh % 2])
                SC.activation(eb[:], pp[:], AF.Copy, scale=RS128)
                li = SC.activation(sq[:], pp[:], AF.Square, scale=RS128)
                prj_evac[hh % 2] = sSc.seal(SC)
                SY.wait_ge(sSc.h, prj_evac[hh % 2])
                prj_store[hh % 2] = sO.inc(SY.dma_start(
                    out=b.d["qsc"][hh * 128:(hh + 1) * 128, sl:sl + 512],
                    in_=eb[:]), 16)
                PE.wait_ge(sSc.h, prj_evac[hh % 2])
                PE.wait_ge(sVe.h, nr_evac)
                li = PE.matmul(ps_nr[:], b._ones_col_bf[:], sq[:], start=True, stop=True)
                nr_done = sPe.inc(li, 1)
                sq_use[hh % 2] = nr_done
                VE.wait_ge(sPe.h, nr_done)
                VE.wait_ge(sO.h, qn_store)
                li = VE.tensor_copy(qnsl[:], ps_nr[:])
                nr_evac = sVe.seal(VE)
                SY.wait_ge(sVe.h, nr_evac)
                qn_store = sO.inc(SY.dma_start(out=b.d["qn2"][0:1, hh * T + sl:hh * T + sl + 512],
                                               in_=qnsl[:]), 16)
        SC.wait_ge(sVe.h, sVe.n)
        for g in range(2):
            li = SC.activation(kmax[g][:], kmax[g][:], AF.Sqrt)
        sSc.seal(SC)
        VE.wait_ge(sSc.h, sSc.n)
        for g in range(2):
            li = VE.tensor_scalar_mul(kmax[g][:], kmax[g][:], 1.0)
            VE.drain()
        sVe.seal(VE)
        qnrow = rowf
        for hh in range(8):
            gl = hh % 2
            SY.wait_ge(sO.h, sO.n)
            SY.wait_ge(sSc.h, sSc.n)
            la = sA.inc(SY.dma_start(out=qnrow[:], in_=b.d["qn2"][0:1, hh * T:(hh + 1) * T]), 16)
            sA.inc(SY.dma_start(out=sqbuf[:], in_=b.d["sqrow"][0:1, hh * T:(hh + 1) * T]), 16)
            SC.wait_ge(sA.h, la)
            SC.wait_ge(sVe.h, sVe.n)
            li = SC.activation(qnrow[:], qnrow[:], AF.Sqrt)
            sSc.seal(SC)
            VE.wait_ge(sSc.h, sSc.n)
            VE.tensor_scalar(qnrow[:], qnrow[:], kmax[gl][0:1, 0:1], None, OP.mult)
            VE.drain()
            VE.wait_ge(sA.h, sA.n)
            li = VE.tensor_add(qnrow[:], qnrow[:], sqbuf[:])
            sVe.seal(VE)
            SC.wait_ge(sVe.h, sVe.n)
            li = SC.activation(rowb[:], qnrow[:], AF.Copy, scale=-1.0)
            sSc.seal(SC)
            SY.wait_ge(sSc.h, sSc.n)
            sO.inc(SY.dma_start(out=b.d["rng_"][0:1, hh * T:(hh + 1) * T], in_=rowb[:]), 16)

        def a0(role):
            GP.wait_ge(sO.h, sO.n)
            b.sems["prot"].inc(GP.dma_start(
                out=b.d["xno"][:, :],
                in_=b.d["xnf"][:, role * TOK:(role + 1) * TOK]), 16)
        b.arm_pair("a0", a0)
        b.sems["prot"].wait(GP)
        SY.wait_ge(sO.h, sO.n)
        SY.wait_ge(b.sems["prot"].h, b.sems["prot"].n)


def emit_ffn0(b: B):
    nc, PE, VE, SC, SY, GP = b.nc, b.PE, b.VE, b.SC, b.SY, b.GP
    sW, sA, sO = b.sems["wload"], b.sems["aload"], b.sems["store"]
    sPe, sVe, sSc = b.sems["pe"], b.sems["ve"], b.sems["sc"]
    with ExitStack() as es:
        xq = [b.sb(es, f"xq{i}", (128, 16, 512), BF16) for i in range(2)]
        acc = b.sb(es, "acc", (128, 16, 512), F32)
        hbuf = [b.sb(es, f"hb{i}", (128, 512), F32) for i in range(8)]
        hs = [b.sb(es, f"hs{i}", (128, 512), BF16) for i in range(8)]
        wgw = [b.sb(es, f"wgw{i}", (128, 16, 128), BF16) for i in range(2)]
        wuw = [b.sb(es, f"wuw{i}", (128, 16, 128), BF16) for i in range(2)]
        wdw = [b.sb(es, f"wdw{i}", (128, 16, 128), BF16) for i in range(8)]
        sg = [b.sb(es, f"sg{i}", (128, 512), F32) for i in range(2)]
        af = [b.sb(es, f"af{i}", (128, 512), F32) for i in range(4)]
        cc = [b.sb(es, f"cc{i}", (128, 512), BF16) for i in range(6)]
        st = b.sb(es, "st", (128, 512), BF16)
        kp = [b.sb(es, f"kp{i}", (128, 512), BF16) for i in range(4)]
        g0t = [b.sb(es, f"g0t{i}", (128, 512), F32) for i in range(2)]
        ps_g = [b.ps(es, f"ps_g{i}", (128, 512)) for i in range(2)]
        ps_u = [b.ps(es, f"ps_u{i}", (128, 512)) for i in range(2)]
        ps_w = [b.ps(es, f"ps_w{i}", (128, 512)) for i in range(4)]

        xq_use = [0, 0]
        slab_use = [0, 0]
        gu_evac = [0, 0]
        wd_use = [0] * 8
        hs_use = 0
        win_evac = [0, 0]
        g0_store = [0, 0]
        acc_free = 0

        for q in range(4):
            ql = q * 512
            qb = q % 2
            GP.wait_ge(sPe.h, xq_use[qb])
            lq = sW.inc(GP.dma_start(
                out=xq[qb][:],
                in_=b.d["xno"][:, ql:ql + 512].rearrange("(kb p) n -> p kb n", p=128)), 16)
            for grp in range(8):
                cbs = [2 * grp + o for o in (0, 16, 32, 48)] + \
                      [2 * grp + 1 + o for o in (0, 16, 32, 48)]
                wdld = []
                for ci, cb in enumerate(cbs):
                    GP.wait_ge(sPe.h, wd_use[ci])
                    wdld.append(sW.inc(GP.dma_start(
                        out=wdw[ci][:],
                        in_=b.d["wd_t"][cb].rearrange("mb i o -> i mb o")), 16))
                mm_done = [0] * 8
                for ci, cb in enumerate(cbs):
                    pb = ci % 2
                    GP.wait_ge(sPe.h, slab_use[pb])
                    sW.inc(GP.dma_start(
                        out=wgw[pb][:],
                        in_=b.d["wg_t"][cb].rearrange("kb i o -> i kb o")), 16)
                    lw = sW.inc(GP.dma_start(
                        out=wuw[pb][:],
                        in_=b.d["wu_t"][cb].rearrange("kb i o -> i kb o")), 16)
                    PE.wait_ge(sW.h, lw)
                    if ci == 0:
                        PE.wait_ge(sW.h, lq)
                    PE.wait_ge(sVe.h, gu_evac[pb])
                    for kb in range(16):
                        PE.matmul(ps_g[pb][:], wgw[pb][:, kb, :], xq[qb][:, kb, :],
                                  start=(kb == 0), stop=(kb == 15))
                    for kb in range(16):
                        li = PE.matmul(ps_u[pb][:], wuw[pb][:, kb, :], xq[qb][:, kb, :],
                                       start=(kb == 0), stop=(kb == 15))
                    mm_done[ci] = sPe.inc(li, 1)
                    slab_use[pb] = mm_done[ci]
                    SC.wait_ge(sPe.h, mm_done[ci])
                    li = SC.activation(sg[pb][:], ps_g[pb][:], AF.Silu)
                    sc_done = sSc.seal(SC)
                    VE.wait_ge(sSc.h, sc_done)
                    VE.wait_ge(sPe.h, hs_use)
                    li = VE.tensor_mul(hbuf[ci][:], sg[pb][:], ps_u[pb][:])
                    gu_evac[pb] = sVe.seal(VE)
                xq_use[qb] = mm_done[7]
                for qd in range(2):
                    hh4 = hbuf[qd * 4:qd * 4 + 4]
                    SC.wait_ge(sVe.h, sVe.n)
                    for j in range(4):
                        li = SC.activation(af[j][:], hh4[j][:], AF.Abs)
                    sSc.seal(SC)
                    VE.wait_ge(sSc.h, sSc.n)
                    VE.wait_ge(sPe.h, hs_use)
                    pairs = [(0, 1), (0, 2), (0, 3), (1, 2), (1, 3), (2, 3)]
                    for ci2, (aa, bb) in enumerate(pairs):
                        VE.tensor_tensor(cc[ci2][:], af[aa][:], af[bb][:], OP.is_ge)
                    VE.drain()
                    VE.tensor_add(st[:], cc[0][:], cc[1][:])
                    VE.drain()
                    VE.tensor_add(st[:], st[:], cc[2][:])
                    VE.drain()
                    VE.tensor_scalar(kp[0][:], st[:], 2.0, None, OP.is_ge)
                    VE.tensor_add(st[:], cc[3][:], cc[4][:])
                    VE.drain()
                    VE.tensor_sub(st[:], st[:], cc[0][:])
                    VE.drain()
                    VE.tensor_scalar(kp[1][:], st[:], 1.0, None, OP.is_ge)
                    VE.tensor_add(st[:], cc[1][:], cc[3][:])
                    VE.drain()
                    VE.tensor_sub(st[:], cc[5][:], st[:])
                    VE.drain()
                    VE.tensor_scalar(kp[2][:], st[:], 0.0, None, OP.is_ge)
                    VE.tensor_add(st[:], cc[2][:], cc[4][:])
                    VE.drain()
                    VE.tensor_add(st[:], st[:], cc[5][:])
                    VE.drain()
                    VE.tensor_scalar(kp[3][:], st[:], 1.0, None, OP.is_le)
                    VE.drain()
                    for j in range(4):
                        li = VE.tensor_mul(hs[qd * 4 + j][:], hh4[j][:], kp[j][:])
                    sVe.seal(VE)
                hs_ready = sVe.n
                PE.wait_ge(sW.h, wdld[-1])
                PE.wait_ge(sVe.h, hs_ready)
                for wi0, w0 in enumerate(range(0, 16, 2)):
                    ms = [w0, w0 + 1]
                    bs = (wi0 % 2) * 2
                    PE.wait_ge(sVe.h, win_evac[wi0 % 2])
                    for ci in range(8):
                        for mi, m in enumerate(ms):
                            li = PE.matmul(ps_w[bs + mi][:], wdw[ci][:, m, :], hs[ci][:],
                                           start=(ci == 0), stop=(ci == 7))
                    wd_done = sPe.inc(li, 1)
                    VE.wait_ge(sPe.h, wd_done)
                    if grp == 0:
                        VE.wait_ge(sSc.h, acc_free)
                    for mi, m in enumerate(ms):
                        if grp == 0:
                            li = VE.tensor_copy(acc[:, m, :], ps_w[bs + mi][:])
                        else:
                            li = VE.tensor_add(acc[:, m, :], acc[:, m, :], ps_w[bs + mi][:])
                    win_evac[wi0 % 2] = sVe.seal(VE)
                for ci in range(8):
                    wd_use[ci] = sPe.n
                hs_use = sPe.n
            SC.wait_ge(sVe.h, sVe.n)
            for m in range(16):
                gb = m % 2
                SC.wait_ge(sO.h, g0_store[gb])
                li = SC.activation(g0t[gb][:], acc[:, m, :], AF.Silu)
                sc_done = sSc.seal(SC)
                SY.wait_ge(sSc.h, sc_done)
                g0_store[gb] = sO.inc(SY.dma_start(
                    out=b.d["g0"][m * 128:(m + 1) * 128, ql:ql + 512], in_=g0t[gb][:]), 16)
            acc_free = sSc.n
        SY.wait_ge(sO.h, sO.n)
        GP.wait_ge(sO.h, sO.n)


def emit_attn(b: B):
    nc, PE, VE, SC, SY, GP = b.nc, b.PE, b.VE, b.SC, b.SY, b.GP
    sW, sA, sO = b.sems["wload"], b.sems["aload"], b.sems["store"]
    sPe, sVe, sSc = b.sems["pe"], b.sems["ve"], b.sems["sc"]
    sPr = b.sems["prot"]
    with ExitStack() as es:
        b.init_protocol(es)
        aofull = b.sb(es, "aofull", (128, 16, TOK), BF16)
        with ExitStack() as es1:
            aosb = b.sb(es1, "aosb", (128, 8, T), BF16)
            ktsb = b.sb(es1, "ktsb", (128, 2, T), BF16)
            vtsb = b.sb(es1, "vtsb", (128, 32, 256), BF16)
            qsb = b.sb(es1, "qsb", (128, T), BF16)
            masksb = b.sb(es1, "masksb", (128, 4, 512), BF16)
            alibsb = b.sb(es1, "alibsb", (128, 256), F32)
            rn = b.sb(es1, "rn", (1, T), BF16)
            att = [b.sb(es1, f"att{i}", (128, 512), BF16) for i in range(3)]
            denrow = b.sb(es1, "denrow", (1, 512), F32)
            avsb = b.sb(es1, "avsb", (128, 512), BF16)
            ps_s = [b.ps(es1, f"ps_s{i}", (128, 512)) for i in range(2)]
            ps_den = b.ps(es1, "ps_den", (1, 512))
            ps_av = b.ps(es1, "ps_av", (128, 512))
            ps_bc = b.ps(es1, "ps_bc", (128, 512))

            lkm = sW.inc(GP.dma_start(out=masksb[:], in_=b.d["maskc"].rearrange("v p n -> p v n")), 16)
            la = sA.inc(SY.dma_start(out=alibsb[:], in_=b.d["alibic"][:, :]), 16)
            SC.wait_ge(sA.h, la)
            VE.wait_ge(sA.h, la)
            VE.wait_ge(sW.h, lkm)
            lk = sW.inc(GP.dma_start(out=ktsb[:], in_=b.d["kT"].rearrange("(g p) n -> p g n", p=128)), 16)
            lk = sW.inc(GP.dma_start(out=vtsb[:], in_=b.d["vTok"].rearrange("(tb p) h -> p tb h", p=128)), 16)
            qsb2 = [qsb, b.sb(es1, "qsb_b", (128, T), BF16)]
            rn2 = [rn, b.sb(es1, "rn_b", (1, T), BF16)]
            q_use = [0, 0]
            exp_done = {}
            denav_done = {}
            mask_done = {}
            fin_evac = 0
            av_free = 0
            for hh in range(8):
                gl = hh % 2
                hb = hh % 2
                SY.wait_ge(sPe.h, q_use[hb])
                lq = sA.inc(SY.dma_start(out=qsb2[hb][:], in_=b.d["qsc"][hh * 128:(hh + 1) * 128, :]), 16)
                lr = sA.inc(SY.dma_start(out=rn2[hb][:], in_=b.d["rng_"][0:1, hh * T:(hh + 1) * T]), 16)
                first_pe = True
                for qc in range(8):
                    nkb = 4 * (qc + 1)

                    def scores(kb):
                        pb = kb % 2
                        if kb >= 2:
                            PE.wait_ge(sSc.h, exp_done[kb - 2])
                        if first_pe:
                            PE.wait_ge(sA.h, lr)
                            PE.wait_ge(sW.h, lk)
                        PE.matmul(ps_s[pb][:], ktsb[:, gl, kb * 128:(kb + 1) * 128],
                                  qsb2[hb][:, qc * 512:(qc + 1) * 512], start=True, stop=False)
                        li = PE.matmul(ps_s[pb][:], b._ones_row_bf[:],
                                       rn2[hb][0:1, qc * 512:(qc + 1) * 512], start=False, stop=True)
                        sreq = sPe.inc(li, 1)
                        if kb >= 4 * qc:
                            VE.wait_ge(sPe.h, sreq)
                            li = VE.tensor_add(ps_s[pb][:], ps_s[pb][:], masksb[:, kb - 4 * qc, :])
                            mask_done[kb] = sVe.seal(VE)
                        else:
                            mask_done[kb] = None
                        return sreq

                    def do_exp(kb, sreq):
                        if kb >= 3 and (kb - 3) in denav_done:
                            SC.wait_ge(sPe.h, denav_done[kb - 3])
                        if mask_done[kb] is not None:
                            SC.wait_ge(sVe.h, mask_done[kb])
                        else:
                            SC.wait_ge(sPe.h, sreq)
                        li = SC.activation(att[kb % 3][:], ps_s[kb % 2][:], AF.Exp,
                                           bias=alibsb[:, hh * 32 + kb:hh * 32 + kb + 1])
                        exp_done[kb] = sSc.seal(SC)

                    def denav(kb):
                        PE.wait_ge(sSc.h, exp_done[kb])
                        if kb == 0:
                            PE.wait_ge(sSc.h, av_free)
                            PE.wait_ge(sVe.h, fin_evac)
                        PE.matmul(ps_den[:], b._ones_col_bf[:], att[kb % 3][:],
                                  start=(kb == 0), stop=(kb == nkb - 1))
                        li = PE.matmul(ps_av[:], vtsb[:, kb, gl * 128:(gl + 1) * 128],
                                       att[kb % 3][:], start=(kb == 0), stop=(kb == nkb - 1))
                        denav_done[kb] = sPe.inc(li, 1)

                    exp_done.clear(); denav_done.clear(); mask_done.clear()
                    sreq_prev = scores(0)
                    first_pe = False
                    do_exp(0, sreq_prev)
                    for kb in range(1, nkb):
                        sreq = scores(kb)
                        denav(kb - 1)
                        do_exp(kb, sreq)
                    denav(nkb - 1)
                    q_use[hb] = sPe.n
                    SC.wait_ge(sPe.h, denav_done[nkb - 1])
                    SC.wait_ge(sVe.h, fin_evac)
                    SC.activation(denrow[:], ps_den[:], AF.Ln)
                    SC.drain()
                    li = SC.activation(denrow[:], denrow[:], AF.Exp, scale=-1.0)
                    rd_done = sSc.seal(SC)
                    PE.wait_ge(sSc.h, rd_done)
                    PE.wait_ge(sVe.h, fin_evac)
                    li = PE.matmul(ps_bc[:], b._ones_row_f32[:], denrow[:], start=True, stop=True)
                    bc_done = sPe.inc(li, 1)
                    li = SC.activation(avsb[:], ps_av[:], AF.Copy)
                    av_done = sSc.seal(SC)
                    av_free = av_done
                    VE.wait_ge(sPe.h, bc_done)
                    VE.wait_ge(sSc.h, av_done)
                    li = VE.tensor_mul(aosb[:, hh, qc * 512:(qc + 1) * 512], avsb[:], ps_bc[:])
                    fin_evac = sVe.seal(VE)
            def a1(role):
                other_cols = slice((1 - role) * TOK, (2 - role) * TOK)
                GP.wait_ge(sVe.h, sVe.n)
                k = sPr.inc(GP.dma_start(
                    out=b.d["sh_ao"][role].rearrange("(hh p) n -> p hh n", p=128),
                    in_=aosb[:, :, other_cols]), 16)
                GP.wait_ge(sPr.h, k)
                sPr.inc(GP.dma_start(
                    out=b.d["sh_flag"][0, role:role + 1, 0:1],
                    in_=b._nonce_sb[0:1, 0:1]), 16)
            b.arm_pair("a1", a1)

            def a2(role):
                own_cols = slice(role * TOK, (role + 1) * TOK)
                b.poll("a2", 0, role)
                sPr.inc(GP.dma_start(
                    out=aofull[:, 0:8, :],
                    in_=aosb[:, :, own_cols]), 16)
                sPr.inc(GP.dma_start(
                    out=aofull[:, 8:16, :],
                    in_=b.d["sh_ao"][1 - role].rearrange("(hh p) n -> p hh n", p=128)), 16)
                GP.wait_ge(sPr.h, sPr.n)
            b.arm_pair("a2", a2)
        b.barrier()
        with ExitStack() as es2:
            wosb = b.sb(es2, "wosb", (128, 16, 2048), BF16)
            bo0 = b.sb(es2, "bo0", (128, 16), F32)
            g0t = b.sb(es2, "g0t", (128, 512), F32)
            xot = b.sb(es2, "xot", (128, 512), F32)
            s_t = b.sb(es2, "s_t", (128, 512), F32)
            x1t = b.sb(es2, "x1t", (128, 512), F32)
            ps_o = [b.ps(es2, f"ps_o{i}", (128, 512)) for i in range(2)]
            lw = sW.inc(GP.dma_start(out=wosb[:], in_=b.d["wo0_t"].rearrange("kb i mb o -> i kb (mb o)")), 16)
            la = sA.inc(SY.dma_start(out=bo0[:], in_=b.d["bo0c"][:, :]), 16)
            GP.wait_ge(sPr.h, sPr.n)
            PE.wait_ge(sW.h, sW.n)
            PE.wait_ge(sA.h, sA.n)
            PE.wait_ge(sPr.h, sPr.n)
            for n in range(4):
                nl = n * 512
                for m in range(16):
                    pb = m % 2
                    PE.wait_ge(sVe.h, sVe.n)
                    for kb in range(16):
                        li = PE.matmul(ps_o[pb][:], wosb[:, kb, m * 128:(m + 1) * 128],
                                       aofull[:, kb, nl:nl + 512], start=(kb == 0), stop=(kb == 15))
                    sPe.inc(li, 1)
                    SY.wait_ge(sVe.h, sVe.n)
                    la = sA.inc(SY.dma_start(out=g0t[:], in_=b.d["g0"][m * 128:(m + 1) * 128, nl:nl + 512]), 16)
                    la = sA.inc(SY.dma_start(out=xot[:], in_=b.d["xo"][m * 128:(m + 1) * 128, nl:nl + 512]), 16)
                    VE.wait_ge(sPe.h, sPe.n)
                    VE.wait_ge(sA.h, la)
                    VE.wait_ge(sO.h, sO.n)
                    VE.tensor_scalar(s_t[:], ps_o[pb][:], bo0[:, m:m + 1], None, OP.add)
                    VE.tensor_mul(x1t[:], s_t[:], g0t[:])
                    li = VE.tensor_add(x1t[:], x1t[:], xot[:])
                    sVe.seal(VE)
                    SY.wait_ge(sVe.h, sVe.n)
                    sO.inc(SY.dma_start(out=b.d["x1"][m * 128:(m + 1) * 128, nl:nl + 512],
                                        in_=x1t[:]), 16)
            SY.wait_ge(sO.h, sO.n)
        GP.wait_ge(sO.h, sO.n)


def emit_ln1mix(b: B):
    nc, PE, VE, SC, SY, GP = b.nc, b.PE, b.VE, b.SC, b.SY, b.GP
    sW, sA, sO = b.sems["wload"], b.sems["aload"], b.sems["store"]
    sPe, sVe, sSc = b.sems["pe"], b.sems["ve"], b.sems["sc"]
    sPr = b.sems["prot"]
    with ExitStack() as es:
        b.init_protocol(es)
        sumcb = b.sb(es, "sumcb", (128, 1), BF16)
        cols = b.sb(es, "cols", (128, 8, 16), F32)
        bcol = b.sb(es, "bcol", (128, 16), BF16)
        mcol = b.sb(es, "mcol", (128, 1), F32)
        sW.inc(GP.dma_start(out=sumcb[:], in_=b.d["sumcol"][:, :]), 16)
        for i, nm in enumerate(("bkc", "bvc", "brc", "uc", "tmkc", "tmvc", "tmrc")):
            sA.inc(SY.dma_start(out=cols[:, i, :], in_=b.d[nm][:, :]), 16)
        sA.inc(SY.dma_start(out=mcol[:], in_=b.d["maskcol"][:, :]), 16)
        with ExitStack() as es1:
            xc = b.sb(es1, "xc", (128, 16, 512), F32)
            x2 = b.sb(es1, "x2", (128, 16, 512), BF16)
            xbf = b.sb(es1, "xbf", (128, 16, 512), BF16)
            xnb = b.sb(es1, "xnb", (128, 16, 512), BF16)
            tmpf = b.sb(es1, "tmpf", (128, 512), F32)
            murow = b.sb(es1, "murow", (1, 512), F32)
            e2row = b.sb(es1, "e2row", (1, 512), F32)
            varrow = b.sb(es1, "varrow", (1, 512), F32)
            ivrow = b.sb(es1, "ivrow", (1, 512), F32)
            ps_mu = b.ps(es1, "ps_mu", (1, 512))
            ps_e2 = b.ps(es1, "ps_e2", (1, 512))
            ps_mub = b.ps(es1, "ps_mub", (128, 512))
            ps_ivb = b.ps(es1, "ps_ivb", (128, 512))
            for ch in range(4):
                sl = 512 * ch
                if ch >= 1:
                    SY.wait_ge(sVe.h, sVe.n)
                    GP.wait_ge(sPe.h, sPe.n)
                la = sA.inc(SY.dma_start(
                    out=xc[:], in_=b.d["x1"][:, sl:sl + 512].rearrange("(t p) n -> p t n", p=128)), 16)
                lw = sW.inc(GP.dma_start(
                    out=xbf[:], in_=b.d["x1"][:, sl:sl + 512].rearrange("(t p) n -> p t n", p=128)), 16)
                SC.wait_ge(sW.h, lw)
                SC.wait_ge(sPe.h, sPe.n)
                for t in range(16):
                    li = SC.activation(x2[:, t, :], xbf[:, t, :], AF.Square)
                sSc.seal(SC)
                PE.wait_ge(sSc.h, sSc.n)
                PE.wait_ge(sW.h, sW.n)
                PE.wait_ge(sVe.h, sVe.n)
                for t in range(16):
                    PE.matmul(ps_mu[:], sumcb[:], xbf[:, t, :], start=(t == 0), stop=(t == 15))
                for t in range(16):
                    li = PE.matmul(ps_e2[:], sumcb[:], x2[:, t, :], start=(t == 0), stop=(t == 15))
                sPe.inc(li, 1)
                VE.wait_ge(sPe.h, sPe.n)
                VE.tensor_copy(murow[:], ps_mu[:])
                VE.tensor_copy(e2row[:], ps_e2[:])
                VE.drain()
                VE.tensor_mul(varrow[:], murow[:], murow[:])
                VE.drain()
                VE.tensor_sub(varrow[:], e2row[:], varrow[:])
                VE.drain()
                li = VE.tensor_scalar_add(varrow[:], varrow[:], EPS)
                sVe.seal(VE)
                SC.wait_ge(sVe.h, sVe.n)
                SC.activation(varrow[:], varrow[:], AF.Ln)
                SC.drain()
                li = SC.activation(ivrow[:], varrow[:], AF.Exp, scale=-0.5)
                sSc.seal(SC)
                PE.wait_ge(sSc.h, sSc.n)
                PE.matmul(ps_mub[:], b._ones_row_f32[:], murow[:], start=True, stop=True)
                li = PE.matmul(ps_ivb[:], b._ones_row_f32[:], ivrow[:], start=True, stop=True)
                sPe.inc(li, 1)
                VE.wait_ge(sPe.h, sPe.n)
                VE.wait_ge(sA.h, la)
                VE.wait_ge(sO.h, sO.n)
                for t in range(16):
                    VE.tensor_sub(tmpf[:], xc[:, t, :], ps_mub[:])
                    li = VE.tensor_mul(xnb[:, t, :], tmpf[:], ps_ivb[:])
                sVe.seal(VE)
                SY.wait_ge(sVe.h, sVe.n)
                sO.inc(SY.dma_start(
                    out=b.d["xn1"][:, sl:sl + 512].rearrange("(t p) n -> p t n", p=128),
                    in_=xnb[:]), 16)
        def a3(role):
            GP.wait_ge(sO.h, sO.n)
            k = sPr.inc(GP.dma_start(
                out=b.d["sh_bcol"][role],
                in_=b.d["xn1"][:, TOK - 1:TOK].rearrange("(t p) a -> p (t a)", p=128),
                allow_slow_non_contiguous=True), 16)
            GP.wait_ge(sPr.h, k)
            k = sPr.inc(GP.dma_start(
                out=b.d["sh_flag"][1, role:role + 1, 0:1],
                in_=b._nonce_sb[0:1, 1:2]), 16)
            b.poll("a3", 1, role)
            k = sPr.inc(GP.dma_start(out=bcol[:], in_=b.d["sh_bcol"][1 - role]), 16)
            GP.wait_ge(sPr.h, k)
        b.arm_pair("a3", a3)

        b.barrier()
        with ExitStack() as es2:
            wp = b.sb(es2, "wp", (128, 16, 2048), BF16)
            xn1c = b.sb(es2, "xn1c", (128, 16, 512), BF16)
            xsc = b.sb(es2, "xsc", (128, 16, 512), BF16)
            mixc = b.sb(es2, "mixc", (128, 16, 512), BF16)
            evac = [b.sb(es2, f"pevac{i}", (128, 512), BF16) for i in range(2)]
            bku2 = [b.sb(es2, f"bku{i}", (128, 1), F32) for i in range(2)]
            ps_p = [b.ps(es2, f"ps_p{i}", (128, 512)) for i in range(2)]
            prj_evac = [0, 0]
            prj_store = [0, 0]
            VE.wait_ge(sPr.h, sPr.n)
            for pi, (wnm, onm, tmi, bi) in enumerate((
                    ("wk_t", "kt1", 4, 0), ("wv_t", "v1", 5, 1), ("wr_t", "r1", 6, 2))):
                GP.wait_ge(sPe.h, sPe.n)
                lw = sW.inc(GP.dma_start(out=wp[:], in_=b.d[wnm].rearrange("kb i mb o -> i kb (mb o)")), 16)
                for n in range(4):
                    nl = n * 512
                    SY.wait_ge(sVe.h, sVe.n)
                    la = sA.inc(SY.dma_start(
                        out=xn1c[:],
                        in_=b.d["xn1"][:, nl:nl + 512].rearrange("(t p) n -> p t n", p=128)), 16)
                    if n == 0:
                        la = sA.inc(SY.dma_start(
                            out=xsc[:, :, 1:512],
                            in_=b.d["xn1"][:, 0:511].rearrange("(t p) n -> p t n", p=128)), 16)
                    else:
                        la = sA.inc(SY.dma_start(
                            out=xsc[:],
                            in_=b.d["xn1"][:, nl - 1:nl + 511].rearrange("(t p) n -> p t n", p=128)), 16)
                    VE.wait_ge(sA.h, la)
                    if n == 0:
                        for t in range(16):
                            VE.tensor_scalar(xsc[:, t, 0:1], bcol[:, t:t + 1], mcol[:, 0:1], None, OP.mult)
                    for t in range(16):
                        VE.tensor_sub(mixc[:, t, :], xn1c[:, t, :], xsc[:, t, :])
                        li = VE.scalar_tensor_tensor(
                            mixc[:, t, :], mixc[:, t, :], cols[:, tmi, t:t + 1],
                            xsc[:, t, :], OP.mult, OP.add)
                    sVe.seal(VE)
                    if pi == 2:
                        SC.wait_ge(sVe.h, sVe.n)
                        for t in range(16):
                            li = SC.activation(mixc[:, t, :], mixc[:, t, :], AF.Silu)
                        sSc.seal(SC)
                        PE.wait_ge(sSc.h, sSc.n)
                    PE.wait_ge(sVe.h, sVe.n)
                    PE.wait_ge(sW.h, lw)
                    for m in range(16):
                        pb = m % 2
                        PE.wait_ge(sVe.h, prj_evac[pb])
                        for kb in range(16):
                            li = PE.matmul(ps_p[pb][:], wp[:, kb, m * 128:(m + 1) * 128],
                                           mixc[:, kb, :], start=(kb == 0), stop=(kb == 15))
                        mm_done = sPe.inc(li, 1)
                        eb = evac[pb]
                        if pi == 0:
                            VE.tensor_tensor(bku2[pb][:], cols[:, 0, m:m + 1], cols[:, 3, m:m + 1], OP.add)
                        else:
                            VE.tensor_scalar(bku2[pb][:], cols[:, bi, m:m + 1], 0.0, None, OP.add)
                        VE.drain()
                        VE.wait_ge(sPe.h, mm_done)
                        VE.wait_ge(sO.h, prj_store[pb])
                        li = VE.tensor_scalar(eb[:], ps_p[pb][:], bku2[pb][:, 0:1], None, OP.add)
                        prj_evac[pb] = sVe.seal(VE)
                        SY.wait_ge(sVe.h, prj_evac[pb])
                        prj_store[pb] = sO.inc(SY.dma_start(
                            out=b.d[onm][m * 128:(m + 1) * 128, nl:nl + 512], in_=eb[:]), 16)
        SY.wait_ge(sO.h, sO.n)
        GP.wait_ge(sO.h, sO.n)


def emit_wkv(b: B):
    nc, PE, VE, SC, SY, GP = b.nc, b.PE, b.VE, b.SC, b.SY, b.GP
    sW, sA, sO = b.sems["wload"], b.sems["aload"], b.sems["store"]
    sPe, sVe, sSc = b.sems["pe"], b.sems["ve"], b.sems["sc"]
    sPr = b.sems["prot"]
    with ExitStack() as es:
        b.init_protocol(es)
        kt = b.sb(es, "kt", (128, TOK), BF16)
        vt = b.sb(es, "vt", (128, TOK), BF16)
        rt = b.sb(es, "rt", (128, TOK), BF16)
        tau = b.sb(es, "tau", (128, TOK), F32)
        e1 = b.sb(es, "e1", (128, TOK), F32)
        e2 = b.sb(es, "e2", (128, TOK), F32)
        e2v = b.sb(es, "e2v", (128, TOK), F32)
        ba = b.sb(es, "ba", (128, TOK), F32)
        bb = b.sb(es, "bb", (128, TOK), F32)
        al = b.sb(es, "al", (128, TOK), F32)
        zz = b.sb(es, "zz", (128, TOK), F32)
        rwk = b.sb(es, "rwk", (128, TOK), BF16)
        wcol = b.sb(es, "wcol", (128, 16), F32)
        mcol = b.sb(es, "mcol", (128, 1), F32)
        pkt = b.sb(es, "pkt", (128, 2), F32)
        rcv = b.sb(es, "rcv", (128, 2), F32)
        a0c = b.sb(es, "a0c", (128, 2), F32)
        sA.inc(SY.dma_start(out=wcol[:], in_=b.d["wc"][:, :]), 16)
        sA.inc(SY.dma_start(out=mcol[:], in_=b.d["maskcol"][:, :]), 16)
        VE.wait_ge(sA.h, sA.n)
        li = VE.memset(zz[:], 0.0)
        sVe.seal(VE)
        for dt in range(16):
            rs = slice(dt * 128, (dt + 1) * 128)
            SY.wait_ge(sVe.h, sVe.n)
            la = sA.inc(SY.dma_start(out=kt[:], in_=b.d["kt1"][rs, :]), 16)
            la = sA.inc(SY.dma_start(out=vt[:], in_=b.d["v1"][rs, :]), 16)
            la = sA.inc(SY.dma_start(out=rt[:], in_=b.d["r1"][rs, :]), 16)
            VE.wait_ge(sA.h, la)
            VE.tensor_scalar(tau[:], kt[:], wcol[:, dt:dt + 1], None, OP.max)
            VE.drain()
            li = VE.tensor_scalar(e1[:], tau[:], wcol[:, dt:dt + 1], None, OP.add)
            sVe.seal(VE)
            SC.wait_ge(sVe.h, sVe.n)
            li = SC.activation(e1[:], e1[:], AF.Exp, scale=-1.0)
            sSc.seal(SC)
            VE.wait_ge(sSc.h, sSc.n)
            li = VE.tensor_sub(e2[:], kt[:], tau[:])
            sVe.seal(VE)
            SC.wait_ge(sVe.h, sVe.n)
            li = SC.activation(e2[:], e2[:], AF.Exp)
            sSc.seal(SC)
            VE.wait_ge(sSc.h, sSc.n)
            VE.tensor_mul(e2v[:], e2[:], vt[:])
            VE.tensor_tensor_scan(ba[:], e1[:], e2v[:], 0.0, OP.mult, OP.add)
            VE.tensor_tensor_scan(bb[:], e1[:], e2[:], 0.0, OP.mult, OP.add)
            VE.tensor_tensor_scan(al[:], e1[:], zz[:], 1.0, OP.mult, OP.add)
            VE.tensor_copy(pkt[:, 0:1], ba[:, TOK - 1:TOK])
            li = VE.tensor_copy(pkt[:, 1:2], bb[:, TOK - 1:TOK])
            sVe.seal(VE)

            def a4(role, _dt=dt):
                GP.wait_ge(sVe.h, sVe.n)
                k = sPr.inc(GP.dma_start(
                    out=b.d["sh_carry"][role][:, 2 * _dt:2 * _dt + 2], in_=pkt[:]), 16)
                GP.wait_ge(sPr.h, k)
                k = sPr.inc(GP.dma_start(
                    out=b.d["sh_flag"][2 + _dt, role:role + 1, 0:1],
                    in_=b._nonce_sb[0:1, 2 + _dt:3 + _dt]), 16)
                b.poll(f"a4_{_dt}", 2 + _dt, role)
                k = sPr.inc(GP.dma_start(
                    out=rcv[:], in_=b.d["sh_carry"][1 - role][:, 2 * _dt:2 * _dt + 2]), 16)
                GP.wait_ge(sPr.h, k)
            b.arm_pair(f"a4_{dt}", a4)
            VE.wait_ge(sPr.h, sPr.n)
            VE.tensor_scalar(a0c[:], rcv[:], mcol[:, 0:1], None, OP.mult)
            VE.drain()
            VE.scalar_tensor_tensor(ba[:], al[:], a0c[:, 0:1], ba[:], OP.mult, OP.add)
            li = VE.scalar_tensor_tensor(bb[:], al[:], a0c[:, 1:2], bb[:], OP.mult, OP.add)
            sVe.seal(VE)
            SC.wait_ge(sVe.h, sVe.n)
            SC.activation(bb[:], bb[:], AF.Ln)
            SC.drain()
            li = SC.activation(bb[:], bb[:], AF.Exp, scale=-1.0)
            sSc.seal(SC)
            VE.wait_ge(sSc.h, sSc.n)
            VE.tensor_mul(ba[:], ba[:], bb[:])
            li = VE.tensor_mul(rwk[:], ba[:], rt[:])
            sVe.seal(VE)
            SY.wait_ge(sVe.h, sVe.n)
            sO.inc(SY.dma_start(out=b.d["rwkv"][rs, :], in_=rwk[:]), 16)
        SY.wait_ge(sO.h, sO.n)
        GP.wait_ge(sO.h, sO.n)


def emit_moe(b: B):
    nc, PE, VE, SC, SY, GP = b.nc, b.PE, b.VE, b.SC, b.SY, b.GP
    sW, sA, sO = b.sems["wload"], b.sems["aload"], b.sems["store"]
    sPe, sVe, sSc = b.sems["pe"], b.sems["ve"], b.sems["sc"]
    with ExitStack() as es:
        with ExitStack() as es1:
            wp = b.sb(es1, "wp", (128, 16, 2048), BF16)
            rwc = b.sb(es1, "rwc", (128, 16, 512), BF16)
            bo1 = b.sb(es1, "bo1", (128, 16), F32)
            s1t = b.sb(es1, "s1t", (128, 512), F32)
            ps_p = [b.ps(es1, f"ps_p{i}", (128, 512)) for i in range(2)]
            lw = sW.inc(GP.dma_start(out=wp[:], in_=b.d["wo1_t"].rearrange("kb i mb o -> i kb (mb o)")), 16)
            la = sA.inc(SY.dma_start(out=bo1[:], in_=b.d["bo1c"][:, :]), 16)
            s1tb = [s1t, b.sb(es1, "s1t_b", (128, 512), F32)]
            rwcb = [rwc, b.sb(es1, "rwc_b", (128, 16, 512), BF16)]
            p1_evac = [0, 0]
            p1_store = [0, 0]
            rw_use = [0, 0]
            for n in range(4):
                nl = n * 512
                nb = n % 2
                SY.wait_ge(sPe.h, rw_use[nb])
                lr = sA.inc(SY.dma_start(
                    out=rwcb[nb][:], in_=b.d["rwkv"][:, nl:nl + 512].rearrange("(t p) n -> p t n", p=128)), 16)
                PE.wait_ge(sA.h, lr)
                PE.wait_ge(sW.h, lw)
                for m in range(16):
                    pb = m % 2
                    PE.wait_ge(sVe.h, p1_evac[pb])
                    for kb in range(16):
                        li = PE.matmul(ps_p[pb][:], wp[:, kb, m * 128:(m + 1) * 128],
                                       rwcb[nb][:, kb, :], start=(kb == 0), stop=(kb == 15))
                    mm_done = sPe.inc(li, 1)
                    VE.wait_ge(sPe.h, mm_done)
                    VE.wait_ge(sO.h, p1_store[pb])
                    li = VE.tensor_scalar(s1tb[pb][:], ps_p[pb][:], bo1[:, m:m + 1], None, OP.add)
                    p1_evac[pb] = sVe.seal(VE)
                    SY.wait_ge(sVe.h, p1_evac[pb])
                    p1_store[pb] = sO.inc(SY.dma_start(
                        out=b.d["s1"][m * 128:(m + 1) * 128, nl:nl + 512], in_=s1tb[pb][:]), 16)
                rw_use[nb] = sPe.n
        b.barrier()
        with ExitStack() as es2:
            wgr = b.sb(es2, "wgr", (128, 16, 8), BF16)
            eyef = b.sb(es2, "eyef", (128, 128), F32)
            xnc = b.sb(es2, "xnc", (128, 16, 512), BF16)
            lgf = b.sb(es2, "lgf", (8, 512), F32)
            lgt = b.sb(es2, "lgt", (128, 8), F32)
            m1 = b.sb(es2, "m1", (128, 1), F32)
            nm1 = b.sb(es2, "nm1", (128, 1), F32)
            e8 = b.sb(es2, "e8", (128, 8), F32)
            eqm = b.sb(es2, "eqm", (128, 8), F32)
            msk8 = b.sb(es2, "msk8", (128, 8), F32)
            m2 = b.sb(es2, "m2", (128, 1), F32)
            kp8 = b.sb(es2, "kp8", (128, 8), F32)
            w8 = b.sb(es2, "w8", (128, 8), F32)
            den8 = b.sb(es2, "den8", (128, 1), F32)
            c8 = b.sb(es2, "c8", (128, 8), F32)
            c8p = b.sb(es2, "c8p", (128, 32), F32)
            negc = b.sb(es2, "negc", (128, 1), F32)
            c8t = b.sb(es2, "c8t", (8, TOK), F32)
            ps_l = b.ps(es2, "ps_l", (8, 512))
            ps_t = b.ps(es2, "ps_t", (128, 128))
            lw = sW.inc(GP.dma_start(out=wgr[:], in_=b.d["wgr_t"].rearrange("kb i e -> i kb e")), 16)
            la = sA.inc(SY.dma_start(out=eyef[:], in_=b.d["eye"][:, :]), 16)
            VE.memset(c8p[:], 0.0)
            li = VE.memset(negc[:], NEG)
            sVe.seal(VE)
            for n in range(4):
                nl = n * 512
                SY.wait_ge(sPe.h, sPe.n)
                lx = sA.inc(SY.dma_start(
                    out=xnc[:], in_=b.d["xn1"][:, nl:nl + 512].rearrange("(t p) n -> p t n", p=128)), 16)
                PE.wait_ge(sA.h, sA.n)
                PE.wait_ge(sW.h, lw)
                PE.wait_ge(sVe.h, sVe.n)
                for kb in range(16):
                    li = PE.matmul(ps_l[:], wgr[:, kb, :], xnc[:, kb, :],
                                   start=(kb == 0), stop=(kb == 15))
                sPe.inc(li, 1)
                VE.wait_ge(sPe.h, sPe.n)
                li = VE.tensor_copy(lgf[:], ps_l[:])
                sVe.seal(VE)
                for j in range(4):
                    tok0 = nl + j * 128
                    PE.wait_ge(sVe.h, sVe.n)
                    li = PE.transpose(ps_t[:, 0:8], lgf[0:8, j * 128:(j + 1) * 128], eyef[0:8, 0:8])
                    sPe.inc(li, 1)
                    VE.wait_ge(sPe.h, sPe.n)
                    VE.tensor_copy(lgt[:], ps_t[:, 0:8])
                    VE.drain()
                    VE.reduce_max(m1[:], lgt[:], axis=AX)
                    VE.drain()
                    li = VE.tensor_scalar(nm1[:], m1[:], -1.0, None, OP.mult)
                    sVe.seal(VE)
                    SC.wait_ge(sVe.h, sVe.n)
                    li = SC.activation(e8[:], lgt[:], AF.Exp, bias=nm1[:, 0:1])
                    sSc.seal(SC)
                    VE.wait_ge(sSc.h, sSc.n)
                    VE.tensor_scalar(eqm[:], lgt[:], m1[:, 0:1], None, OP.is_equal)
                    VE.drain()
                    VE.scalar_tensor_tensor(msk8[:], eqm[:], negc[:, 0:1], lgt[:], OP.mult, OP.add)
                    VE.drain()
                    VE.reduce_max(m2[:], msk8[:], axis=AX)
                    VE.drain()
                    VE.tensor_scalar(kp8[:], lgt[:], m2[:, 0:1], None, OP.is_ge)
                    VE.drain()
                    VE.tensor_mul(w8[:], e8[:], kp8[:])
                    VE.drain()
                    li = VE.reduce_sum(den8[:], w8[:], axis=AX)
                    sVe.seal(VE)
                    SC.wait_ge(sVe.h, sVe.n)
                    SC.activation(den8[:], den8[:], AF.Ln)
                    SC.drain()
                    li = SC.activation(den8[:], den8[:], AF.Exp, scale=-1.0)
                    sSc.seal(SC)
                    VE.wait_ge(sSc.h, sSc.n)
                    VE.tensor_scalar(c8[:], w8[:], den8[:, 0:1], None, OP.mult)
                    VE.drain()
                    li = VE.tensor_copy(c8p[:, 0:8], c8[:])
                    sVe.seal(VE)
                    PE.wait_ge(sVe.h, sVe.n)
                    li = PE.transpose(ps_t[0:32, 0:128], c8p[:], eyef[:])
                    sPe.inc(li, 1)
                    VE.wait_ge(sPe.h, sPe.n)
                    li = VE.tensor_copy(c8t[0:8, tok0:tok0 + 128], ps_t[0:8, 0:128])
                    sVe.seal(VE)
            SY.wait_ge(sVe.h, sVe.n)
            sO.inc(SY.dma_start(out=b.d["crow"][:, :], in_=c8t[:]), 16)
            SY.wait_ge(sO.h, sO.n)
        b.barrier()
        with ExitStack() as es3:
            xnc = [b.sb(es3, f"xnc{i}", (128, 16, 512), BF16) for i in range(2)]
            w1a = [b.sb(es3, f"w1a{i}", (128, 16, 128), BF16) for i in range(2)]
            w1b = [b.sb(es3, f"w1b{i}", (128, 16, 128), BF16) for i in range(2)]
            w2sb = b.sb(es3, "w2sb", (128, 11, 2048), BF16)
            hseb = b.sb(es3, "hseb", (128, 11, 1024), BF16)
            acc = b.sb(es3, "acc", (128, 16, 1024), F32)
            crowb = b.sb(es3, "crowb", (1, 1024), F32)
            sg = [b.sb(es3, f"sg{i}", (128, 512), F32) for i in range(2)]
            hpp = [b.sb(es3, f"hpp{i}", (128, 512), F32) for i in range(2)]
            g1t = [b.sb(es3, f"g1t{i}", (128, 512), F32) for i in range(2)]
            x1t = [b.sb(es3, f"x1t{i}", (128, 512), F32) for i in range(2)]
            s1t = [b.sb(es3, f"s1t3{i}", (128, 512), F32) for i in range(2)]
            outt = [b.sb(es3, f"outt{i}", (128, 512), F32) for i in range(2)]
            ps_a = [b.ps(es3, f"ps_a{i}", (128, 512)) for i in range(2)]
            ps_b = [b.ps(es3, f"ps_b{i}", (128, 512)) for i in range(2)]
            ps_c = [b.ps(es3, f"ps_c{i}", (128, 512)) for i in range(2)]
            ps_w = [b.ps(es3, f"ps_w{i}", (128, 512)) for i in range(2)]
            slab_use = [0, 0]
            gu_evac = [0, 0]
            xnc_use = [0, 0]
            psc_free = 0
            win_evac = [0, 0]
            crow_use = 0
            hs_free = 0
            w2_use = 0
            acc_free = 0
            comb_done = [0, 0]
            xs_done = [0, 0]

            def load_w1(e, mp):
                pb = mp % 2
                GP.wait_ge(sPe.h, slab_use[pb])
                sW.inc(GP.dma_start(
                    out=w1a[pb][:], in_=b.d["w1_t"][e, :, mp].rearrange("kb i o -> i kb o")), 16)
                return sW.inc(GP.dma_start(
                    out=w1b[pb][:], in_=b.d["w1_t"][e, :, mp + 11].rearrange("kb i o -> i kb o")), 16)

            for hf in range(2):
                hl = hf * 1024
                for e in range(NE):
                    SY.wait_ge(sPe.h, crow_use)
                    lc = sA.inc(SY.dma_start(out=crowb[:], in_=b.d["crow"][e:e + 1, hl:hl + 1024]), 16)
                    lx = [0, 0]
                    for n2 in range(2):
                        GP.wait_ge(sPe.h, xnc_use[n2])
                        lx[n2] = sW.inc(GP.dma_start(
                            out=xnc[n2][:],
                            in_=b.d["xn1"][:, hl + n2 * 512:hl + (n2 + 1) * 512]
                                 .rearrange("(t p) n -> p t n", p=128)), 16)
                    w1ld = [0] * 11
                    w1ld[0] = load_w1(e, 0)
                    GP.wait_ge(sPe.h, w2_use)
                    lw2 = sW.inc(GP.dma_start(
                        out=w2sb[:], in_=b.d["w2_t"][e].rearrange("kb i mb o -> i kb (mb o)")), 16)
                    for mp in range(11):
                        pb = mp % 2
                        if mp + 1 < 11:
                            w1ld[mp + 1] = load_w1(e, mp + 1)
                        for n2 in range(2):
                            pa = (2 * mp + n2) % 2
                            PE.wait_ge(sW.h, max(w1ld[mp], lx[n2]))
                            PE.wait_ge(sVe.h, gu_evac[pa])
                            for kb in range(16):
                                PE.matmul(ps_a[pa][:], w1a[pb][:, kb, :], xnc[n2][:, kb, :],
                                          start=(kb == 0), stop=(kb == 15))
                            for kb in range(16):
                                li = PE.matmul(ps_b[pa][:], w1b[pb][:, kb, :], xnc[n2][:, kb, :],
                                               start=(kb == 0), stop=(kb == 15))
                            if mp == 0:
                                if n2 == 0:
                                    PE.wait_ge(sA.h, lc)
                                    PE.wait_ge(sVe.h, psc_free)
                                li = PE.matmul(ps_c[n2][:], b._ones_row_f32[:],
                                               crowb[0:1, n2 * 512:(n2 + 1) * 512],
                                               start=True, stop=True)
                            u_done = sPe.inc(li, 1)
                            if n2 == 1:
                                slab_use[pb] = u_done
                            xnc_use[n2] = max(xnc_use[n2], u_done)
                            SC.wait_ge(sPe.h, u_done)
                            li = SC.activation(sg[pa][:], ps_a[pa][:], AF.Silu)
                            sc_done = sSc.seal(SC)
                            VE.wait_ge(sSc.h, sc_done)
                            if mp == 0 and n2 == 0:
                                VE.wait_ge(sPe.h, hs_free)
                            VE.tensor_mul(hpp[pa][:], sg[pa][:], ps_b[pa][:])
                            VE.drain()
                            li = VE.tensor_mul(hseb[:, mp, n2 * 512:(n2 + 1) * 512],
                                               hpp[pa][:], ps_c[n2][:])
                            gu_evac[pa] = sVe.seal(VE)
                    psc_free = sVe.n
                    hs_ready = sVe.n
                    PE.wait_ge(sW.h, lw2)
                    PE.wait_ge(sVe.h, hs_ready)
                    tiles = [(m, n2) for m in range(16) for n2 in range(2)]
                    for ui, (m, n2) in enumerate(tiles):
                        bs = ui % 2
                        PE.wait_ge(sVe.h, win_evac[bs])
                        for kb in range(11):
                            li = PE.matmul(ps_w[bs][:], w2sb[:, kb, m * 128:(m + 1) * 128],
                                           hseb[:, kb, n2 * 512:(n2 + 1) * 512],
                                           start=(kb == 0), stop=(kb == 10))
                        wd_done = sPe.inc(li, 1)
                        VE.wait_ge(sPe.h, wd_done)
                        if e == 0:
                            VE.wait_ge(sSc.h, acc_free)
                        if e == 0:
                            li = VE.tensor_copy(acc[:, m, n2 * 512:(n2 + 1) * 512], ps_w[bs][:])
                        else:
                            li = VE.tensor_add(acc[:, m, n2 * 512:(n2 + 1) * 512],
                                               acc[:, m, n2 * 512:(n2 + 1) * 512], ps_w[bs][:])
                        win_evac[bs] = sVe.seal(VE)
                    crow_use = sPe.n
                    hs_free = sPe.n
                    w2_use = sPe.n
                SC.wait_ge(sVe.h, sVe.n)
                for m in range(16):
                    for n2 in range(2):
                        nl = hl + n2 * 512
                        gb = (2 * m + n2) % 2
                        SC.wait_ge(sVe.h, xs_done[gb])
                        li = SC.activation(g1t[gb][:], acc[:, m, n2 * 512:(n2 + 1) * 512], AF.Silu)
                        sc_done = sSc.seal(SC)
                        SY.wait_ge(sVe.h, xs_done[gb])
                        la = sA.inc(SY.dma_start(out=x1t[gb][:], in_=b.d["x1"][m * 128:(m + 1) * 128, nl:nl + 512]), 16)
                        la = sA.inc(SY.dma_start(out=s1t[gb][:], in_=b.d["s1"][m * 128:(m + 1) * 128, nl:nl + 512]), 16)
                        VE.wait_ge(sA.h, la)
                        VE.wait_ge(sSc.h, sc_done)
                        VE.wait_ge(sO.h, comb_done[gb])
                        VE.tensor_mul(outt[gb][:], s1t[gb][:], g1t[gb][:])
                        li = VE.tensor_add(outt[gb][:], outt[gb][:], x1t[gb][:])
                        xs_done[gb] = sVe.seal(VE)
                        SY.wait_ge(sVe.h, xs_done[gb])
                        comb_done[gb] = sO.inc(SY.dma_start(
                            out=b.d["outT"][m * 128:(m + 1) * 128, nl:nl + 512],
                            in_=outt[gb][:]), 16)
                acc_free = sSc.n
        SY.wait_ge(sO.h, sO.n)
        GP.wait_ge(sO.h, sO.n)


def _emit_common_consts(b, es):
    b._ones_row_f32 = b.sb(es, "ones_row_f32", (1, 128), F32)
    b._ones_col_bf = b.sb(es, "ones_col_bf", (128, 1), BF16)
    b._ones_row_bf = b.sb(es, "ones_row_bf", (1, 128), BF16)
    b.GP.memset(b._ones_row_f32[:], 1.0)
    b.GP.memset(b._ones_row_bf[:], 1.0)
    li = b.GP.memset(b._ones_col_bf[:], 1.0)
    sGp = b.sems["gp"]
    sGp.inc(li, 1)
    for eng in (b.PE, b.VE, b.SC):
        eng.wait_ge(sGp.h, sGp.n)



def build_launch(stages, dev=False):
    b = B(stages, dev=dev)
    with ExitStack() as ges:
        b._ges = ges
        _emit_common_consts(b, ges)
        for i, st in enumerate(stages):
            if i:
                b.barrier()
            EMITTERS[st](b)
    return b


EMITTERS = {
    "ln0kvq": emit_ln0kvq,
    "ffn0": emit_ffn0,
    "attn": emit_attn,
    "ln1mix": emit_ln1mix,
    "wkv": emit_wkv,
    "moe": emit_moe,
}


def kernel(**inputs):
    per_core = _stage_all(inputs)
    arrs = [dict(pc) for pc in per_core]
    for stages in LAUNCH_PLAN:
        b = build_launch(stages)
        needed = set()
        for st in stages:
            needed |= set(STAGE_IO[st][0])
            needed |= set(STAGE_IO[st][1])
        in_maps = []
        for c in range(8):
            in_maps.append({name: arrs[c][name] for name in b.input_names})
        res = b.run(in_maps)
        for c in range(8):
            for k, v in res.results[c].items():
                arrs[c][k] = v
    out = np.empty((NB, T, D), np.float32)
    for c in range(8):
        role, batch = c % 2, c // 2
        out[batch, role * TOK:(role + 1) * TOK, :] = np.asarray(arrs[c]["outT"], np.float32).T
    return out
